# version 05
# baseline (speedup 1.0000x reference)
import numpy as np
from contextlib import ExitStack

import concourse.bass as bass
import concourse.mybir as mybir
from concourse.bass_utils import run_bass_kernel_spmd

F32 = mybir.dt.float32
BF16 = mybir.dt.bfloat16
AF = mybir.ActivationFunctionType
ALU = mybir.AluOpType
AX = mybir.AxisListType

D = 2048
S = 4096
T = 1024
NH = 32
NKV = 8
HD = 64
II = 8192
DC = D // 128
ICN = II // 128
LN_EPS = 1e-5
SUB_EPS = 1e-6
EPS_Q = 1e-5
NEG_BIG = -1.0e30
NG0, NG1 = 8, 16
NGT = NG0 + NG1


class Ctr:
    def __init__(self, h):
        self.h = h
        self.n = 0

    def inc(self, inst, k=1):
        inst.then_inc(self.h, k)
        self.n += k
        return inst

    def wait(self, eng, n=None):
        eng.wait_ge(self.h, self.n if n is None else n)


class Buf:

    def __init__(self, ctr=None, depth=1):
        self.acq = 0
        self.depth = depth
        self.marks = []

    def acquire(self, eng):
        idx = self.acq - self.depth
        if 0 <= idx < len(self.marks):
            for c, n in self.marks[idx]:
                eng.wait_ge(c.h, n)
        elif idx >= len(self.marks):
            raise RuntimeError("Buf.acquire before matching release")
        self.acq += 1

    def release(self, *ctrs):
        self.marks.append([(c, c.n) for c in ctrs])


class Ring:

    def __init__(self, rdys, fre, n):
        self.rdys = rdys
        self.fre = fre
        self.n = n
        self.p = 0

    def produce(self, eng):
        slot = self.p % self.n
        if self.p >= self.n:
            eng.wait_ge(self.fre.h, self.p - self.n + 1)
        self.p += 1
        return slot

    def mark(self, inst):
        slot = (self.p - 1) % self.n
        self.rdys[slot].inc(inst, 16)

    def wait_ready(self, eng, i):
        slot = i % self.n
        eng.wait_ge(self.rdys[slot].h, 16 * (i // self.n + 1))

    def release(self, eng, ctr):
        eng.wait_ge(ctr.h, ctr.n)
        self.fre.inc(eng.nop())


DEBUG = False
SERIAL_D1 = False
NO_M2 = False


def build_program():
    nc = bass.Bass("TRN2", target_bir_lowering=False, detect_race_conditions=False)
    dkind = "ExternalOutput" if DEBUG else "Internal"

    def din(name, shape):
        return nc.dram_tensor(name, list(shape), F32, kind="ExternalInput").ap()

    x_own = din("x_own", (D, T))
    x_full = din("x_full", (D, S))
    wq_d = din("wq", (D, D))
    wk_d = din("wk", (D, NKV * HD))
    wv_d = din("wv", (D, NKV * HD))
    wo_d = din("wo", (D, D))
    wgT = din("wgT", (D, II))
    wuT = din("wuT", (D, II))
    wdT = din("wdT", (II, D))
    g_col = din("g_col", (128, ICN))
    rope_cf = din("rope_cf", (128, S))
    rope_sf = din("rope_sf", (128, S))
    rope_co = din("rope_co", (128, T))
    rope_so = din("rope_so", (128, T))
    perm_in = din("perm", (128, 128))
    iden_in = din("iden", (128, 128))
    ones_in = din("ones", (128, 128))
    e2_in = din("e2", (128, 2))
    dmask_in = din("dmask", (128, 2048))
    pat_in = din("pat", (128, NGT))
    y_out = nc.dram_tensor("y", [T, D], F32, kind="ExternalOutput").ap()
    act_scr = nc.dram_tensor("act_scr", [II, T], F32, kind=dkind).ap()
    xln1_scr = nc.dram_tensor("xln1_scr", [D, T], F32, kind=dkind).ap()
    y2_scr = nc.dram_tensor("y2_scr", [D, T], F32, kind=dkind).ap()
    if DEBUG:
        dbg_att = nc.dram_tensor("dbg_att", [D, T], F32, kind="ExternalOutput").ap()
        dbg_misc = nc.dram_tensor("dbg_misc", [128, 64], F32, kind="ExternalOutput").ap()
        dbg_nx = nc.dram_tensor("dbg_nx", [D, T], F32, kind="ExternalOutput").ap()
        dbg_q = nc.dram_tensor("dbg_q", [D, T], F32, kind="ExternalOutput").ap()
        dbg_rg = nc.dram_tensor("dbg_rg", [II, T], F32, kind="ExternalOutput").ap()
        dbg_k = nc.dram_tensor("dbg_k", [512, S], F32, kind="ExternalOutput").ap()
        dbg_v = nc.dram_tensor("dbg_v", [128, 32 * NKV * 65], F32, kind="ExternalOutput").ap()

    st0 = ExitStack()

    def sb(stk, name, shape, dt=BF16, side=None):
        return stk.enter_context(nc.sbuf_tensor(name, list(shape), dt, side=side))

    def pt(stk, name, shape, dt=F32):
        return stk.enter_context(nc.psum_tensor(name, list(shape), dt))

    def sem(name):
        return Ctr(st0.enter_context(nc.semaphore(name)))

    iden_b = sb(st0, "iden_b", (128, 128))
    iden_f = sb(st0, "iden_f", (128, 128), F32)
    ones_f = sb(st0, "ones_f", (128, 128), F32)
    ones_b = sb(st0, "ones_b", (128, 128))
    pat_s = sb(st0, "pat_s", (128, NGT), F32)
    gcol_s = sb(st0, "gcol_s", (128, ICN), F32)
    biastbl = sb(st0, "biastbl", (128, NGT), F32)
    colA = sb(st0, "colA", (128, 16), F32)
    wabs = sb(st0, "wabs", (128, 200), F32)
    stg = sb(st0, "stg", (128, 128), F32)
    scl = sb(st0, "scl", (1, 128), F32)
    rows = None

    def row(q, half):
        return rows[32 * half:32 * half + 1, 512 * q:512 * (q + 1)]

    R_S1, R_S2, R_MEAN, R_RSTD, R_A, R_B, R_C, R_DD = range(8)

    qs = sem("qs")
    qg = sem("qg")
    qa = sem("qa")
    st = sem("st")
    sv = sem("sv")
    ss = sem("ss")
    bar = sem("bar")

    block = st0.enter_context(nc.Block())
    ENGS = ["tensor", "vector", "scalar", "gpsimd", "sync"]

    def sec(name):
        return getattr(block, name)

    def vsafe(v, inst):
        sv.inc(inst)
        v.wait_ge(sv.h, sv.n)
        return inst

    def barrier():
        for e in ENGS:
            @sec(e)
            def _(eng):
                eng.nop().then_inc(bar.h, 1)
        bar.n += 5
        tgt = bar.n
        for e in ENGS:
            @sec(e)
            def _(eng):
                eng.wait_ge(bar.h, tgt)

    def broadcast_row(slotq, dst_tile, pbank, pbuf):
        for half in range(2):
            @sec("tensor")
            def _(t, half=half):
                sv.wait(t)
                ss.wait(t)
                t.wait_ge(qs.h, qs.n)
                pbuf.acquire(t)
                _li = t.matmul(pbank[:, 0:512], ones_f[32 * half:32 * half + 1, :],
                         row(slotq, half), start=True, stop=True)
                st.inc(_li)

            @sec("vector")
            def _(v, half=half):
                st.wait(v)
                _li = v.tensor_copy(dst_tile[:, 512 * half:512 * (half + 1)],
                                  pbank[:, 0:512])
                sv.inc(_li)
                pbuf.release(sv)

    @sec("gpsimd")
    def _(g):
        for dst, src in [(iden_b[:], iden_in), (ones_b[:], ones_in)]:
            qg.inc(g.dma_start(out=dst, in_=src), 16)

    @sec("sync")
    def _(s):
        for dst, src in [(iden_f[:], iden_in), (ones_f[:], ones_in),
                         (pat_s[:], pat_in), (gcol_s[:], g_col)]:
            qs.inc(s.dma_start(out=dst, in_=src), 16)

    barrier()

    stQ = ExitStack()
    Qf = sb(stQ, "Qf", (128, DC, T))
    n_x = Qf

    stKV = ExitStack()
    Kf = sb(stKV, "Kf", (128, 4, S))
    Vt = sb(stKV, "Vt", (128, 32, NKV, 65))
    Kd = sb(stKV, "Kd", (128, 4, T))
    Vd = sb(stKV, "Vd", (128, 8, NKV, 65))

    with ExitStack() as stT:
        perm_b = sb(stT, "perm_b", (128, 128))
        e2_b = sb(stT, "e2_b", (128, 2))
        cf_b = sb(stT, "cf_b", (128, S))
        sf_b = sb(stT, "sf_b", (128, S))
        co_b = sb(stT, "co_b", (128, T))
        so_b = sb(stT, "so_b", (128, T))
        xstr = sb(stT, "xstr", (128, 2, DC, 256))
        wk_s = sb(stT, "wk_s", (128, DC, 512))
        wv_s = sb(stT, "wv_s", (128, DC, 512))
        wq_s = sb(stT, "wq_s", (128, 2, DC, 256))
        kpre = sb(stT, "kpre", (128, 2, 256))
        ktmp = sb(stT, "ktmp", (128, 256))
        ktmp2 = sb(stT, "ktmp2", (128, 256))
        sqt = sb(stT, "sqt", (128, 1024))
        pp0 = pt(stT, "pp0", (128, 512))
        pp1 = pt(stT, "pp1", (128, 512))
        pr0 = pt(stT, "pr0", (128, 256))
        pr1 = pt(stT, "pr1", (128, 256))
        pnm = pt(stT, "pnm", (128, 512))
        ptf = pt(stT, "ptf", (128, 128))

        xring = Ring([sem("xr_r0"), sem("xr_r1")], sem("rf1"), 2)
        wqring = Ring([sem("wq_r0"), sem("wq_r1")], sem("rf2"), 2)
        bpp = [Buf(sem("bpp0")), Buf(sem("bpp1"))]
        bpr = [Buf(sem("bpr0")), Buf(sem("bpr1"))]
        bkp = [Buf(sem("bkp0")), Buf(sem("bkp1"))]
        bnm = Buf(sem("bnm"))
        btf = Buf(sem("btf"))

        @sec("gpsimd")
        def _(g):
            for dst, src in [(perm_b[:], perm_in), (e2_b[:], e2_in),
                             (cf_b[:], rope_cf), (sf_b[:], rope_sf),
                             (co_b[:], rope_co), (so_b[:], rope_so)]:
                qg.inc(g.dma_start(out=dst, in_=src), 16)
            qg.inc(g.dma_start(out=wk_s[:],
                               in_=wk_d.rearrange("(c p) n -> p c n", p=128)), 16)
            qg.inc(g.dma_start(out=wv_s[:],
                               in_=wv_d.rearrange("(c p) n -> p c n", p=128)), 16)
        kvw_mark = qg.n

        NT_KV = 20

        def xsrc(i):
            if i < 16:
                return x_full.rearrange("(c p) t -> p c t", p=128)[:, :, 256 * i:256 * (i + 1)]
            j = i - 16
            return x_own.rearrange("(c p) t -> p c t", p=128)[:, :, 256 * j:256 * (j + 1)]

        @sec("gpsimd")
        def _(g):
            for i in range(NT_KV):
                slot = xring.produce(g)
                xring.mark(g.dma_start(out=xstr[:, slot, :, :], in_=xsrc(i)))

        pcnt = 0
        for ti in range(NT_KV):
            own = ti >= 16
            toff = 256 * (ti - 16) if own else 256 * ti
            cf, sf_ = (co_b, so_b) if own else (cf_b, sf_b)
            kdst = Kd if own else Kf
            for c in range(4):
                par = pcnt % 2
                pcnt += 1
                pbt, pb = (pp0, bpp[0]) if par == 0 else (pp1, bpp[1])
                prt, prb = (pr0, bpr[0]) if par == 0 else (pr1, bpr[1])
                kpb = bkp[par]

                @sec("tensor")
                def _(t, ti=ti, c=c, pbt=pbt, pb=pb):
                    xring.wait_ready(t, ti)
                    if ti == 0 and c == 0:
                        t.wait_ge(qg.h, kvw_mark)
                    pb.acquire(t)
                    for k in range(DC):
                        _li = t.matmul(pbt[:, 0:256], wk_s[:, k, 128 * c:128 * (c + 1)],
                                 xstr[:, ti % 2, k, :],
                                 start=(k == 0), stop=(k == DC - 1))
                    st.inc(_li)

                @sec("scalar")
                def _(sc, pbt=pbt, pb=pb, kpb=kpb, par=par):
                    st.wait(sc)
                    kpb.acquire(sc)
                    _li = sc.copy(kpre[:, par, :], pbt[:, 0:256])
                    ss.inc(_li)
                    pb.release(ss)

                @sec("tensor")
                def _(t, prt=prt, prb=prb, par=par):
                    ss.wait(t)
                    prb.acquire(t)
                    _li = t.matmul(prt[:], perm_b[:], kpre[:, par, :], start=True, stop=True)
                    st.inc(_li)

                @sec("vector")
                def _(v, c=c, prt=prt, prb=prb, kpb=kpb, par=par,
                      kdst=kdst, cf=cf, sf_=sf_, toff=toff):
                    st.wait(v)
                    _li = v.tensor_mul(ktmp[:], kpre[:, par, :], cf[:, toff:toff + 256])
                    _li = v.tensor_mul(ktmp2[:], prt[:], sf_[:, toff:toff + 256])
                    _li = v.tensor_add(kdst[:, c, toff:toff + 256], ktmp[:], ktmp2[:])
                    sv.inc(_li)
                    prb.release(sv)
                    kpb.release(sv)

            for m in range(2):
                par = pcnt % 2
                pcnt += 1
                pbt, pb = (pp0, bpp[0]) if par == 0 else (pp1, bpp[1])
                tc = (toff + 128 * m) // 128

                @sec("tensor")
                def _(t, ti=ti, m=m, pbt=pbt, pb=pb):
                    pb.acquire(t)
                    for k in range(DC):
                        _li = t.matmul(pbt[:], xstr[:, ti % 2, k, 128 * m:128 * (m + 1)],
                                 wv_s[:, k, :],
                                 start=(k == 0), stop=(k == DC - 1))
                    st.inc(_li)

                @sec("vector")
                def _(v, pbt=pbt, pb=pb, tc=tc, own=own):
                    st.wait(v)
                    dst = Vd if own else Vt
                    sv.inc(v.tensor_copy(
                        dst[:, tc, :, 0:64],
                        pbt[:].rearrange("p (h n) -> p h n", h=NKV)))
                    pb.release(sv)

            @sec("tensor")
            def _(t):
                sv.wait(t)
                xring.release(t, st)

        @sec("vector")
        def _(v):
            sv.wait(v)
            _li = v.memset(Vt[:, :, :, 64:65], 1.0)
            _li = v.memset(Vd[:, :, :, 64:65], 1.0)
            sv.inc(_li)

        kv_done_sv = sv.n

        @sec("gpsimd")
        def _(g):
            g.wait_ge(sv.h, kv_done_sv)
            for e8 in range(8):
                slot = wqring.produce(g)
                wqring.mark(g.dma_start(
                    out=wq_s[:, slot, :, :],
                    in_=wq_d.rearrange("(c p) n -> p c n", p=128)[:, :, 256 * e8:256 * (e8 + 1)]))
                for ot in range(4):
                    slot = xring.produce(g)
                    xring.mark(g.dma_start(out=xstr[:, slot, :, :],
                                           in_=xsrc(16 + ot)))

        for e8 in range(8):
            for ot in range(4):
                tidx = NT_KV + 4 * e8 + ot
                toff = 256 * ot
                for q2 in range(2):
                    qc = 2 * e8 + q2
                    par = pcnt % 2
                    pcnt += 1
                    pbt, pb = (pp0, bpp[0]) if par == 0 else (pp1, bpp[1])
                    prt, prb = (pr0, bpr[0]) if par == 0 else (pr1, bpr[1])
                    kpb = bkp[par]

                    @sec("tensor")
                    def _(t, e8=e8, q2=q2, tidx=tidx, pbt=pbt, pb=pb):
                        xring.wait_ready(t, tidx)
                        wqring.wait_ready(t, e8)
                        pb.acquire(t)
                        for k in range(DC):
                            _li = t.matmul(pbt[:, 0:256],
                                     wq_s[:, e8 % 2, k, 128 * q2:128 * (q2 + 1)],
                                     xstr[:, tidx % 2, k, :],
                                     start=(k == 0), stop=(k == DC - 1))
                        st.inc(_li)

                    @sec("scalar")
                    def _(sc, pbt=pbt, pb=pb, kpb=kpb, par=par):
                        st.wait(sc)
                        kpb.acquire(sc)
                        _li = sc.copy(kpre[:, par, :], pbt[:, 0:256])
                        ss.inc(_li)
                        pb.release(ss)

                    @sec("tensor")
                    def _(t, prt=prt, prb=prb, par=par):
                        ss.wait(t)
                        prb.acquire(t)
                        _li = t.matmul(prt[:], perm_b[:], kpre[:, par, :],
                                 start=True, stop=True)
                        st.inc(_li)

                    @sec("vector")
                    def _(v, qc=qc, toff=toff, prt=prt, prb=prb, kpb=kpb, par=par):
                        st.wait(v)
                        _li = v.tensor_mul(ktmp[:], kpre[:, par, :], co_b[:, toff:toff + 256])
                        _li = v.tensor_mul(ktmp2[:], prt[:], so_b[:, toff:toff + 256])
                        _li = v.tensor_add(Qf[:, qc, toff:toff + 256], ktmp[:], ktmp2[:])
                        sv.inc(_li)
                        prb.release(sv)
                        kpb.release(sv)

                @sec("tensor")
                def _(t):
                    sv.wait(t)
                    xring.release(t, st)

            @sec("tensor")
            def _(t):
                wqring.release(t, st)

        ncol = 0

        bsq = Buf()

        def normmax_tiles(tiles):
            nonlocal ncol
            first = ncol
            for ap1024 in tiles:
                @sec("scalar")
                def _(sc, ap1024=ap1024):
                    sv.wait(sc)
                    bsq.acquire(sc)
                    _li = sc.activation(sqt[:], ap1024, AF.Square)
                    ss.inc(_li)

                for j in range(2):
                    @sec("tensor")
                    def _(t, j=j):
                        ss.wait(t)
                        bnm.acquire(t)
                        _li = t.matmul(pnm[0:2, 0:512], e2_b[:],
                                 sqt[:, 512 * j:512 * (j + 1)],
                                 start=True, stop=True)
                        st.inc(_li)
                        if j == 1:
                            bsq.release(st)

                    @sec("vector")
                    def _(v, ncol=ncol):
                        st.wait(v)
                        _li = v.tensor_reduce(stg[0:2, 2 + ncol:3 + ncol], pnm[0:2, 0:512],
                                        AX.X, ALU.max)
                        sv.inc(_li)
                        bnm.release(sv)
                    ncol += 1
            return first

        @sec("vector")
        def _(v):
            sv.wait(v)
            _li = v.memset(stg[:], 0.0)
            sv.inc(_li)

        kfirst = normmax_tiles([Kf[:, c, 1024 * n:1024 * (n + 1)]
                                for c in range(4) for n in range(4)])
        qfirst = normmax_tiles([Qf[:, qc, :] for qc in range(DC)])

        for which, first, last, out_c in ((0, kfirst, qfirst, 0), (1, qfirst, ncol, 1)):
            @sec("vector")
            def _(v, first=first, last=last):
                sv.wait(v)
                vsafe(v, v.tensor_reduce(stg[0:2, 0:1], stg[0:2, 2 + first:2 + last],
                                AX.X, ALU.max))

            @sec("tensor")
            def _(t):
                sv.wait(t)
                btf.acquire(t)
                _li = t.transpose(ptf[:], stg[:], iden_f[:])
                st.inc(_li)

            @sec("vector")
            def _(v, out_c=out_c):
                st.wait(v)
                vsafe(v, v.tensor_reduce(scl[0:1, out_c:out_c + 1], ptf[0:1, 0:2],
                                AX.X, ALU.max))
                btf.release(sv)

        @sec("vector")
        def _(v):
            sv.wait(v)
            vsafe(v, v.tensor_mul(scl[0:1, 2:3], scl[0:1, 0:1], scl[0:1, 1:2]))

        @sec("scalar")
        def _(sc):
            sv.wait(sc)
            _li = sc.activation(scl[0:1, 3:4], scl[0:1, 2:3], AF.Sqrt)
            ss.inc(_li)

        @sec("vector")
        def _(v):
            ss.wait(v)
            vsafe(v, v.tensor_scalar_mul(scl[0:1, 4:5], scl[0:1, 3:4], -0.126))

        @sec("tensor")
        def _(t):
            sv.wait(t)
            bnm.acquire(t)
            _li = t.matmul(pnm[:, 0:1], ones_f[0:1, :], scl[0:1, 4:5], start=True, stop=True)
            st.inc(_li)

        @sec("vector")
        def _(v):
            st.wait(v)
            _li = v.tensor_copy(colA[:, 0:1], pnm[:, 0:1])
            sv.inc(_li)
            bnm.release(sv)
            v.wait_ge(sv.h, sv.n)
            sv.inc(v.tensor_scalar(biastbl[:], pat_s[:], colA[:, 0:1], None, ALU.add))

        if DEBUG:
            @sec("gpsimd")
            def _(g):
                g.wait_ge(sv.h, sv.n)
                qg.inc(g.dma_start(out=dbg_q.rearrange("(c p) t -> p c t", p=128),
                                   in_=Qf[:]), 16)
                qg.inc(g.dma_start(out=dbg_k.rearrange("(c p) t -> p c t", p=128),
                                   in_=Kf[:]), 16)
                qg.inc(g.dma_start(out=dbg_v[:],
                                   in_=Vt[:].rearrange("p a h n -> p (a h n)")), 16)
                qg.inc(g.dma_start(out=dbg_misc[:, 0:NGT], in_=biastbl[:]), 16)
                qg.inc(g.dma_start(out=dbg_misc[:, 56:64],
                                   in_=stg[:, 0:8]), 16)
                g.wait_ge(qg.h, qg.n)

            @sec("sync")
            def _(s):
                ss.wait(s)
                sv.wait(s)
                qs.inc(s.dma_start(out=dbg_misc[0:1, 40:45], in_=scl[0:1, 0:5]), 16)

        barrier()

    stAtt = ExitStack()
    att = sb(stAtt, "att", (128, DC, T), side="right")

    with ExitStack() as stB:
        dmask_b = sb(stB, "dmask_b", (128, 2048))
        ebuf = sb(stB, "ebuf", (128, 2, 1024))
        avsb = sb(stB, "avsb", (64, 512))
        recb = sb(stB, "recb", (1, 512))
        recf = sb(stB, "recf", (1, 512), F32)
        w1str = sb(stB, "w1str", (128, 2, 2048), F32)
        psc0 = pt(stB, "psc0", (128, 1024))
        psc1 = pt(stB, "psc1", (128, 1024))
        pav = pt(stB, "pav", (65, 512))
        pbc = pt(stB, "pbc", (128, 512))

        w1ring = Ring([sem("w1_r0"), sem("w1_r1")], sem("rf3"), 2)
        bsc = [Buf(sem("bsc0")), Buf(sem("bsc1"))]
        beb = [Buf(sem("beb0")), Buf(sem("beb1"))]
        bav = Buf(sem("bav"))
        bbc = Buf(sem("bbc"))

        @sec("gpsimd")
        def _(g):
            qg.inc(g.dma_start(out=dmask_b[:], in_=dmask_in), 16)
        dm_mark = qg.n

        NW1 = 192

        def w1_src(i):
            if i < 64:
                return wgT.rearrange("(c p) n -> p c n", p=128)[:, i // 4, 2048 * (i % 4):2048 * (i % 4 + 1)]
            if i < 128:
                j = i - 64
                return wuT.rearrange("(c p) n -> p c n", p=128)[:, j // 4, 2048 * (j % 4):2048 * (j % 4 + 1)]
            j = i - 128
            return wdT.rearrange("(c p) h -> p c h", p=128)[:, j:j + 1, :] \
                .rearrange("p c n -> p (c n)")

        @sec("sync")
        def _(s):
            for i in range(NW1):
                slot = w1ring.produce(s)
                w1ring.mark(s.dma_start(out=w1str[:, slot, :], in_=w1_src(i)))

        @sec("vector")
        def _(v):
            for i in range(NW1):
                w1ring.wait_ready(v, i)
                _li = v.tensor_reduce(wabs[:, i:i + 1], w1str[:, i % 2, :], AX.X,
                                ALU.add, apply_absolute_value=True)
                sv.inc(_li)
                w1ring.release(v, sv)

        b0heads = [h for h in range(NH) if (h // 4) % 2 == 0]
        b64heads = [h for h in range(NH) if (h // 4) % 2 == 1]
        QCH = {}
        QB = {}
        for i in range(16):
            QCH[b0heads[i]] = i
            QB[b0heads[i]] = 0
            QCH[b64heads[i]] = i
            QB[b64heads[i]] = 64
        gidx = 0
        for h in range(NH):
            kvh = h // 4
            kc = kvh // 2
            kb0 = (kvh % 2) * 64
            qc2 = QCH[h]
            qb0 = QB[h]
            for s_ in range(2):
                NG = NG0 if s_ == 0 else NG1
                qoff = 512 * s_
                for gg in range(NG):
                    par = gidx % 2
                    gidx += 1
                    psct, bscb = (psc0, bsc[0]) if par == 0 else (psc1, bsc[1])
                    bebb = beb[par]
                    diag = gg >= NG - 2
                    dg = gg - (NG - 2)
                    biascol = (NG0 if s_ == 1 else 0) + gg

                    @sec("tensor")
                    def _(t, s_=s_, gg=gg, psct=psct, bscb=bscb, diag=diag,
                          dg=dg, kc=kc, kb0=kb0, qc2=qc2, qb0=qb0, qoff=qoff,
                          gidx=gidx):
                        if gidx == 1:
                            t.wait_ge(qg.h, dm_mark)
                        bscb.acquire(t)
                        for j in range(2):
                            if diag:
                                ksl = Kd[kb0:kb0 + 64, kc,
                                         qoff + 256 * dg + 128 * j:qoff + 256 * dg + 128 * (j + 1)]
                            else:
                                ksl = Kf[kb0:kb0 + 64, kc,
                                         256 * gg + 128 * j:256 * gg + 128 * (j + 1)]
                            _li = t.matmul(psct[:, 512 * j:512 * (j + 1)], ksl,
                                     Qf[qb0:qb0 + 64, qc2, qoff:qoff + 512],
                                     start=True, stop=True)
                        st.inc(_li)

                    @sec("scalar")
                    def _(sc, par=par, psct=psct, bscb=bscb, bebb=bebb,
                          biascol=biascol):
                        st.wait(sc)
                        bebb.acquire(sc)
                        _li = sc.activation(ebuf[:, par, :], psct[:], AF.Exp,
                                      bias=biastbl[:, biascol:biascol + 1],
                                      scale=0.125)
                        ss.inc(_li)
                        bscb.release(ss)

                    if diag:
                        @sec("vector")
                        def _(v, par=par, dg=dg):
                            ss.wait(v)
                            sv.inc(v.tensor_mul(ebuf[:, par, :], ebuf[:, par, :],
                                                dmask_b[:, 1024 * dg:1024 * (dg + 1)]))

                    @sec("tensor")
                    def _(t, s_=s_, gg=gg, par=par, bebb=bebb, diag=diag,
                          dg=dg, kvh=kvh, NG=NG):
                        if diag:
                            sv.wait(t)
                        else:
                            ss.wait(t)
                        if gg == 0:
                            bav.acquire(t)
                        for j in range(2):
                            if diag:
                                vtile = Vd[:, 4 * s_ + 2 * dg + j, kvh, :]
                            else:
                                vtile = Vt[:, 2 * gg + j, kvh, :]
                            _li = t.matmul(pav[:], vtile,
                                     ebuf[:, par, 512 * j:512 * (j + 1)],
                                     start=(gg == 0 and j == 0),
                                     stop=(gg == NG - 1 and j == 1))
                        st.inc(_li)
                        bebb.release(st)

                @sec("vector")
                def _(v):
                    st.wait(v)
                    _li = v.reciprocal(recf[:], pav[64:65, :])
                    sv.inc(v.tensor_copy(recb[:], recf[:]))

                @sec("scalar")
                def _(sc):
                    st.wait(sc)
                    sv.wait(sc)
                    _li = sc.copy(avsb[:], pav[0:64, :])
                    ss.inc(_li)

                @sec("tensor")
                def _(t):
                    sv.wait(t)
                    ss.wait(t)
                    bav.release(sv, ss)
                    bbc.acquire(t)
                    _li = t.matmul(pbc[0:64, :], ones_b[0:1, 0:64], recb[:],
                             start=True, stop=True)
                    st.inc(_li)

                @sec("vector")
                def _(v, qc2=qc2, qb0=qb0, qoff=qoff):
                    st.wait(v)
                    sv.inc(v.tensor_mul(
                        att[qb0:qb0 + 64, qc2, qoff:qoff + 512],
                        avsb[:], pbc[0:64, :]))
                    bbc.release(sv)

        barrier()

        if DEBUG:
            @sec("gpsimd")
            def _(g):
                qg.inc(g.dma_start(out=dbg_att.rearrange("(c p) t -> p c t", p=128),
                                   in_=att[:]), 16)
                g.wait_ge(qg.h, qg.n)

    stKV.close()
    stRows = ExitStack()
    rows = sb(stRows, "rows", (128, 8 * 512), F32)

    with ExitStack() as stC:
        wo_s = sb(stC, "wo_s", (128, 2, DC, 256))
        xo_f = sb(stC, "xo_f", (128, DC, T), F32)
        ctile = sb(stC, "ctile", (128, 1024), F32)
        ctil2 = sb(stC, "ctil2", (128, 1024), F32)
        cbf = sb(stC, "cbf", (128, 1024))
        cbf2 = sb(stC, "cbf2", (128, 1024))
        mxf = sb(stC, "mxf", (128, 1024), F32)
        qmx = sb(stC, "qmx", (128, 16), F32)
        trow = sb(stC, "trow", (128, 512), F32)
        sxb = sb(stC, "sxb", (128, T), F32)
        bcst = sb(stC, "bcst", (128, 2, 1024), F32)
        pw0 = pt(stC, "pw0", (128, 1024))
        pw1 = pt(stC, "pw1", (128, 1024))
        prow1 = pt(stC, "prow1", (128, 512))
        prow2 = pt(stC, "prow2", (128, 512))
        pbc0 = pt(stC, "pbc0", (128, 512))
        ptf2 = pt(stC, "ptf2", (128, 512))

        woring = Ring([sem("wo_r0"), sem("wo_r1")], sem("rf4"), 2)
        bcb = Buf()
        bpw = [Buf(sem("bpw0")), Buf(sem("bpw1"))]
        brow = Buf(sem("brow"))
        bbc0 = Buf(sem("bbc0"))
        btf2 = Buf(sem("btf2"))

        @sec("gpsimd")
        def _(g):
            for e8 in range(8):
                slot = woring.produce(g)
                woring.mark(g.dma_start(
                    out=wo_s[:, slot, :, :],
                    in_=wo_d.rearrange("(c p) n -> p c n", p=128)[:, :, 256 * e8:256 * (e8 + 1)]))

        @sec("sync")
        def _(s):
            qs.inc(s.dma_start(out=xo_f[:],
                               in_=x_own.rearrange("(c p) t -> p c t", p=128)), 16)
        xo_mark = qs.n

        for dc in range(DC):
            par = dc % 2
            pwt, pwb = (pw0, bpw[0]) if par == 0 else (pw1, bpw[1])
            e8 = dc // 2
            q2 = dc % 2

            @sec("tensor")
            def _(t, dc=dc, pwt=pwt, pwb=pwb, e8=e8, q2=q2):
                woring.wait_ready(t, e8)
                pwb.acquire(t)
                for k in range(DC):
                    for th in range(2):
                        _li = t.matmul(pwt[:, 512 * th:512 * (th + 1)],
                                 wo_s[:, e8 % 2, k, 128 * q2:128 * (q2 + 1)],
                                 att[:, k, 512 * th:512 * (th + 1)],
                                 start=(k == 0), stop=(k == DC - 1))
                st.inc(_li)
                if q2 == 1:
                    woring.release(t, st)

            @sec("vector")
            def _(v, dc=dc, pwt=pwt, pwb=pwb):
                st.wait(v)
                if dc == 0:
                    v.wait_ge(qs.h, xo_mark)
                _li = v.tensor_add(xo_f[:, dc, :], pwt[:], xo_f[:, dc, :])
                sv.inc(_li)
                pwb.release(sv)

            @sec("scalar")
            def _(sc, dc=dc):
                sv.wait(sc)
                bcb.acquire(sc)
                _li = sc.copy(cbf[:], xo_f[:, dc, :])
                _li = sc.activation(cbf2[:], xo_f[:, dc, :], AF.Square)
                ss.inc(_li)

            @sec("tensor")
            def _(t, dc=dc):
                ss.wait(t)
                if dc == 0:
                    brow.acquire(t)
                for th in range(2):
                    _li = t.matmul(prow1[32 * th:32 * th + 1, 0:512], ones_b[:, 0:1],
                             cbf[:, 512 * th:512 * (th + 1)],
                             start=(dc == 0), stop=(dc == DC - 1))
                    _li = t.matmul(prow2[32 * th:32 * th + 1, 0:512], ones_b[:, 0:1],
                             cbf2[:, 512 * th:512 * (th + 1)],
                             start=(dc == 0), stop=(dc == DC - 1))
                st.inc(_li)
                bcb.release(st)

        @sec("vector")
        def _(v):
            st.wait(v)
            _li = v.tensor_copy(row(R_S1, 0), prow1[0:1, 0:512])
            _li = v.tensor_copy(row(R_S1, 1), prow1[32:33, 0:512])
            _li = v.tensor_copy(row(R_S2, 0), prow2[0:1, 0:512])
            _li = v.tensor_copy(row(R_S2, 1), prow2[32:33, 0:512])
            sv.inc(_li)
            brow.release(sv)

        def ln_rows_math():
            for half in range(2):
                @sec("vector")
                def _(v, half=half):
                    v.wait_ge(qs.h, qs.n)
                    _li = v.tensor_scalar_mul(row(R_MEAN, half), row(R_S1, half), 1.0 / D)
                    _li = v.tensor_mul(row(R_A, half), row(R_MEAN, half), row(R_MEAN, half))
                    _li = v.scalar_tensor_tensor(row(R_B, half), row(R_S2, half), 1.0 / D,
                                           row(R_A, half), ALU.mult, ALU.subtract)
                    _li = v.tensor_scalar_add(row(R_B, half), row(R_B, half), LN_EPS)
                    _li = v.reciprocal(row(R_C, half), row(R_B, half))
                    sv.inc(_li)

                @sec("scalar")
                def _(sc, half=half):
                    sv.wait(sc)
                    _li = sc.activation(row(R_RSTD, half), row(R_C, half), AF.Sqrt)
                    ss.inc(_li)

                @sec("vector")
                def _(v, half=half):
                    ss.wait(v)
                    _li = v.tensor_mul(row(R_A, half), row(R_RSTD, half), row(R_RSTD, half))
                    _li = v.tensor_mul(row(R_A, half), row(R_A, half), row(R_B, half))
                    _li = v.tensor_scalar(row(R_A, half), row(R_A, half), -0.5, 1.5,
                                    ALU.mult, ALU.add)
                    _li = v.tensor_mul(row(R_RSTD, half), row(R_RSTD, half), row(R_A, half))
                    sv.inc(_li)

        ln_rows_math()
        broadcast_row(R_MEAN, bcst[:, 0, :], pbc0, bbc0)
        broadcast_row(R_RSTD, bcst[:, 1, :], pbc0, bbc0)

        for dc in range(DC):
            @sec("vector")
            def _(v, dc=dc):
                sv.wait(v)
                _li = v.scalar_tensor_tensor(ctile[:], xo_f[:, dc, :], 0.0,
                                       bcst[:, 0, :], ALU.add, ALU.subtract)
                sv.inc(v.tensor_mul(xo_f[:, dc, :], ctile[:], bcst[:, 1, :]))

            @sec("scalar")
            def _(sc, dc=dc):
                sv.wait(sc)
                _li = sc.activation(ctil2[:], xo_f[:, dc, :], AF.Abs)
                ss.inc(_li)

            @sec("vector")
            def _(v, dc=dc):
                ss.wait(v)
                if dc == 0:
                    sv.inc(v.tensor_copy(mxf[:], ctil2[:]))
                else:
                    sv.inc(v.tensor_max(mxf[:], mxf[:], ctil2[:]))

        xln1_store_mark = [0]

        @sec("scalar")
        def _(sc):
            sv.wait(sc)
            qa.inc(sc.dma_start(out=xln1_scr.rearrange("(c p) t -> p c t", p=128),
                                in_=xo_f[:]), 16)
        xln1_store_mark[0] = qa.n

        for tb in range(8):
            @sec("tensor")
            def _(t, tb=tb):
                sv.wait(t)
                if tb % 4 == 0:
                    btf2.acquire(t)
                _li = t.transpose(ptf2[:, 128 * (tb % 4):128 * (tb % 4 + 1)],
                            mxf[:, 128 * tb:128 * (tb + 1)], iden_f[:])
                st.inc(_li)

            @sec("vector")
            def _(v, tb=tb):
                st.wait(v)
                _li = v.tensor_reduce(qmx[:, tb:tb + 1],
                                ptf2[:, 128 * (tb % 4):128 * (tb % 4 + 1)],
                                AX.X, ALU.max)
                sv.inc(_li)
                if tb % 4 == 3:
                    btf2.release(sv)

        @sec("vector")
        def _(v):
            sv.wait(v)
            vsafe(v, v.tensor_scalar_max(qmx[:, 8:16], qmx[:, 0:8], EPS_Q))
            _li = v.memset(stg[:], 0.0)
            vsafe(v, _li)
            sv.inc(v.tensor_copy(stg[:, 0:8], qmx[:, 8:16]))

        @sec("tensor")
        def _(t):
            sv.wait(t)
            btf2.acquire(t)
            _li = t.transpose(ptf2[:, 0:128], stg[:], iden_f[:])
            st.inc(_li)

        @sec("vector")
        def _(v):
            st.wait(v)
            v.wait_ge(qs.h, qs.n)
            sv.inc(v.tensor_copy(trow[0:8, 0:128], ptf2[0:8, 0:128]))
            btf2.release(sv)

        @sec("tensor")
        def _(t):
            sv.wait(t)
            btf2.acquire(t)
            for half in range(2):
                for cc in range(4):
                    _li = t.matmul(ptf2[32 * half:32 * half + 1, 128 * cc:128 * (cc + 1)],
                                   iden_f[0:8, 4 * half + cc:4 * half + cc + 1],
                                   trow[0:8, 0:128], start=True, stop=True)
            st.inc(_li)

        @sec("vector")
        def _(v):
            st.wait(v)
            _li = v.tensor_copy(row(R_C, 0), ptf2[0:1, 0:512])
            _li = v.tensor_copy(row(R_C, 1), ptf2[32:33, 0:512])
            sv.inc(_li)
            btf2.release(sv)

        for half in range(2):
            @sec("vector")
            def _(v, half=half):
                sv.wait(v)
                _li = v.reciprocal(row(R_A, half), row(R_C, half))
                sv.inc(v.tensor_scalar_mul(row(R_A, half), row(R_A, half), 127.0))

        broadcast_row(R_A, sxb[:], pbc0, bbc0)

        for dc in range(DC):
            @sec("vector")
            def _(v, dc=dc):
                sv.wait(v)
                _li = v.tensor_mul(ctile[:], xo_f[:, dc, :], sxb[:])
                _li = v.tensor_copy(ctil2[:], ctile[:])
                for k in range(7, -1, -1):
                    _li = v.tensor_single_scalar(cbf[:], ctil2[:],
                                           float(2 ** k) - 128.5, ALU.is_ge)
                    _li = v.scalar_tensor_tensor(ctil2[:], cbf[:], -float(2 ** k),
                                           ctil2[:], ALU.mult, ALU.add)
                _li = v.tensor_sub(ctile[:], ctile[:], ctil2[:])
                sv.inc(v.tensor_scalar_add(n_x[:, dc, :], ctile[:], -128.0))

        @sec("vector")
        def _(v):
            sv.wait(v)
            st.wait(v)
            v.memset(stg[:], 0.0)
            for w in range(3):
                vsafe(v, v.tensor_reduce(wabs[:, 192 + w:193 + w],
                                         wabs[:, 64 * w:64 * (w + 1)],
                                         AX.X, ALU.add))
            sv.inc(v.tensor_copy(stg[:, 0:3], wabs[:, 192:195]))

        @sec("tensor")
        def _(t):
            sv.wait(t)
            btf2.acquire(t)
            _li = t.transpose(ptf2[:, 0:128], stg[:], iden_f[:])
            st.inc(_li)

        @sec("vector")
        def _(v):
            st.wait(v)
            sv.inc(v.tensor_copy(trow[0:3, 0:128], ptf2[0:3, 0:128]))
            btf2.release(sv)

        @sec("tensor")
        def _(t):
            sv.wait(t)
            btf2.acquire(t)
            for w in range(3):
                _li = t.matmul(ptf2[0:1, 128 * w:128 * (w + 1)],
                               iden_f[0:8, w:w + 1], trow[0:8, 0:128],
                               start=True, stop=True)
            st.inc(_li)

        @sec("vector")
        def _(v):
            st.wait(v)
            _li = v.tensor_copy(rows[0:1, 512 * R_DD:512 * R_DD + 384],
                                ptf2[0:1, 0:384])
            sv.inc(_li)
            btf2.release(sv)

        @sec("vector")
        def _(v):
            sv.wait(v)
            base = 512 * R_DD
            for w in range(3):
                _li = v.tensor_reduce(scl[0:1, 8 + w:9 + w],
                                rows[0:1, base + 128 * w:base + 128 * (w + 1)],
                                AX.X, ALU.add)
            vsafe(v, _li)
            vsafe(v, v.tensor_scalar(scl[0:1, 11:14], scl[0:1, 8:11],
                                     1.0 / (II * D), EPS_Q, ALU.mult, ALU.max))
            _li = v.tensor_scalar_mul(scl[0:1, 14:17], scl[0:1, 11:14], 0.5)
            vsafe(v, v.tensor_scalar_mul(scl[0:1, 17:20], scl[0:1, 11:14], -0.5))
            vsafe(v, v.tensor_mul(scl[0:1, 20:21], scl[0:1, 11:12], scl[0:1, 11:12]))
            vsafe(v, v.tensor_mul(scl[0:1, 21:22], scl[0:1, 20:21], scl[0:1, 12:13]))

        for i, col in enumerate([14, 17, 15, 18, 16, 19]):
            @sec("tensor")
            def _(t, col=col):
                sv.wait(t)
                bbc0.acquire(t)
                _li = t.matmul(pbc0[:, 0:1], ones_f[0:1, :], scl[0:1, col:col + 1],
                         start=True, stop=True)
                st.inc(_li)

            @sec("vector")
            def _(v, i=i):
                st.wait(v)
                sv.inc(v.tensor_copy(colA[:, 1 + i:2 + i], pbc0[:, 0:1]))
                bbc0.release(sv)

        for half in range(2):
            @sec("vector")
            def _(v, half=half):
                sv.wait(v)
                _li = v.tensor_scalar_mul(row(R_B, half), row(R_C, half), 1.0 / 127.0)
                _li = v.tensor_mul(row(R_DD, half), row(R_B, half), row(R_B, half))
                _li = v.tensor_mul(row(R_DD, half), row(R_DD, half), row(R_B, half))
                sv.inc(v.tensor_scalar(row(R_DD, half), row(R_DD, half),
                                       scl[0:1, 21:22], None, ALU.mult))

        if DEBUG:
            @sec("gpsimd")
            def _(g):
                g.wait_ge(sv.h, sv.n)
                qg.inc(g.dma_start(out=dbg_nx.rearrange("(c p) t -> p c t", p=128),
                                   in_=n_x[:]), 16)
                qg.inc(g.dma_start(out=dbg_misc[:, 48:56], in_=qmx[:, 0:8]), 16)
                qg.inc(g.dma_start(out=dbg_misc[:, 32:39], in_=colA[:, 0:7]), 16)
                g.wait_ge(qg.h, qg.n)

        barrier()

    stAtt.close()

    with ExitStack() as stD:
        wstr = sb(stD, "wstr", (128, 2, DC, 256), F32)
        twg = sb(stD, "twg", (128, DC, 256))
        twu = sb(stD, "twu", (128, DC, 256))
        qscr = sb(stD, "qscr", (128, 4096))
        rg_f = sb(stD, "rg_f", (128, 1024), F32)
        rg2 = sb(stD, "rg2", (128, 1024), F32)
        rawstg = sb(stD, "rawstg", (128, 8, 1024), F32)
        raw2b = sb(stD, "raw2b", (128, 1024))
        absg = sb(stD, "absg", (128, 1024), F32)
        mxrun = sb(stD, "mxrun", (128, 1024), F32)
        trow2 = sb(stD, "trow2", (128, 512), F32)
        pg0 = pt(stD, "pg0", (128, 1024))
        pg1 = pt(stD, "pg1", (128, 1024))
        pu = pt(stD, "pu", (128, 1024))
        pm2 = pt(stD, "pm2", (128, 512))

        wdring = Ring([sem("wd_r0"), sem("wd_r1")], sem("rf5"), 2)
        btg = Buf()
        btu = Buf()
        brb = Buf()
        brg = Buf()
        bpg = [Buf(sem("bpg0")), Buf(sem("bpg1"))]
        bpu = Buf(sem("bpu"))
        bm2 = Buf(sem("bm2"))
        brs = Buf(sem("brs"), depth=2)

        @sec("sync")
        def _(s):
            for ib in range(32):
                for wsrc in (wgT, wuT):
                    slot = wdring.produce(s)
                    wdring.mark(s.dma_start(
                        out=wstr[:, slot, :, :],
                        in_=wsrc.rearrange("(c p) n -> p c n", p=128)[:, :, 256 * ib:256 * (ib + 1)]))

        for ib in range(32):
            for wi, tw, thr in ((0, twg, 1), (1, twu, 3)):
                tile_i = 2 * ib + wi

                @sec("vector")
                def _(v, tile_i=tile_i, tw=tw, thr=thr, wi=wi):
                    wdring.wait_ready(v, tile_i)
                    (btg if wi == 0 else btu).acquire(v)
                    src = wstr[:, tile_i % 2, :, :].rearrange("p c n -> p (c n)")
                    twv = tw[:].rearrange("p c n -> p (c n)")
                    _li = v.tensor_scalar(twv, src, colA[:, thr:thr + 1], None, ALU.is_ge)
                    _li = v.tensor_scalar(qscr[:], src, colA[:, thr + 1:thr + 2], None,
                                    ALU.is_le)
                    sv.inc(v.tensor_sub(twv, twv, qscr[:]))
                    wdring.release(v, sv)

            for c2 in range(2):
                ic = 2 * ib + c2
                par = ic % 2
                pgt, pgb = (pg0, bpg[0]) if par == 0 else (pg1, bpg[1])

                @sec("tensor")
                def _(t, c2=c2, pgt=pgt, pgb=pgb):
                    sv.wait(t)
                    pgb.acquire(t)
                    for k in range(DC):
                        for th in range(2):
                            _li = t.matmul(pgt[:, 512 * th:512 * (th + 1)],
                                     twg[:, k, 128 * c2:128 * (c2 + 1)],
                                     n_x[:, k, 512 * th:512 * (th + 1)],
                                     start=(k == 0), stop=(k == DC - 1))
                    st.inc(_li)
                    if c2 == 1:
                        btg.release(st)
                    bpu.acquire(t)
                    for k in range(DC):
                        for th in range(2):
                            _li = t.matmul(pu[:, 512 * th:512 * (th + 1)],
                                     twu[:, k, 128 * c2:128 * (c2 + 1)],
                                     n_x[:, k, 512 * th:512 * (th + 1)],
                                     start=(k == 0), stop=(k == DC - 1))
                    st.inc(_li)
                    if c2 == 1:
                        btu.release(st)

                @sec("scalar")
                def _(sc, pgt=pgt):
                    st.wait(sc, st.n - 1)
                    brg.acquire(sc)
                    _li = sc.activation(rg_f[:], pgt[:], AF.Relu)
                    _li = sc.activation(rg2[:], rg_f[:], AF.Square)
                    ss.inc(_li)

                @sec("vector")
                def _(v, ic=ic, pgb=pgb):
                    ss.wait(v)
                    pgb.release(ss)
                    st.wait(v)
                    if ic % 4 == 0:
                        brs.acquire(v)
                    _li = v.tensor_mul(rawstg[:, ic % 8, :], rg2[:], pu[:])
                    sv.inc(_li)
                    brg.release(sv)
                    bpu.release(sv)

                if DEBUG:
                    @sec("scalar")
                    def _(sc, ic=ic):
                        sv.wait(sc)
                        qa.inc(sc.dma_start(
                            out=dbg_rg.rearrange("(c p) t -> p c t", p=128)[:, ic:ic + 1, :],
                            in_=rg_f[:]), 16)

                @sec("scalar")
                def _(sc, ic=ic):
                    sv.wait(sc)
                    brb.acquire(sc)
                    _li = sc.activation(raw2b[:], rawstg[:, ic % 8, :], AF.Square)
                    _li = sc.activation(absg[:], rawstg[:, ic % 8, :], AF.Abs,
                                  scale=gcol_s[:, ic:ic + 1])
                    ss.inc(_li)

                @sec("tensor")
                def _(t, ic=ic):
                    ss.wait(t)
                    if ic == 0:
                        bm2.acquire(t)
                    if not NO_M2:
                        for th in range(2):
                            _li = t.matmul(pm2[32 * th:32 * th + 1, 0:512], ones_b[:, 0:1],
                                     raw2b[:, 512 * th:512 * (th + 1)],
                                     start=(ic == 0), stop=(ic == ICN - 1))
                        st.inc(_li)
                    else:
                        st.inc(t.nop())
                    brb.release(st)

                @sec("vector")
                def _(v, ic=ic):
                    ss.wait(v)
                    if ic == 0:
                        sv.inc(v.tensor_copy(mxrun[:], absg[:]))
                    else:
                        sv.inc(v.tensor_max(mxrun[:], mxrun[:], absg[:]))

                if ic % 4 == 3:
                    grp = ic // 4

                    @sec("scalar")
                    def _(sc, ic=ic, grp=grp):
                        sv.wait(sc)
                        c0 = ic - 3
                        qa.inc(sc.dma_start(
                            out=act_scr.rearrange("(c p) t -> p c t", p=128)[:, c0:c0 + 4, :],
                            in_=rawstg[:, (grp % 2) * 4:(grp % 2) * 4 + 4, :]), 16)
                        brs.release(ss, qa)
                if SERIAL_D1:
                    barrier()

        @sec("vector")
        def _(v):
            st.wait(v)
            _li = v.tensor_copy(row(R_S1, 0), pm2[0:1, 0:512])
            _li = v.tensor_copy(row(R_S1, 1), pm2[32:33, 0:512])
            sv.inc(_li)
            bm2.release(sv)

        def qmx2_col(tb):
            return stg[:, 16 + tb:17 + tb]

        for tb in range(8):
            @sec("tensor")
            def _(t, tb=tb):
                sv.wait(t)
                if tb % 4 == 0:
                    bm2.acquire(t)
                _li = t.transpose(pg0[:, 128 * (tb % 4):128 * (tb % 4 + 1)],
                            mxrun[:, 128 * tb:128 * (tb + 1)], iden_f[:])
                st.inc(_li)

            @sec("vector")
            def _(v, tb=tb):
                st.wait(v)
                _li = v.tensor_reduce(qmx2_col(tb), pg0[:, 128 * (tb % 4):128 * (tb % 4 + 1)],
                                AX.X, ALU.max)
                sv.inc(_li)
                if tb % 4 == 3:
                    bm2.release(sv)

        @sec("vector")
        def _(v):
            sv.wait(v)
            _li = v.memset(stg[:, 0:8], 0.0)
            vsafe(v, _li)
            sv.inc(v.tensor_copy(stg[:, 0:8], stg[:, 16:24]))

        @sec("tensor")
        def _(t):
            sv.wait(t)
            bm2.acquire(t)
            _li = t.transpose(pg0[:, 0:128], stg[:], iden_f[:])
            st.inc(_li)

        @sec("vector")
        def _(v):
            st.wait(v)
            v.wait_ge(qs.h, qs.n)
            _li = v.tensor_copy(trow2[0:8, 0:128], pg0[0:8, 0:128])
            sv.inc(_li)
            bm2.release(sv)

        @sec("tensor")
        def _(t):
            sv.wait(t)
            bm2.acquire(t)
            for half in range(2):
                for cc in range(4):
                    _li = t.matmul(pm2[32 * half:32 * half + 1, 128 * cc:128 * (cc + 1)],
                                   iden_f[0:8, 4 * half + cc:4 * half + cc + 1],
                                   trow2[0:8, 0:128], start=True, stop=True)
            st.inc(_li)

        @sec("vector")
        def _(v):
            st.wait(v)
            _li = v.tensor_copy(row(R_S2, 0), pm2[0:1, 0:512])
            _li = v.tensor_copy(row(R_S2, 1), pm2[32:33, 0:512])
            sv.inc(_li)
            bm2.release(sv)

        for half in range(2):
            @sec("vector")
            def _(v, half=half):
                sv.wait(v)
                _li = v.tensor_mul(row(R_A, half), row(R_DD, half), row(R_DD, half))
                _li = v.tensor_scalar_mul(row(R_B, half), row(R_S1, half), 1.0 / II)
                _li = v.tensor_mul(row(R_B, half), row(R_B, half), row(R_A, half))
                _li = v.tensor_scalar_add(row(R_B, half), row(R_B, half), SUB_EPS)
                _li = v.reciprocal(row(R_C, half), row(R_B, half))
                sv.inc(_li)

            @sec("scalar")
            def _(sc, half=half):
                sv.wait(sc)
                _li = sc.activation(row(R_RSTD, half), row(R_C, half), AF.Sqrt)
                ss.inc(_li)

            @sec("vector")
            def _(v, half=half):
                ss.wait(v)
                _li = v.tensor_mul(row(R_A, half), row(R_RSTD, half), row(R_RSTD, half))
                _li = v.tensor_mul(row(R_A, half), row(R_A, half), row(R_B, half))
                _li = v.tensor_scalar(row(R_A, half), row(R_A, half), -0.5, 1.5,
                                ALU.mult, ALU.add)
                _li = v.tensor_mul(row(R_RSTD, half), row(R_RSTD, half), row(R_A, half))
                _li = v.tensor_mul(row(R_MEAN, half), row(R_DD, half), row(R_RSTD, half))
                _li = v.tensor_mul(row(R_B, half), row(R_MEAN, half), row(R_S2, half))
                _li = v.tensor_scalar_max(row(R_B, half), row(R_B, half), EPS_Q)
                _li = v.reciprocal(row(R_C, half), row(R_B, half))
                _li = v.tensor_scalar_mul(row(R_C, half), row(R_C, half), 127.0)
                _li = v.tensor_mul(row(R_A, half), row(R_MEAN, half), row(R_C, half))
                _li = v.tensor_scalar_mul(row(R_B, half), row(R_B, half), 1.0 / 127.0)
                sv.inc(v.tensor_scalar(row(R_B, half), row(R_B, half),
                                       scl[0:1, 13:14], None, ALU.mult))

        barrier()

    with ExitStack() as stD2:
        fs2b = sb(stD2, "fs2b", (128, T), F32)
        dsb = sb(stD2, "dsb", (128, T), F32)
        wdstr = sb(stD2, "wdstr", (128, 8, 512), F32)
        twd = sb(stD2, "twd", (128, 8, 512))
        qscr2 = sb(stD2, "qscr2", (128, 4096))
        n_act = sb(stD2, "n_act", (128, ICN, 512))
        actrd = sb(stD2, "actrd", (128, 4, 512), F32)
        xq_t = sb(stD2, "xq_t", (128, 4, 512), F32)
        y2stg = sb(stD2, "y2stg", (128, 4, 512), F32)
        ct1 = sb(stD2, "ct1", (128, 512), F32)
        ct2 = sb(stD2, "ct2", (128, 512), F32)
        cb1 = sb(stD2, "cb1", (128, 512))
        pd = [pt(stD2, "pd%d" % i, (128, 512)) for i in range(4)]
        px0 = pt(stD2, "px0", (128, 1024))
        px1 = pt(stD2, "px1", (128, 1024))

        bbx = Buf(sem("bbx"))
        by2 = Buf()
        bpd = Buf(sem("bpd"))
        btw = Buf(sem("btw"))
        wd2ring = Ring([sem("wd2_r0")], sem("rf6"), 1)
        acring = Ring([sem("ac_r0")], sem("rf7"), 1)
        xqring = Ring([sem("xq_r0")], sem("rf8"), 1)

        broadcast_row(R_A, fs2b[:], px0, bbx)
        broadcast_row(R_B, dsb[:], px0, bbx)

        for th in range(2):
            for blk in range(16):
                @sec("scalar")
                def _(sc, th=th, blk=blk):
                    acring.produce(sc)
                    acring.mark(sc.dma_start(
                        out=actrd[:],
                        in_=act_scr.rearrange("(c p) t -> p c t", p=128)
                        [:, 4 * blk:4 * (blk + 1), 512 * th:512 * (th + 1)]))

                for c4 in range(4):
                    ic = 4 * blk + c4

                    @sec("vector")
                    def _(v, th=th, blk=blk, c4=c4, ic=ic):
                        acring.wait_ready(v, 16 * th + blk)
                        _li = v.scalar_tensor_tensor(ct1[:], actrd[:, c4, :],
                                               gcol_s[:, ic:ic + 1],
                                               fs2b[:, 512 * th:512 * (th + 1)],
                                               ALU.mult, ALU.mult)
                        _li = v.tensor_copy(ct2[:], ct1[:])
                        for k in range(7, -1, -1):
                            _li = v.tensor_single_scalar(cb1[:], ct2[:],
                                                   float(2 ** k) - 128.5, ALU.is_ge)
                            _li = v.scalar_tensor_tensor(ct2[:], cb1[:], -float(2 ** k),
                                                   ct2[:], ALU.mult, ALU.add)
                        _li = v.tensor_sub(ct1[:], ct1[:], ct2[:])
                        _li = v.tensor_scalar_add(n_act[:, ic, :], ct1[:], -128.0)
                        sv.inc(_li)
                        if c4 == 3:
                            acring.release(v, sv)

            for hq in range(4):
                @sec("sync")
                def _(s, th=th, hq=hq):
                    if th == 0 and hq == 0:
                        s.wait_ge(qa.h, xln1_store_mark[0])
                    xqring.produce(s)
                    xqring.mark(s.dma_start(
                        out=xq_t[:],
                        in_=xln1_scr.rearrange("(c p) t -> p c t", p=128)
                        [:, 4 * hq:4 * (hq + 1), 512 * th:512 * (th + 1)]))

                for ib8 in range(8):
                    tile_i = (th * 4 + hq) * 8 + ib8

                    @sec("sync")
                    def _(s, hq=hq, ib8=ib8):
                        wd2ring.produce(s)
                        wd2ring.mark(s.dma_start(
                            out=wdstr[:],
                            in_=wdT.rearrange("(c p) h -> p c h", p=128)
                            [:, 8 * ib8:8 * (ib8 + 1), 512 * hq:512 * (hq + 1)]))

                    @sec("vector")
                    def _(v, tile_i=tile_i):
                        wd2ring.wait_ready(v, tile_i)
                        btw.acquire(v)
                        src = wdstr[:].rearrange("p c n -> p (c n)")
                        twv = twd[:].rearrange("p c n -> p (c n)")
                        _li = v.tensor_scalar(twv, src, colA[:, 5:6], None, ALU.is_ge)
                        _li = v.tensor_scalar(qscr2[:], src, colA[:, 6:7], None, ALU.is_le)
                        sv.inc(v.tensor_sub(twv, twv, qscr2[:]))
                        wd2ring.release(v, sv)

                    @sec("tensor")
                    def _(t, ib8=ib8):
                        sv.wait(t)
                        if ib8 == 0:
                            bpd.acquire(t)
                        for ic8 in range(8):
                            ic = 8 * ib8 + ic8
                            for hc in range(4):
                                _li = t.matmul(pd[hc][:, 0:512],
                                         twd[:, ic8, 128 * hc:128 * (hc + 1)],
                                         n_act[:, ic, :],
                                         start=(ic == 0), stop=(ic == ICN - 1))
                        st.inc(_li)
                        btw.release(st)

                for hc in range(4):
                    @sec("vector")
                    def _(v, th=th, hq=hq, hc=hc):
                        st.wait(v)
                        xqring.wait_ready(v, th * 4 + hq)
                        if hc == 0:
                            by2.acquire(v)
                        _li = v.tensor_mul(ct1[:], pd[hc][:, 0:512],
                                     dsb[:, 512 * th:512 * (th + 1)])
                        _li = v.tensor_add(y2stg[:, hc, :], ct1[:], xq_t[:, hc, :])
                        sv.inc(_li)
                        if hc == 3:
                            bpd.release(sv)
                            xqring.release(v, sv)

                @sec("scalar")
                def _(sc, th=th, hq=hq):
                    sv.wait(sc)
                    qa.inc(sc.dma_start(
                        out=y2_scr.rearrange("(c p) t -> p c t", p=128)
                        [:, 4 * hq:4 * (hq + 1), 512 * th:512 * (th + 1)],
                        in_=y2stg[:]), 16)
                    by2.release(qa)

        barrier()

    with ExitStack() as stE:
        y2f = sb(stE, "y2f", (128, DC, T), F32)
        cbf1k = sb(stE, "cbf1k", (128, 1024))
        cbf1k2 = sb(stE, "cbf1k2", (128, 1024))
        bcst2 = sb(stE, "bcst2", (128, 2, 1024), F32)
        ytm = sb(stE, "ytm", (128, 2, 2048), F32)
        ctk = sb(stE, "ctk", (128, 1024), F32)
        trow3 = sb(stE, "trow3", (128, 512), F32)
        pe0 = pt(stE, "pe0", (128, 512))
        pe1 = pt(stE, "pe1", (128, 512))
        pe2 = pt(stE, "pe2", (128, 512))
        pe3 = pt(stE, "pe3", (128, 512))

        brw2 = Buf(sem("brw2"))
        bce = Buf()
        bbx2 = Buf(sem("bbx2"))
        bpe = [Buf(sem("bpe0")), Buf(sem("bpe1"))]

        @sec("sync")
        def _(s):
            qa.wait(s)
            qs.inc(s.dma_start(out=y2f[:],
                               in_=y2_scr.rearrange("(c p) t -> p c t", p=128)), 16)
        y2_mark = qs.n

        for dc in range(DC):
            @sec("scalar")
            def _(sc, dc=dc):
                if dc == 0:
                    sc.wait_ge(qs.h, y2_mark)
                bce.acquire(sc)
                _li = sc.copy(cbf1k[:], y2f[:, dc, :])
                _li = sc.activation(cbf1k2[:], y2f[:, dc, :], AF.Square)
                ss.inc(_li)

            @sec("tensor")
            def _(t, dc=dc):
                ss.wait(t)
                if dc == 0:
                    brw2.acquire(t)
                for th in range(2):
                    _li = t.matmul(pe0[32 * th:32 * th + 1, 0:512], ones_b[:, 0:1],
                             cbf1k[:, 512 * th:512 * (th + 1)],
                             start=(dc == 0), stop=(dc == DC - 1))
                    _li = t.matmul(pe1[32 * th:32 * th + 1, 0:512], ones_b[:, 0:1],
                             cbf1k2[:, 512 * th:512 * (th + 1)],
                             start=(dc == 0), stop=(dc == DC - 1))
                st.inc(_li)
                bce.release(st)

        @sec("vector")
        def _(v):
            st.wait(v)
            _li = v.tensor_copy(row(R_S1, 0), pe0[0:1, 0:512])
            _li = v.tensor_copy(row(R_S1, 1), pe0[32:33, 0:512])
            _li = v.tensor_copy(row(R_S2, 0), pe1[0:1, 0:512])
            _li = v.tensor_copy(row(R_S2, 1), pe1[32:33, 0:512])
            sv.inc(_li)
            brw2.release(sv)

        for half in range(2):
            @sec("vector")
            def _(v, half=half):
                sv.wait(v)
                _li = v.tensor_scalar_mul(row(R_MEAN, half), row(R_S1, half), 1.0 / D)
                _li = v.tensor_mul(row(R_A, half), row(R_MEAN, half), row(R_MEAN, half))
                _li = v.scalar_tensor_tensor(row(R_B, half), row(R_S2, half), 1.0 / D,
                                       row(R_A, half), ALU.mult, ALU.subtract)
                _li = v.tensor_scalar_add(row(R_B, half), row(R_B, half), LN_EPS)
                _li = v.reciprocal(row(R_C, half), row(R_B, half))
                sv.inc(_li)

            @sec("scalar")
            def _(sc, half=half):
                sv.wait(sc)
                _li = sc.activation(row(R_RSTD, half), row(R_C, half), AF.Sqrt)
                ss.inc(_li)

            @sec("vector")
            def _(v, half=half):
                ss.wait(v)
                _li = v.tensor_mul(row(R_A, half), row(R_RSTD, half), row(R_RSTD, half))
                _li = v.tensor_mul(row(R_A, half), row(R_A, half), row(R_B, half))
                _li = v.tensor_scalar(row(R_A, half), row(R_A, half), -0.5, 1.5,
                                ALU.mult, ALU.add)
                _li = v.tensor_mul(row(R_RSTD, half), row(R_RSTD, half), row(R_A, half))
                sv.inc(_li)

        broadcast_row(R_MEAN, bcst2[:, 0, :], pe2, bbx2)
        broadcast_row(R_RSTD, bcst2[:, 1, :], pe2, bbx2)

        for dc in range(DC):
            @sec("vector")
            def _(v, dc=dc):
                sv.wait(v)
                _li = v.scalar_tensor_tensor(ctk[:], y2f[:, dc, :], 0.0,
                                       bcst2[:, 0, :], ALU.add, ALU.subtract)
                sv.inc(v.tensor_mul(y2f[:, dc, :], ctk[:], bcst2[:, 1, :]))

        ytm_mark = {}
        for tb in range(8):
            for dc in range(DC):
                par = dc % 2
                pxt = pe2 if par == 0 else pe3
                pxb = bpe[par]

                @sec("tensor")
                def _(t, tb=tb, dc=dc, pxt=pxt, pxb=pxb):
                    sv.wait(t)
                    pxb.acquire(t)
                    _li = t.transpose(pxt[:, 0:128], y2f[:, dc, 128 * tb:128 * (tb + 1)],
                                iden_f[:])
                    st.inc(_li)

                @sec("vector")
                def _(v, tb=tb, dc=dc, pxt=pxt, pxb=pxb):
                    st.wait(v)
                    if dc == 0 and tb >= 2:
                        v.wait_ge(qa.h, ytm_mark[tb - 2])
                    _li = v.tensor_copy(ytm[:, tb % 2, 128 * dc:128 * (dc + 1)],
                                      pxt[:, 0:128])
                    sv.inc(_li)
                    pxb.release(sv)

            @sec("scalar")
            def _(sc, tb=tb):
                sv.wait(sc)
                qa.inc(sc.dma_start(out=y_out[128 * tb:128 * (tb + 1), :],
                                    in_=ytm[:, tb % 2, :]), 16)
            ytm_mark[tb] = qa.n
        @sec("scalar")
        def _(sc):
            qa.wait(sc)

    stRows.close()
    stQ.close()
    st0.close()
    return nc


_NC_CACHE = None


def _get_nc():
    global _NC_CACHE
    if _NC_CACHE is None:
        _NC_CACHE = build_program()
    return _NC_CACHE


def _host_tables():
    inv_freq = 1.0 / (10000.0 ** (np.arange(0, HD, 2, dtype=np.float64) / HD))

    def rope_expand(positions):
        ang = positions[:, None].astype(np.float64) * inv_freq[None, :]
        c = np.cos(ang)
        s = np.sin(ang)
        ce = np.empty((HD, len(positions)), np.float64)
        se = np.empty((HD, len(positions)), np.float64)
        ce[0::2] = c.T
        ce[1::2] = c.T
        se[0::2] = -s.T
        se[1::2] = s.T
        cf = np.vstack([ce, ce]).astype(np.float32)
        sf = np.vstack([se, se]).astype(np.float32)
        return np.ascontiguousarray(cf), np.ascontiguousarray(sf)

    cf_full, sf_full = rope_expand(np.arange(S))
    perm = np.zeros((128, 128), np.float32)
    for i in range(0, 128, 2):
        perm[i, i + 1] = 1.0
        perm[i + 1, i] = 1.0
    iden = np.eye(128, dtype=np.float32)
    ones = np.ones((128, 128), np.float32)
    e2 = np.zeros((128, 2), np.float32)
    e2[0:64, 0] = 1.0
    e2[64:128, 1] = 1.0
    dm = np.zeros((128, 2048), np.float32)
    q = np.arange(512)
    for kb in range(4):
        k_abs = 128 * kb + np.arange(128)
        dm[:, 512 * kb:512 * (kb + 1)] = (k_abs[:, None] <= q[None, :])
    return cf_full, sf_full, perm, iden, ones, e2, dm, rope_expand


def kernel(L0_attn_x, L0_attn_wq, L0_attn_wk, L0_attn_wv, L0_attn_wo,
           L0_ffn_w_gate, L0_ffn_w_up, L0_ffn_w_down, L0_ffn_subln_weight):
    x = np.ascontiguousarray(np.asarray(L0_attn_x, dtype=np.float32))
    wq = np.ascontiguousarray(np.asarray(L0_attn_wq, dtype=np.float32))
    wk = np.ascontiguousarray(np.asarray(L0_attn_wk, dtype=np.float32))
    wv = np.ascontiguousarray(np.asarray(L0_attn_wv, dtype=np.float32))
    wo = np.ascontiguousarray(np.asarray(L0_attn_wo, dtype=np.float32))
    wg = np.asarray(L0_ffn_w_gate, dtype=np.float32)
    wu = np.asarray(L0_ffn_w_up, dtype=np.float32)
    wd = np.asarray(L0_ffn_w_down, dtype=np.float32)
    g = np.asarray(L0_ffn_subln_weight, dtype=np.float32)

    b0heads = [h for h in range(NH) if (h // 4) % 2 == 0]
    b64heads = [h for h in range(NH) if (h // 4) % 2 == 1]
    perm_a = np.empty(D, np.int64)
    for i in range(16):
        perm_a[128 * i:128 * i + 64] = 64 * b0heads[i] + np.arange(64)
        perm_a[128 * i + 64:128 * (i + 1)] = 64 * b64heads[i] + np.arange(64)
    wq = np.ascontiguousarray(wq[:, perm_a])
    wo = np.ascontiguousarray(wo[perm_a, :])
    xT = [np.ascontiguousarray(x[b].T) for b in range(2)]
    wgT_h = np.ascontiguousarray(wg.T)
    wuT_h = np.ascontiguousarray(wu.T)
    wdT_h = np.ascontiguousarray(wd.T)
    g_col_h = np.ascontiguousarray(g.reshape(ICN, 128).T)

    cf_full, sf_full, perm, iden, ones, e2, dm, rope_expand = _host_tables()

    in_maps = []
    meta = []
    for c in range(8):
        b = c // 4
        j = c % 4
        pA, pB = j, 4 + j
        meta.append((b, pA, pB))
        pos = np.concatenate([np.arange(512 * pA, 512 * (pA + 1)),
                              np.arange(512 * pB, 512 * (pB + 1))])
        xo = np.ascontiguousarray(
            np.concatenate([xT[b][:, 512 * pA:512 * (pA + 1)],
                            xT[b][:, 512 * pB:512 * (pB + 1)]], axis=1))
        co, so = rope_expand(pos)
        pat = np.zeros((128, NGT), np.float32)
        for s_, p, ngf, base in ((0, pA, 6, 0), (1, pB, 14, NG0)):
            need = 2 * p
            for gg_ in range(ngf):
                if gg_ >= need:
                    pat[:, base + gg_] = NEG_BIG
        in_maps.append({
            "x_own": xo, "x_full": xT[b],
            "wq": wq, "wk": wk, "wv": wv, "wo": wo,
            "wgT": wgT_h, "wuT": wuT_h, "wdT": wdT_h, "g_col": g_col_h,
            "rope_cf": cf_full, "rope_sf": sf_full,
            "rope_co": co, "rope_so": so,
            "perm": perm, "iden": iden, "ones": ones, "e2": e2,
            "dmask": dm, "pat": pat,
        })

    nc = _get_nc()
    res = run_bass_kernel_spmd(nc, in_maps, core_ids=list(range(8)))

    out = np.empty((2, S, D), np.float32)
    for c in range(8):
        b, pA, pB = meta[c]
        yc = np.asarray(res.results[c]["y"])
        out[b, 512 * pA:512 * (pA + 1), :] = yc[0:512]
        out[b, 512 * pB:512 * (pB + 1), :] = yc[512:1024]
    return out


# version 07
# speedup vs baseline: 1.0013x; 1.0013x over previous
import numpy as np
from contextlib import ExitStack

import concourse.bass as bass
import concourse.mybir as mybir
from concourse.bass_utils import run_bass_kernel_spmd

F32 = mybir.dt.float32
BF16 = mybir.dt.bfloat16
AF = mybir.ActivationFunctionType
ALU = mybir.AluOpType
AX = mybir.AxisListType

D = 2048
S = 4096
T = 1024
NH = 32
NKV = 8
HD = 64
II = 8192
DC = D // 128
ICN = II // 128
LN_EPS = 1e-5
SUB_EPS = 1e-6
EPS_Q = 1e-5
NEG_BIG = -1.0e30
NG0, NG1 = 8, 16
NGT = NG0 + NG1


class Ctr:
    def __init__(self, h):
        self.h = h
        self.n = 0

    def inc(self, inst, k=1):
        inst.then_inc(self.h, k)
        self.n += k
        return inst

    def wait(self, eng, n=None):
        eng.wait_ge(self.h, self.n if n is None else n)


class Buf:

    def __init__(self, ctr=None, depth=1):
        self.acq = 0
        self.depth = depth
        self.marks = []

    def acquire(self, eng):
        idx = self.acq - self.depth
        if 0 <= idx < len(self.marks):
            for c, n in self.marks[idx]:
                eng.wait_ge(c.h, n)
        elif idx >= len(self.marks):
            raise RuntimeError("Buf.acquire before matching release")
        self.acq += 1

    def release(self, *ctrs):
        self.marks.append([(c, c.n) for c in ctrs])


class Ring:

    def __init__(self, rdys, fre, n):
        self.rdys = rdys
        self.fre = fre
        self.n = n
        self.p = 0

    def produce(self, eng):
        slot = self.p % self.n
        if self.p >= self.n:
            eng.wait_ge(self.fre.h, self.p - self.n + 1)
        self.p += 1
        return slot

    def mark(self, inst):
        slot = (self.p - 1) % self.n
        self.rdys[slot].inc(inst, 16)

    def wait_ready(self, eng, i):
        slot = i % self.n
        eng.wait_ge(self.rdys[slot].h, 16 * (i // self.n + 1))

    def release(self, eng, ctr):
        eng.wait_ge(ctr.h, ctr.n)
        self.fre.inc(eng.nop())


DEBUG = False
SERIAL_D1 = False
NO_M2 = False


def build_program():
    nc = bass.Bass("TRN2", target_bir_lowering=False, detect_race_conditions=False)
    dkind = "ExternalOutput" if DEBUG else "Internal"

    def din(name, shape):
        return nc.dram_tensor(name, list(shape), F32, kind="ExternalInput").ap()

    x_own = din("x_own", (D, T))
    x_full = din("x_full", (D, S))
    wq_d = din("wq", (D, D))
    wk_d = din("wk", (D, NKV * HD))
    wv_d = din("wv", (D, NKV * HD))
    wo_d = din("wo", (D, D))
    wgT = din("wgT", (D, II))
    wuT = din("wuT", (D, II))
    wdT = din("wdT", (II, D))
    g_col = din("g_col", (128, ICN))
    rope_cf = din("rope_cf", (128, S))
    rope_sf = din("rope_sf", (128, S))
    rope_co = din("rope_co", (128, T))
    rope_so = din("rope_so", (128, T))
    perm_in = din("perm", (128, 128))
    iden_in = din("iden", (128, 128))
    ones_in = din("ones", (128, 128))
    e2_in = din("e2", (128, 2))
    dmask_in = din("dmask", (128, 2048))
    pat_in = din("pat", (128, NGT))
    y_out = nc.dram_tensor("y", [T, D], F32, kind="ExternalOutput").ap()
    act_scr = nc.dram_tensor("act_scr", [II, T], F32, kind=dkind).ap()
    xln1_scr = nc.dram_tensor("xln1_scr", [D, T], F32, kind=dkind).ap()
    y2_scr = nc.dram_tensor("y2_scr", [D, T], F32, kind=dkind).ap()
    if DEBUG:
        dbg_att = nc.dram_tensor("dbg_att", [D, T], F32, kind="ExternalOutput").ap()
        dbg_misc = nc.dram_tensor("dbg_misc", [128, 64], F32, kind="ExternalOutput").ap()
        dbg_nx = nc.dram_tensor("dbg_nx", [D, T], F32, kind="ExternalOutput").ap()
        dbg_q = nc.dram_tensor("dbg_q", [D, T], F32, kind="ExternalOutput").ap()
        dbg_rg = nc.dram_tensor("dbg_rg", [II, T], F32, kind="ExternalOutput").ap()
        dbg_k = nc.dram_tensor("dbg_k", [512, S], F32, kind="ExternalOutput").ap()
        dbg_v = nc.dram_tensor("dbg_v", [128, 32 * NKV * 65], F32, kind="ExternalOutput").ap()

    st0 = ExitStack()

    def sb(stk, name, shape, dt=BF16, side=None):
        return stk.enter_context(nc.sbuf_tensor(name, list(shape), dt, side=side))

    def pt(stk, name, shape, dt=F32):
        return stk.enter_context(nc.psum_tensor(name, list(shape), dt))

    def sem(name):
        return Ctr(st0.enter_context(nc.semaphore(name)))

    iden_b = sb(st0, "iden_b", (128, 128))
    iden_f = sb(st0, "iden_f", (128, 128), F32)
    ones_f = sb(st0, "ones_f", (128, 128), F32)
    ones_b = sb(st0, "ones_b", (128, 128))
    pat_s = sb(st0, "pat_s", (128, NGT), F32)
    gcol_s = sb(st0, "gcol_s", (128, ICN), F32)
    biastbl = sb(st0, "biastbl", (128, NGT), F32)
    colA = sb(st0, "colA", (128, 16), F32)
    wabs = sb(st0, "wabs", (128, 200), F32)
    stg = sb(st0, "stg", (128, 128), F32)
    scl = sb(st0, "scl", (1, 128), F32)
    rows = None

    def row(q, half):
        return rows[32 * half:32 * half + 1, 512 * q:512 * (q + 1)]

    R_S1, R_S2, R_MEAN, R_RSTD, R_A, R_B, R_C, R_DD = range(8)

    qs = sem("qs")
    qg = sem("qg")
    qa = sem("qa")
    st = sem("st")
    sv = sem("sv")
    ss = sem("ss")
    bar = sem("bar")

    block = st0.enter_context(nc.Block())
    ENGS = ["tensor", "vector", "scalar", "gpsimd", "sync"]

    def sec(name):
        return getattr(block, name)

    def vsafe(v, inst):
        sv.inc(inst)
        v.wait_ge(sv.h, sv.n)
        return inst

    def barrier():
        for e in ENGS:
            @sec(e)
            def _(eng):
                eng.nop().then_inc(bar.h, 1)
        bar.n += 5
        tgt = bar.n
        for e in ENGS:
            @sec(e)
            def _(eng):
                eng.wait_ge(bar.h, tgt)

    def broadcast_row(slotq, dst_tile, pbank, pbuf):
        for half in range(2):
            @sec("tensor")
            def _(t, half=half):
                sv.wait(t)
                ss.wait(t)
                t.wait_ge(qs.h, qs.n)
                pbuf.acquire(t)
                _li = t.matmul(pbank[:, 0:512], ones_f[32 * half:32 * half + 1, :],
                         row(slotq, half), start=True, stop=True)
                st.inc(_li)

            @sec("vector")
            def _(v, half=half):
                st.wait(v)
                _li = v.tensor_copy(dst_tile[:, 512 * half:512 * (half + 1)],
                                  pbank[:, 0:512])
                sv.inc(_li)
                pbuf.release(sv)

    @sec("gpsimd")
    def _(g):
        for dst, src in [(iden_b[:], iden_in), (ones_b[:], ones_in)]:
            qg.inc(g.dma_start(out=dst, in_=src), 16)

    @sec("sync")
    def _(s):
        for dst, src in [(iden_f[:], iden_in), (ones_f[:], ones_in),
                         (pat_s[:], pat_in), (gcol_s[:], g_col)]:
            qs.inc(s.dma_start(out=dst, in_=src), 16)

    barrier()

    stQ = ExitStack()
    Qf = sb(stQ, "Qf", (128, DC, T))
    n_x = Qf

    stKV = ExitStack()
    Kf = sb(stKV, "Kf", (128, 4, S))
    Vt = sb(stKV, "Vt", (128, 32, NKV, 65))
    Kd = sb(stKV, "Kd", (128, 4, T))
    Vd = sb(stKV, "Vd", (128, 8, NKV, 65))

    with ExitStack() as stT:
        perm_b = sb(stT, "perm_b", (128, 128))
        e2_b = sb(stT, "e2_b", (128, 2))
        cf_b = sb(stT, "cf_b", (128, S))
        sf_b = sb(stT, "sf_b", (128, S))
        co_b = sb(stT, "co_b", (128, T))
        so_b = sb(stT, "so_b", (128, T))
        xstr = sb(stT, "xstr", (128, 2, DC, 256))
        wk_s = sb(stT, "wk_s", (128, DC, 512))
        wv_s = sb(stT, "wv_s", (128, DC, 512))
        wq_s = sb(stT, "wq_s", (128, 2, DC, 256))
        kpre = sb(stT, "kpre", (128, 2, 256))
        ktmp = sb(stT, "ktmp", (128, 256))
        ktmp2 = sb(stT, "ktmp2", (128, 256))
        sqt = sb(stT, "sqt", (128, 1024))
        pp0 = pt(stT, "pp0", (128, 512))
        pp1 = pt(stT, "pp1", (128, 512))
        pr0 = pt(stT, "pr0", (128, 256))
        pr1 = pt(stT, "pr1", (128, 256))
        pnm = pt(stT, "pnm", (128, 512))
        ptf = pt(stT, "ptf", (128, 128))

        xring = Ring([sem("xr_r0"), sem("xr_r1")], sem("rf1"), 2)
        wqring = Ring([sem("wq_r0"), sem("wq_r1")], sem("rf2"), 2)
        bpp = [Buf(sem("bpp0")), Buf(sem("bpp1"))]
        bpr = [Buf(sem("bpr0")), Buf(sem("bpr1"))]
        bkp = [Buf(sem("bkp0")), Buf(sem("bkp1"))]
        bnm = Buf(sem("bnm"))
        btf = Buf(sem("btf"))

        @sec("gpsimd")
        def _(g):
            for dst, src in [(perm_b[:], perm_in), (e2_b[:], e2_in),
                             (cf_b[:], rope_cf), (sf_b[:], rope_sf),
                             (co_b[:], rope_co), (so_b[:], rope_so)]:
                qg.inc(g.dma_start(out=dst, in_=src), 16)
            qg.inc(g.dma_start(out=wk_s[:],
                               in_=wk_d.rearrange("(c p) n -> p c n", p=128)), 16)
            qg.inc(g.dma_start(out=wv_s[:],
                               in_=wv_d.rearrange("(c p) n -> p c n", p=128)), 16)
        kvw_mark = qg.n

        NT_KV = 20

        def xsrc(i):
            if i < 16:
                return x_full.rearrange("(c p) t -> p c t", p=128)[:, :, 256 * i:256 * (i + 1)]
            j = i - 16
            return x_own.rearrange("(c p) t -> p c t", p=128)[:, :, 256 * j:256 * (j + 1)]

        @sec("gpsimd")
        def _(g):
            for i in range(NT_KV):
                slot = xring.produce(g)
                xring.mark(g.dma_start(out=xstr[:, slot, :, :], in_=xsrc(i)))

        pcnt = 0
        for ti in range(NT_KV):
            own = ti >= 16
            toff = 256 * (ti - 16) if own else 256 * ti
            cf, sf_ = (co_b, so_b) if own else (cf_b, sf_b)
            kdst = Kd if own else Kf
            for c in range(4):
                par = pcnt % 2
                pcnt += 1
                pbt, pb = (pp0, bpp[0]) if par == 0 else (pp1, bpp[1])
                prt, prb = (pr0, bpr[0]) if par == 0 else (pr1, bpr[1])
                kpb = bkp[par]

                @sec("tensor")
                def _(t, ti=ti, c=c, pbt=pbt, pb=pb):
                    xring.wait_ready(t, ti)
                    if ti == 0 and c == 0:
                        t.wait_ge(qg.h, kvw_mark)
                    pb.acquire(t)
                    for k in range(DC):
                        _li = t.matmul(pbt[:, 0:256], wk_s[:, k, 128 * c:128 * (c + 1)],
                                 xstr[:, ti % 2, k, :],
                                 start=(k == 0), stop=(k == DC - 1))
                    st.inc(_li)

                @sec("scalar")
                def _(sc, pbt=pbt, pb=pb, kpb=kpb, par=par):
                    st.wait(sc)
                    kpb.acquire(sc)
                    _li = sc.copy(kpre[:, par, :], pbt[:, 0:256])
                    ss.inc(_li)
                    pb.release(ss)

                @sec("tensor")
                def _(t, prt=prt, prb=prb, par=par):
                    ss.wait(t)
                    prb.acquire(t)
                    _li = t.matmul(prt[:], perm_b[:], kpre[:, par, :], start=True, stop=True)
                    st.inc(_li)

                @sec("vector")
                def _(v, c=c, prt=prt, prb=prb, kpb=kpb, par=par,
                      kdst=kdst, cf=cf, sf_=sf_, toff=toff):
                    st.wait(v)
                    _li = v.tensor_mul(ktmp[:], kpre[:, par, :], cf[:, toff:toff + 256])
                    _li = v.tensor_mul(ktmp2[:], prt[:], sf_[:, toff:toff + 256])
                    _li = v.tensor_add(kdst[:, c, toff:toff + 256], ktmp[:], ktmp2[:])
                    sv.inc(_li)
                    prb.release(sv)
                    kpb.release(sv)

            for m in range(2):
                par = pcnt % 2
                pcnt += 1
                pbt, pb = (pp0, bpp[0]) if par == 0 else (pp1, bpp[1])
                tc = (toff + 128 * m) // 128

                @sec("tensor")
                def _(t, ti=ti, m=m, pbt=pbt, pb=pb):
                    pb.acquire(t)
                    for k in range(DC):
                        _li = t.matmul(pbt[:], xstr[:, ti % 2, k, 128 * m:128 * (m + 1)],
                                 wv_s[:, k, :],
                                 start=(k == 0), stop=(k == DC - 1))
                    st.inc(_li)

                @sec("vector")
                def _(v, pbt=pbt, pb=pb, tc=tc, own=own):
                    st.wait(v)
                    dst = Vd if own else Vt
                    sv.inc(v.tensor_copy(
                        dst[:, tc, :, 0:64],
                        pbt[:].rearrange("p (h n) -> p h n", h=NKV)))
                    pb.release(sv)

            @sec("tensor")
            def _(t):
                sv.wait(t)
                xring.release(t, st)

        @sec("vector")
        def _(v):
            sv.wait(v)
            _li = v.memset(Vt[:, :, :, 64:65], 1.0)
            _li = v.memset(Vd[:, :, :, 64:65], 1.0)
            sv.inc(_li)

        kv_done_sv = sv.n

        @sec("gpsimd")
        def _(g):
            g.wait_ge(sv.h, kv_done_sv)
            for e8 in range(8):
                slot = wqring.produce(g)
                wqring.mark(g.dma_start(
                    out=wq_s[:, slot, :, :],
                    in_=wq_d.rearrange("(c p) n -> p c n", p=128)[:, :, 256 * e8:256 * (e8 + 1)]))
                for ot in range(4):
                    slot = xring.produce(g)
                    xring.mark(g.dma_start(out=xstr[:, slot, :, :],
                                           in_=xsrc(16 + ot)))

        for e8 in range(8):
            for ot in range(4):
                tidx = NT_KV + 4 * e8 + ot
                toff = 256 * ot
                for q2 in range(2):
                    qc = 2 * e8 + q2
                    par = pcnt % 2
                    pcnt += 1
                    pbt, pb = (pp0, bpp[0]) if par == 0 else (pp1, bpp[1])
                    prt, prb = (pr0, bpr[0]) if par == 0 else (pr1, bpr[1])
                    kpb = bkp[par]

                    @sec("tensor")
                    def _(t, e8=e8, q2=q2, tidx=tidx, pbt=pbt, pb=pb):
                        xring.wait_ready(t, tidx)
                        wqring.wait_ready(t, e8)
                        pb.acquire(t)
                        for k in range(DC):
                            _li = t.matmul(pbt[:, 0:256],
                                     wq_s[:, e8 % 2, k, 128 * q2:128 * (q2 + 1)],
                                     xstr[:, tidx % 2, k, :],
                                     start=(k == 0), stop=(k == DC - 1))
                        st.inc(_li)

                    @sec("scalar")
                    def _(sc, pbt=pbt, pb=pb, kpb=kpb, par=par):
                        st.wait(sc)
                        kpb.acquire(sc)
                        _li = sc.copy(kpre[:, par, :], pbt[:, 0:256])
                        ss.inc(_li)
                        pb.release(ss)

                    @sec("tensor")
                    def _(t, prt=prt, prb=prb, par=par):
                        ss.wait(t)
                        prb.acquire(t)
                        _li = t.matmul(prt[:], perm_b[:], kpre[:, par, :],
                                 start=True, stop=True)
                        st.inc(_li)

                    @sec("vector")
                    def _(v, qc=qc, toff=toff, prt=prt, prb=prb, kpb=kpb, par=par):
                        st.wait(v)
                        _li = v.tensor_mul(ktmp[:], kpre[:, par, :], co_b[:, toff:toff + 256])
                        _li = v.tensor_mul(ktmp2[:], prt[:], so_b[:, toff:toff + 256])
                        _li = v.tensor_add(Qf[:, qc, toff:toff + 256], ktmp[:], ktmp2[:])
                        sv.inc(_li)
                        prb.release(sv)
                        kpb.release(sv)

                @sec("tensor")
                def _(t):
                    sv.wait(t)
                    xring.release(t, st)

            @sec("tensor")
            def _(t):
                wqring.release(t, st)

        ncol = 0

        bsq = Buf()

        def normmax_tiles(tiles):
            nonlocal ncol
            first = ncol
            for ap1024 in tiles:
                @sec("scalar")
                def _(sc, ap1024=ap1024):
                    sv.wait(sc)
                    bsq.acquire(sc)
                    _li = sc.activation(sqt[:], ap1024, AF.Square)
                    ss.inc(_li)

                for j in range(2):
                    @sec("tensor")
                    def _(t, j=j):
                        ss.wait(t)
                        bnm.acquire(t)
                        _li = t.matmul(pnm[0:2, 0:512], e2_b[:],
                                 sqt[:, 512 * j:512 * (j + 1)],
                                 start=True, stop=True)
                        st.inc(_li)
                        if j == 1:
                            bsq.release(st)

                    @sec("vector")
                    def _(v, ncol=ncol):
                        st.wait(v)
                        _li = v.tensor_reduce(stg[0:2, 2 + ncol:3 + ncol], pnm[0:2, 0:512],
                                        AX.X, ALU.max)
                        sv.inc(_li)
                        bnm.release(sv)
                    ncol += 1
            return first

        @sec("vector")
        def _(v):
            sv.wait(v)
            _li = v.memset(stg[:], 0.0)
            sv.inc(_li)

        kfirst = normmax_tiles([Kf[:, c, 1024 * n:1024 * (n + 1)]
                                for c in range(4) for n in range(4)])
        qfirst = normmax_tiles([Qf[:, qc, :] for qc in range(DC)])

        for which, first, last, out_c in ((0, kfirst, qfirst, 0), (1, qfirst, ncol, 1)):
            @sec("vector")
            def _(v, first=first, last=last):
                sv.wait(v)
                vsafe(v, v.tensor_reduce(stg[0:2, 0:1], stg[0:2, 2 + first:2 + last],
                                AX.X, ALU.max))

            @sec("tensor")
            def _(t):
                sv.wait(t)
                btf.acquire(t)
                _li = t.transpose(ptf[:], stg[:], iden_f[:])
                st.inc(_li)

            @sec("vector")
            def _(v, out_c=out_c):
                st.wait(v)
                vsafe(v, v.tensor_reduce(scl[0:1, out_c:out_c + 1], ptf[0:1, 0:2],
                                AX.X, ALU.max))
                btf.release(sv)

        @sec("vector")
        def _(v):
            sv.wait(v)
            vsafe(v, v.tensor_mul(scl[0:1, 2:3], scl[0:1, 0:1], scl[0:1, 1:2]))

        @sec("scalar")
        def _(sc):
            sv.wait(sc)
            _li = sc.activation(scl[0:1, 3:4], scl[0:1, 2:3], AF.Sqrt)
            ss.inc(_li)

        @sec("vector")
        def _(v):
            ss.wait(v)
            vsafe(v, v.tensor_scalar_mul(scl[0:1, 4:5], scl[0:1, 3:4], -0.126))

        @sec("tensor")
        def _(t):
            sv.wait(t)
            bnm.acquire(t)
            _li = t.matmul(pnm[:, 0:1], ones_f[0:1, :], scl[0:1, 4:5], start=True, stop=True)
            st.inc(_li)

        @sec("vector")
        def _(v):
            st.wait(v)
            _li = v.tensor_copy(colA[:, 0:1], pnm[:, 0:1])
            sv.inc(_li)
            bnm.release(sv)
            v.wait_ge(sv.h, sv.n)
            sv.inc(v.tensor_scalar(biastbl[:], pat_s[:], colA[:, 0:1], None, ALU.add))

        if DEBUG:
            @sec("gpsimd")
            def _(g):
                g.wait_ge(sv.h, sv.n)
                qg.inc(g.dma_start(out=dbg_q.rearrange("(c p) t -> p c t", p=128),
                                   in_=Qf[:]), 16)
                qg.inc(g.dma_start(out=dbg_k.rearrange("(c p) t -> p c t", p=128),
                                   in_=Kf[:]), 16)
                qg.inc(g.dma_start(out=dbg_v[:],
                                   in_=Vt[:].rearrange("p a h n -> p (a h n)")), 16)
                qg.inc(g.dma_start(out=dbg_misc[:, 0:NGT], in_=biastbl[:]), 16)
                qg.inc(g.dma_start(out=dbg_misc[:, 56:64],
                                   in_=stg[:, 0:8]), 16)
                g.wait_ge(qg.h, qg.n)

            @sec("sync")
            def _(s):
                ss.wait(s)
                sv.wait(s)
                qs.inc(s.dma_start(out=dbg_misc[0:1, 40:45], in_=scl[0:1, 0:5]), 16)

        barrier()

    stAtt = ExitStack()
    att = sb(stAtt, "att", (128, DC, T), side="right")

    with ExitStack() as stB:
        dmask_b = sb(stB, "dmask_b", (128, 2048))
        ebuf = sb(stB, "ebuf", (128, 2, 1024))
        avsb = sb(stB, "avsb", (64, 512))
        recb = sb(stB, "recb", (1, 512))
        recf = sb(stB, "recf", (1, 512), F32)
        w1str = sb(stB, "w1str", (128, 2, 2048), F32)
        psc0 = pt(stB, "psc0", (128, 1024))
        psc1 = pt(stB, "psc1", (128, 1024))
        pav = pt(stB, "pav", (65, 512))
        pbc = pt(stB, "pbc", (128, 512))

        w1ring = Ring([sem("w1_r0"), sem("w1_r1")], sem("rf3"), 2)
        bsc = [Buf(sem("bsc0")), Buf(sem("bsc1"))]
        beb = [Buf(sem("beb0")), Buf(sem("beb1"))]
        bav = Buf(sem("bav"))
        bbc = Buf(sem("bbc"))

        @sec("gpsimd")
        def _(g):
            qg.inc(g.dma_start(out=dmask_b[:], in_=dmask_in), 16)
        dm_mark = qg.n

        NW1 = 192

        def w1_src(i):
            if i < 64:
                return wgT.rearrange("(c p) n -> p c n", p=128)[:, i // 4, 2048 * (i % 4):2048 * (i % 4 + 1)]
            if i < 128:
                j = i - 64
                return wuT.rearrange("(c p) n -> p c n", p=128)[:, j // 4, 2048 * (j % 4):2048 * (j % 4 + 1)]
            j = i - 128
            return wdT.rearrange("(c p) h -> p c h", p=128)[:, j:j + 1, :] \
                .rearrange("p c n -> p (c n)")

        @sec("sync")
        def _(s):
            for i in range(NW1):
                slot = w1ring.produce(s)
                w1ring.mark(s.dma_start(out=w1str[:, slot, :], in_=w1_src(i)))

        @sec("vector")
        def _(v):
            for i in range(NW1):
                w1ring.wait_ready(v, i)
                _li = v.tensor_reduce(wabs[:, i:i + 1], w1str[:, i % 2, :], AX.X,
                                ALU.add, apply_absolute_value=True)
                sv.inc(_li)
                w1ring.release(v, sv)

        b0heads = [h for h in range(NH) if (h // 4) % 2 == 0]
        b64heads = [h for h in range(NH) if (h // 4) % 2 == 1]
        QCH = {}
        QB = {}
        for i in range(16):
            QCH[b0heads[i]] = i
            QB[b0heads[i]] = 0
            QCH[b64heads[i]] = i
            QB[b64heads[i]] = 64
        gidx = 0
        for h in range(NH):
            kvh = h // 4
            kc = kvh // 2
            kb0 = (kvh % 2) * 64
            qc2 = QCH[h]
            qb0 = QB[h]
            for s_ in range(2):
                NG = NG0 if s_ == 0 else NG1
                qoff = 512 * s_
                for gg in range(NG):
                    par = gidx % 2
                    gidx += 1
                    psct, bscb = (psc0, bsc[0]) if par == 0 else (psc1, bsc[1])
                    bebb = beb[par]
                    diag = gg >= NG - 2
                    dg = gg - (NG - 2)
                    biascol = (NG0 if s_ == 1 else 0) + gg

                    @sec("tensor")
                    def _(t, s_=s_, gg=gg, psct=psct, bscb=bscb, diag=diag,
                          dg=dg, kc=kc, kb0=kb0, qc2=qc2, qb0=qb0, qoff=qoff,
                          gidx=gidx):
                        if gidx == 1:
                            t.wait_ge(qg.h, dm_mark)
                        bscb.acquire(t)
                        for j in range(2):
                            if diag:
                                ksl = Kd[kb0:kb0 + 64, kc,
                                         qoff + 256 * dg + 128 * j:qoff + 256 * dg + 128 * (j + 1)]
                            else:
                                ksl = Kf[kb0:kb0 + 64, kc,
                                         256 * gg + 128 * j:256 * gg + 128 * (j + 1)]
                            _li = t.matmul(psct[:, 512 * j:512 * (j + 1)], ksl,
                                     Qf[qb0:qb0 + 64, qc2, qoff:qoff + 512],
                                     start=True, stop=True)
                        st.inc(_li)

                    @sec("scalar")
                    def _(sc, par=par, psct=psct, bscb=bscb, bebb=bebb,
                          biascol=biascol):
                        st.wait(sc)
                        bebb.acquire(sc)
                        _li = sc.activation(ebuf[:, par, :], psct[:], AF.Exp,
                                      bias=biastbl[:, biascol:biascol + 1],
                                      scale=0.125)
                        ss.inc(_li)
                        bscb.release(ss)

                    if diag:
                        @sec("vector")
                        def _(v, par=par, dg=dg):
                            ss.wait(v)
                            sv.inc(v.tensor_mul(ebuf[:, par, :], ebuf[:, par, :],
                                                dmask_b[:, 1024 * dg:1024 * (dg + 1)]))

                    @sec("tensor")
                    def _(t, s_=s_, gg=gg, par=par, bebb=bebb, diag=diag,
                          dg=dg, kvh=kvh, NG=NG):
                        if diag:
                            sv.wait(t)
                        else:
                            ss.wait(t)
                        if gg == 0:
                            bav.acquire(t)
                        for j in range(2):
                            if diag:
                                vtile = Vd[:, 4 * s_ + 2 * dg + j, kvh, :]
                            else:
                                vtile = Vt[:, 2 * gg + j, kvh, :]
                            _li = t.matmul(pav[:], vtile,
                                     ebuf[:, par, 512 * j:512 * (j + 1)],
                                     start=(gg == 0 and j == 0),
                                     stop=(gg == NG - 1 and j == 1))
                        st.inc(_li)
                        bebb.release(st)

                @sec("vector")
                def _(v):
                    st.wait(v)
                    _li = v.reciprocal(recf[:], pav[64:65, :])
                    sv.inc(v.tensor_copy(recb[:], recf[:]))

                @sec("scalar")
                def _(sc):
                    st.wait(sc)
                    sv.wait(sc)
                    _li = sc.copy(avsb[:], pav[0:64, :])
                    ss.inc(_li)

                @sec("tensor")
                def _(t):
                    sv.wait(t)
                    ss.wait(t)
                    bav.release(sv, ss)
                    bbc.acquire(t)
                    _li = t.matmul(pbc[0:64, :], ones_b[0:1, 0:64], recb[:],
                             start=True, stop=True)
                    st.inc(_li)

                @sec("vector")
                def _(v, qc2=qc2, qb0=qb0, qoff=qoff):
                    st.wait(v)
                    sv.inc(v.tensor_mul(
                        att[qb0:qb0 + 64, qc2, qoff:qoff + 512],
                        avsb[:], pbc[0:64, :]))
                    bbc.release(sv)

        barrier()

        if DEBUG:
            @sec("gpsimd")
            def _(g):
                qg.inc(g.dma_start(out=dbg_att.rearrange("(c p) t -> p c t", p=128),
                                   in_=att[:]), 16)
                g.wait_ge(qg.h, qg.n)

    stKV.close()
    stRows = ExitStack()
    rows = sb(stRows, "rows", (128, 8 * 512), F32)

    with ExitStack() as stC:
        wo_s = sb(stC, "wo_s", (128, 2, DC, 256))
        xo_f = sb(stC, "xo_f", (128, DC, T), F32)
        ctile = sb(stC, "ctile", (128, 1024), F32)
        ctil2 = sb(stC, "ctil2", (128, 1024), F32)
        cbf = sb(stC, "cbf", (128, 1024))
        cbf2 = sb(stC, "cbf2", (128, 1024))
        mxf = sb(stC, "mxf", (128, 1024), F32)
        qmx = sb(stC, "qmx", (128, 16), F32)
        trow = sb(stC, "trow", (128, 512), F32)
        sxb = sb(stC, "sxb", (128, T), F32)
        bcst = sb(stC, "bcst", (128, 2, 1024), F32)
        pw0 = pt(stC, "pw0", (128, 1024))
        pw1 = pt(stC, "pw1", (128, 1024))
        prow1 = pt(stC, "prow1", (128, 512))
        prow2 = pt(stC, "prow2", (128, 512))
        pbc0 = pt(stC, "pbc0", (128, 512))
        ptf2 = pt(stC, "ptf2", (128, 512))

        woring = Ring([sem("wo_r0"), sem("wo_r1")], sem("rf4"), 2)
        bcb = Buf()
        bpw = [Buf(sem("bpw0")), Buf(sem("bpw1"))]
        brow = Buf(sem("brow"))
        bbc0 = Buf(sem("bbc0"))
        btf2 = Buf(sem("btf2"))

        @sec("gpsimd")
        def _(g):
            for e8 in range(8):
                slot = woring.produce(g)
                woring.mark(g.dma_start(
                    out=wo_s[:, slot, :, :],
                    in_=wo_d.rearrange("(c p) n -> p c n", p=128)[:, :, 256 * e8:256 * (e8 + 1)]))

        @sec("sync")
        def _(s):
            qs.inc(s.dma_start(out=xo_f[:],
                               in_=x_own.rearrange("(c p) t -> p c t", p=128)), 16)
        xo_mark = qs.n

        for dc in range(DC):
            par = dc % 2
            pwt, pwb = (pw0, bpw[0]) if par == 0 else (pw1, bpw[1])
            e8 = dc // 2
            q2 = dc % 2

            @sec("tensor")
            def _(t, dc=dc, pwt=pwt, pwb=pwb, e8=e8, q2=q2):
                woring.wait_ready(t, e8)
                pwb.acquire(t)
                for k in range(DC):
                    for th in range(2):
                        _li = t.matmul(pwt[:, 512 * th:512 * (th + 1)],
                                 wo_s[:, e8 % 2, k, 128 * q2:128 * (q2 + 1)],
                                 att[:, k, 512 * th:512 * (th + 1)],
                                 start=(k == 0), stop=(k == DC - 1))
                st.inc(_li)
                if q2 == 1:
                    woring.release(t, st)

            @sec("vector")
            def _(v, dc=dc, pwt=pwt, pwb=pwb):
                st.wait(v)
                if dc == 0:
                    v.wait_ge(qs.h, xo_mark)
                _li = v.tensor_add(xo_f[:, dc, :], pwt[:], xo_f[:, dc, :])
                sv.inc(_li)
                pwb.release(sv)

            @sec("scalar")
            def _(sc, dc=dc):
                sv.wait(sc)
                bcb.acquire(sc)
                _li = sc.copy(cbf[:], xo_f[:, dc, :])
                _li = sc.activation(cbf2[:], xo_f[:, dc, :], AF.Square)
                ss.inc(_li)

            @sec("tensor")
            def _(t, dc=dc):
                ss.wait(t)
                if dc == 0:
                    brow.acquire(t)
                for th in range(2):
                    _li = t.matmul(prow1[32 * th:32 * th + 1, 0:512], ones_b[:, 0:1],
                             cbf[:, 512 * th:512 * (th + 1)],
                             start=(dc == 0), stop=(dc == DC - 1))
                    _li = t.matmul(prow2[32 * th:32 * th + 1, 0:512], ones_b[:, 0:1],
                             cbf2[:, 512 * th:512 * (th + 1)],
                             start=(dc == 0), stop=(dc == DC - 1))
                st.inc(_li)
                bcb.release(st)

        @sec("vector")
        def _(v):
            st.wait(v)
            _li = v.tensor_copy(row(R_S1, 0), prow1[0:1, 0:512])
            _li = v.tensor_copy(row(R_S1, 1), prow1[32:33, 0:512])
            _li = v.tensor_copy(row(R_S2, 0), prow2[0:1, 0:512])
            _li = v.tensor_copy(row(R_S2, 1), prow2[32:33, 0:512])
            sv.inc(_li)
            brow.release(sv)

        def ln_rows_math():
            for half in range(2):
                @sec("vector")
                def _(v, half=half):
                    v.wait_ge(qs.h, qs.n)
                    _li = v.tensor_scalar_mul(row(R_MEAN, half), row(R_S1, half), 1.0 / D)
                    _li = v.tensor_mul(row(R_A, half), row(R_MEAN, half), row(R_MEAN, half))
                    _li = v.scalar_tensor_tensor(row(R_B, half), row(R_S2, half), 1.0 / D,
                                           row(R_A, half), ALU.mult, ALU.subtract)
                    _li = v.tensor_scalar_add(row(R_B, half), row(R_B, half), LN_EPS)
                    _li = v.reciprocal(row(R_C, half), row(R_B, half))
                    sv.inc(_li)

                @sec("scalar")
                def _(sc, half=half):
                    sv.wait(sc)
                    _li = sc.activation(row(R_RSTD, half), row(R_C, half), AF.Sqrt)
                    ss.inc(_li)

                @sec("vector")
                def _(v, half=half):
                    ss.wait(v)
                    _li = v.tensor_mul(row(R_A, half), row(R_RSTD, half), row(R_RSTD, half))
                    _li = v.tensor_mul(row(R_A, half), row(R_A, half), row(R_B, half))
                    _li = v.tensor_scalar(row(R_A, half), row(R_A, half), -0.5, 1.5,
                                    ALU.mult, ALU.add)
                    _li = v.tensor_mul(row(R_RSTD, half), row(R_RSTD, half), row(R_A, half))
                    sv.inc(_li)

        ln_rows_math()
        broadcast_row(R_MEAN, bcst[:, 0, :], pbc0, bbc0)
        broadcast_row(R_RSTD, bcst[:, 1, :], pbc0, bbc0)

        for dc in range(DC):
            @sec("vector")
            def _(v, dc=dc):
                sv.wait(v)
                _li = v.scalar_tensor_tensor(ctile[:], xo_f[:, dc, :], 0.0,
                                       bcst[:, 0, :], ALU.add, ALU.subtract)
                sv.inc(v.tensor_mul(xo_f[:, dc, :], ctile[:], bcst[:, 1, :]))

            @sec("scalar")
            def _(sc, dc=dc):
                sv.wait(sc)
                _li = sc.activation(ctil2[:], xo_f[:, dc, :], AF.Abs)
                ss.inc(_li)

            @sec("vector")
            def _(v, dc=dc):
                ss.wait(v)
                if dc == 0:
                    sv.inc(v.tensor_copy(mxf[:], ctil2[:]))
                else:
                    sv.inc(v.tensor_max(mxf[:], mxf[:], ctil2[:]))

        xln1_store_mark = [0]

        @sec("scalar")
        def _(sc):
            sv.wait(sc)
            qa.inc(sc.dma_start(out=xln1_scr.rearrange("(c p) t -> p c t", p=128),
                                in_=xo_f[:]), 16)
        xln1_store_mark[0] = qa.n

        for tb in range(8):
            @sec("tensor")
            def _(t, tb=tb):
                sv.wait(t)
                if tb % 4 == 0:
                    btf2.acquire(t)
                _li = t.transpose(ptf2[:, 128 * (tb % 4):128 * (tb % 4 + 1)],
                            mxf[:, 128 * tb:128 * (tb + 1)], iden_f[:])
                st.inc(_li)

            @sec("vector")
            def _(v, tb=tb):
                st.wait(v)
                _li = v.tensor_reduce(qmx[:, tb:tb + 1],
                                ptf2[:, 128 * (tb % 4):128 * (tb % 4 + 1)],
                                AX.X, ALU.max)
                sv.inc(_li)
                if tb % 4 == 3:
                    btf2.release(sv)

        @sec("vector")
        def _(v):
            sv.wait(v)
            vsafe(v, v.tensor_scalar_max(qmx[:, 8:16], qmx[:, 0:8], EPS_Q))
            _li = v.memset(stg[:], 0.0)
            vsafe(v, _li)
            sv.inc(v.tensor_copy(stg[:, 0:8], qmx[:, 8:16]))

        @sec("tensor")
        def _(t):
            sv.wait(t)
            btf2.acquire(t)
            _li = t.transpose(ptf2[:, 0:128], stg[:], iden_f[:])
            st.inc(_li)

        @sec("vector")
        def _(v):
            st.wait(v)
            v.wait_ge(qs.h, qs.n)
            sv.inc(v.tensor_copy(trow[0:8, 0:128], ptf2[0:8, 0:128]))
            btf2.release(sv)

        @sec("tensor")
        def _(t):
            sv.wait(t)
            btf2.acquire(t)
            for half in range(2):
                for cc in range(4):
                    _li = t.matmul(ptf2[32 * half:32 * half + 1, 128 * cc:128 * (cc + 1)],
                                   iden_f[0:8, 4 * half + cc:4 * half + cc + 1],
                                   trow[0:8, 0:128], start=True, stop=True)
            st.inc(_li)

        @sec("vector")
        def _(v):
            st.wait(v)
            _li = v.tensor_copy(row(R_C, 0), ptf2[0:1, 0:512])
            _li = v.tensor_copy(row(R_C, 1), ptf2[32:33, 0:512])
            sv.inc(_li)
            btf2.release(sv)

        for half in range(2):
            @sec("vector")
            def _(v, half=half):
                sv.wait(v)
                _li = v.reciprocal(row(R_A, half), row(R_C, half))
                sv.inc(v.tensor_scalar_mul(row(R_A, half), row(R_A, half), 127.0))

        broadcast_row(R_A, sxb[:], pbc0, bbc0)

        for dc in range(DC):
            @sec("vector")
            def _(v, dc=dc):
                sv.wait(v)
                _li = v.tensor_mul(ctile[:], xo_f[:, dc, :], sxb[:])
                _li = v.tensor_copy(ctil2[:], ctile[:])
                for k in range(7, -1, -1):
                    _li = v.tensor_single_scalar(cbf[:], ctil2[:],
                                           float(2 ** k) - 128.5, ALU.is_ge)
                    _li = v.scalar_tensor_tensor(ctil2[:], cbf[:], -float(2 ** k),
                                           ctil2[:], ALU.mult, ALU.add)
                _li = v.tensor_sub(ctile[:], ctile[:], ctil2[:])
                sv.inc(v.tensor_scalar_add(n_x[:, dc, :], ctile[:], -128.0))

        @sec("vector")
        def _(v):
            sv.wait(v)
            st.wait(v)
            v.memset(stg[:], 0.0)
            for w in range(3):
                vsafe(v, v.tensor_reduce(wabs[:, 192 + w:193 + w],
                                         wabs[:, 64 * w:64 * (w + 1)],
                                         AX.X, ALU.add))
            sv.inc(v.tensor_copy(stg[:, 0:3], wabs[:, 192:195]))

        @sec("tensor")
        def _(t):
            sv.wait(t)
            btf2.acquire(t)
            _li = t.transpose(ptf2[:, 0:128], stg[:], iden_f[:])
            st.inc(_li)

        @sec("vector")
        def _(v):
            st.wait(v)
            sv.inc(v.tensor_copy(trow[0:3, 0:128], ptf2[0:3, 0:128]))
            btf2.release(sv)

        @sec("tensor")
        def _(t):
            sv.wait(t)
            btf2.acquire(t)
            for w in range(3):
                _li = t.matmul(ptf2[0:1, 128 * w:128 * (w + 1)],
                               iden_f[0:8, w:w + 1], trow[0:8, 0:128],
                               start=True, stop=True)
            st.inc(_li)

        @sec("vector")
        def _(v):
            st.wait(v)
            _li = v.tensor_copy(rows[0:1, 512 * R_DD:512 * R_DD + 384],
                                ptf2[0:1, 0:384])
            sv.inc(_li)
            btf2.release(sv)

        @sec("vector")
        def _(v):
            sv.wait(v)
            base = 512 * R_DD
            for w in range(3):
                _li = v.tensor_reduce(scl[0:1, 8 + w:9 + w],
                                rows[0:1, base + 128 * w:base + 128 * (w + 1)],
                                AX.X, ALU.add)
            vsafe(v, _li)
            vsafe(v, v.tensor_scalar(scl[0:1, 11:14], scl[0:1, 8:11],
                                     1.0 / (II * D), EPS_Q, ALU.mult, ALU.max))
            _li = v.tensor_scalar_mul(scl[0:1, 14:17], scl[0:1, 11:14], 0.5)
            vsafe(v, v.tensor_scalar_mul(scl[0:1, 17:20], scl[0:1, 11:14], -0.5))
            vsafe(v, v.tensor_mul(scl[0:1, 20:21], scl[0:1, 11:12], scl[0:1, 11:12]))
            vsafe(v, v.tensor_mul(scl[0:1, 21:22], scl[0:1, 20:21], scl[0:1, 12:13]))

        for i, col in enumerate([14, 17, 15, 18, 16, 19]):
            @sec("tensor")
            def _(t, col=col):
                sv.wait(t)
                bbc0.acquire(t)
                _li = t.matmul(pbc0[:, 0:1], ones_f[0:1, :], scl[0:1, col:col + 1],
                         start=True, stop=True)
                st.inc(_li)

            @sec("vector")
            def _(v, i=i):
                st.wait(v)
                sv.inc(v.tensor_copy(colA[:, 1 + i:2 + i], pbc0[:, 0:1]))
                bbc0.release(sv)

        for half in range(2):
            @sec("vector")
            def _(v, half=half):
                sv.wait(v)
                _li = v.tensor_scalar_mul(row(R_B, half), row(R_C, half), 1.0 / 127.0)
                _li = v.tensor_mul(row(R_DD, half), row(R_B, half), row(R_B, half))
                _li = v.tensor_mul(row(R_DD, half), row(R_DD, half), row(R_B, half))
                sv.inc(v.tensor_scalar(row(R_DD, half), row(R_DD, half),
                                       scl[0:1, 21:22], None, ALU.mult))

        if DEBUG:
            @sec("gpsimd")
            def _(g):
                g.wait_ge(sv.h, sv.n)
                qg.inc(g.dma_start(out=dbg_nx.rearrange("(c p) t -> p c t", p=128),
                                   in_=n_x[:]), 16)
                qg.inc(g.dma_start(out=dbg_misc[:, 48:56], in_=qmx[:, 0:8]), 16)
                qg.inc(g.dma_start(out=dbg_misc[:, 32:39], in_=colA[:, 0:7]), 16)
                g.wait_ge(qg.h, qg.n)

        barrier()

    stAtt.close()

    with ExitStack() as stD:
        wstr = sb(stD, "wstr", (128, 2, DC, 256), F32)
        twg = sb(stD, "twg", (128, DC, 256))
        twu = sb(stD, "twu", (128, DC, 256))
        qscr = sb(stD, "qscr", (128, 4096))
        rg_f = sb(stD, "rg_f", (128, 1024), F32)
        rg2 = sb(stD, "rg2", (128, 1024), F32)
        rawstg = sb(stD, "rawstg", (128, 8, 1024), F32)
        raw2b = sb(stD, "raw2b", (128, 1024))
        absg = sb(stD, "absg", (128, 1024), F32)
        mxrun = sb(stD, "mxrun", (128, 1024), F32)
        trow2 = sb(stD, "trow2", (128, 512), F32)
        pg0 = pt(stD, "pg0", (128, 1024))
        pg1 = pt(stD, "pg1", (128, 1024))
        pu = pt(stD, "pu", (128, 1024))
        pm2 = pt(stD, "pm2", (128, 512))

        wdring = Ring([sem("wd_r0"), sem("wd_r1")], sem("rf5"), 2)
        btg = Buf()
        btu = Buf()
        brb = Buf()
        brg = Buf()
        bpg = [Buf(sem("bpg0")), Buf(sem("bpg1"))]
        bpu = Buf(sem("bpu"))
        bm2 = Buf(sem("bm2"))
        brs = Buf(sem("brs"), depth=2)

        @sec("sync")
        def _(s):
            for ib in range(32):
                for wsrc in (wgT, wuT):
                    slot = wdring.produce(s)
                    wdring.mark(s.dma_start(
                        out=wstr[:, slot, :, :],
                        in_=wsrc.rearrange("(c p) n -> p c n", p=128)[:, :, 256 * ib:256 * (ib + 1)]))

        for ib in range(32):
            for wi, tw, thr in ((0, twg, 1), (1, twu, 3)):
                tile_i = 2 * ib + wi

                @sec("vector")
                def _(v, tile_i=tile_i, tw=tw, thr=thr, wi=wi):
                    wdring.wait_ready(v, tile_i)
                    (btg if wi == 0 else btu).acquire(v)
                    src = wstr[:, tile_i % 2, :, :].rearrange("p c n -> p (c n)")
                    twv = tw[:].rearrange("p c n -> p (c n)")
                    _li = v.tensor_scalar(twv, src, colA[:, thr:thr + 1], None, ALU.is_ge)
                    _li = v.tensor_scalar(qscr[:], src, colA[:, thr + 1:thr + 2], None,
                                    ALU.is_le)
                    sv.inc(v.tensor_sub(twv, twv, qscr[:]))
                    wdring.release(v, sv)

            for c2 in range(2):
                ic = 2 * ib + c2
                par = ic % 2
                pgt, pgb = (pg0, bpg[0]) if par == 0 else (pg1, bpg[1])

                @sec("tensor")
                def _(t, c2=c2, pgt=pgt, pgb=pgb):
                    sv.wait(t)
                    pgb.acquire(t)
                    for k in range(DC):
                        for th in range(2):
                            _li = t.matmul(pgt[:, 512 * th:512 * (th + 1)],
                                     twg[:, k, 128 * c2:128 * (c2 + 1)],
                                     n_x[:, k, 512 * th:512 * (th + 1)],
                                     start=(k == 0), stop=(k == DC - 1))
                    st.inc(_li)
                    if c2 == 1:
                        btg.release(st)
                    bpu.acquire(t)
                    for k in range(DC):
                        for th in range(2):
                            _li = t.matmul(pu[:, 512 * th:512 * (th + 1)],
                                     twu[:, k, 128 * c2:128 * (c2 + 1)],
                                     n_x[:, k, 512 * th:512 * (th + 1)],
                                     start=(k == 0), stop=(k == DC - 1))
                    st.inc(_li)
                    if c2 == 1:
                        btu.release(st)

                @sec("scalar")
                def _(sc, pgt=pgt):
                    st.wait(sc, st.n - 1)
                    brg.acquire(sc)
                    _li = sc.activation(rg_f[:], pgt[:], AF.Relu)
                    _li = sc.activation(rg2[:], rg_f[:], AF.Square)
                    ss.inc(_li)

                @sec("vector")
                def _(v, ic=ic, pgb=pgb):
                    ss.wait(v)
                    pgb.release(ss)
                    st.wait(v)
                    if ic % 4 == 0:
                        brs.acquire(v)
                    _li = v.tensor_mul(rawstg[:, ic % 8, :], rg2[:], pu[:])
                    sv.inc(_li)
                    brg.release(sv)
                    bpu.release(sv)

                if DEBUG:
                    @sec("scalar")
                    def _(sc, ic=ic):
                        sv.wait(sc)
                        qa.inc(sc.dma_start(
                            out=dbg_rg.rearrange("(c p) t -> p c t", p=128)[:, ic:ic + 1, :],
                            in_=rg_f[:]), 16)

                @sec("scalar")
                def _(sc, ic=ic):
                    sv.wait(sc)
                    brb.acquire(sc)
                    _li = sc.activation(raw2b[:], rawstg[:, ic % 8, :], AF.Square)
                    _li = sc.activation(absg[:], rawstg[:, ic % 8, :], AF.Abs,
                                  scale=gcol_s[:, ic:ic + 1])
                    ss.inc(_li)

                @sec("tensor")
                def _(t, ic=ic):
                    ss.wait(t)
                    if ic == 0:
                        bm2.acquire(t)
                    if not NO_M2:
                        for th in range(2):
                            _li = t.matmul(pm2[32 * th:32 * th + 1, 0:512], ones_b[:, 0:1],
                                     raw2b[:, 512 * th:512 * (th + 1)],
                                     start=(ic == 0), stop=(ic == ICN - 1))
                        st.inc(_li)
                    else:
                        st.inc(t.nop())
                    brb.release(st)

                @sec("vector")
                def _(v, ic=ic):
                    ss.wait(v)
                    if ic == 0:
                        sv.inc(v.tensor_copy(mxrun[:], absg[:]))
                    else:
                        sv.inc(v.tensor_max(mxrun[:], mxrun[:], absg[:]))

                if ic % 4 == 3:
                    grp = ic // 4

                    @sec("scalar")
                    def _(sc, ic=ic, grp=grp):
                        sv.wait(sc)
                        c0 = ic - 3
                        qa.inc(sc.dma_start(
                            out=act_scr.rearrange("(c p) t -> p c t", p=128)[:, c0:c0 + 4, :],
                            in_=rawstg[:, (grp % 2) * 4:(grp % 2) * 4 + 4, :]), 16)
                        brs.release(ss, qa)
                if SERIAL_D1:
                    barrier()

        @sec("vector")
        def _(v):
            st.wait(v)
            _li = v.tensor_copy(row(R_S1, 0), pm2[0:1, 0:512])
            _li = v.tensor_copy(row(R_S1, 1), pm2[32:33, 0:512])
            sv.inc(_li)
            bm2.release(sv)

        def qmx2_col(tb):
            return stg[:, 16 + tb:17 + tb]

        for tb in range(8):
            @sec("tensor")
            def _(t, tb=tb):
                sv.wait(t)
                if tb % 4 == 0:
                    bm2.acquire(t)
                _li = t.transpose(pg0[:, 128 * (tb % 4):128 * (tb % 4 + 1)],
                            mxrun[:, 128 * tb:128 * (tb + 1)], iden_f[:])
                st.inc(_li)

            @sec("vector")
            def _(v, tb=tb):
                st.wait(v)
                _li = v.tensor_reduce(qmx2_col(tb), pg0[:, 128 * (tb % 4):128 * (tb % 4 + 1)],
                                AX.X, ALU.max)
                sv.inc(_li)
                if tb % 4 == 3:
                    bm2.release(sv)

        @sec("vector")
        def _(v):
            sv.wait(v)
            _li = v.memset(stg[:, 0:8], 0.0)
            vsafe(v, _li)
            sv.inc(v.tensor_copy(stg[:, 0:8], stg[:, 16:24]))

        @sec("tensor")
        def _(t):
            sv.wait(t)
            bm2.acquire(t)
            _li = t.transpose(pg0[:, 0:128], stg[:], iden_f[:])
            st.inc(_li)

        @sec("vector")
        def _(v):
            st.wait(v)
            v.wait_ge(qs.h, qs.n)
            _li = v.tensor_copy(trow2[0:8, 0:128], pg0[0:8, 0:128])
            sv.inc(_li)
            bm2.release(sv)

        @sec("tensor")
        def _(t):
            sv.wait(t)
            bm2.acquire(t)
            for half in range(2):
                for cc in range(4):
                    _li = t.matmul(pm2[32 * half:32 * half + 1, 128 * cc:128 * (cc + 1)],
                                   iden_f[0:8, 4 * half + cc:4 * half + cc + 1],
                                   trow2[0:8, 0:128], start=True, stop=True)
            st.inc(_li)

        @sec("vector")
        def _(v):
            st.wait(v)
            _li = v.tensor_copy(row(R_S2, 0), pm2[0:1, 0:512])
            _li = v.tensor_copy(row(R_S2, 1), pm2[32:33, 0:512])
            sv.inc(_li)
            bm2.release(sv)

        for half in range(2):
            @sec("vector")
            def _(v, half=half):
                sv.wait(v)
                _li = v.tensor_mul(row(R_A, half), row(R_DD, half), row(R_DD, half))
                _li = v.tensor_scalar_mul(row(R_B, half), row(R_S1, half), 1.0 / II)
                _li = v.tensor_mul(row(R_B, half), row(R_B, half), row(R_A, half))
                _li = v.tensor_scalar_add(row(R_B, half), row(R_B, half), SUB_EPS)
                _li = v.reciprocal(row(R_C, half), row(R_B, half))
                sv.inc(_li)

            @sec("scalar")
            def _(sc, half=half):
                sv.wait(sc)
                _li = sc.activation(row(R_RSTD, half), row(R_C, half), AF.Sqrt)
                ss.inc(_li)

            @sec("vector")
            def _(v, half=half):
                ss.wait(v)
                _li = v.tensor_mul(row(R_A, half), row(R_RSTD, half), row(R_RSTD, half))
                _li = v.tensor_mul(row(R_A, half), row(R_A, half), row(R_B, half))
                _li = v.tensor_scalar(row(R_A, half), row(R_A, half), -0.5, 1.5,
                                ALU.mult, ALU.add)
                _li = v.tensor_mul(row(R_RSTD, half), row(R_RSTD, half), row(R_A, half))
                _li = v.tensor_mul(row(R_MEAN, half), row(R_DD, half), row(R_RSTD, half))
                _li = v.tensor_mul(row(R_B, half), row(R_MEAN, half), row(R_S2, half))
                _li = v.tensor_scalar_max(row(R_B, half), row(R_B, half), EPS_Q)
                _li = v.reciprocal(row(R_C, half), row(R_B, half))
                _li = v.tensor_scalar_mul(row(R_C, half), row(R_C, half), 127.0)
                _li = v.tensor_mul(row(R_A, half), row(R_MEAN, half), row(R_C, half))
                _li = v.tensor_scalar_mul(row(R_B, half), row(R_B, half), 1.0 / 127.0)
                sv.inc(v.tensor_scalar(row(R_B, half), row(R_B, half),
                                       scl[0:1, 13:14], None, ALU.mult))

        barrier()

    with ExitStack() as stD2:
        fs2b = sb(stD2, "fs2b", (128, T), F32)
        dsb = sb(stD2, "dsb", (128, T), F32)
        wdstr = sb(stD2, "wdstr", (128, 8, 512), F32)
        twd = sb(stD2, "twd", (128, 8, 512))
        qscr2 = sb(stD2, "qscr2", (128, 4096))
        n_act = sb(stD2, "n_act", (128, ICN, 512))
        actrd = sb(stD2, "actrd", (128, 4, 512), F32)
        xq_t = sb(stD2, "xq_t", (128, 4, 512), F32)
        y2stg = sb(stD2, "y2stg", (128, 4, 512), F32)
        ct1 = sb(stD2, "ct1", (128, 512), F32)
        ct2 = sb(stD2, "ct2", (128, 512), F32)
        cb1 = sb(stD2, "cb1", (128, 512))
        pd = [pt(stD2, "pd%d" % i, (128, 512)) for i in range(4)]
        px0 = pt(stD2, "px0", (128, 1024))
        px1 = pt(stD2, "px1", (128, 1024))

        bbx = Buf(sem("bbx"))
        by2 = Buf()
        bpd = Buf(sem("bpd"))
        btw = Buf(sem("btw"))
        wd2ring = Ring([sem("wd2_r0")], sem("rf6"), 1)
        acring = Ring([sem("ac_r0")], sem("rf7"), 1)
        xqring = Ring([sem("xq_r0")], sem("rf8"), 1)

        broadcast_row(R_A, fs2b[:], px0, bbx)
        broadcast_row(R_B, dsb[:], px0, bbx)

        for th in range(2):
            for blk in range(16):
                @sec("scalar")
                def _(sc, th=th, blk=blk):
                    acring.produce(sc)
                    acring.mark(sc.dma_start(
                        out=actrd[:],
                        in_=act_scr.rearrange("(c p) t -> p c t", p=128)
                        [:, 4 * blk:4 * (blk + 1), 512 * th:512 * (th + 1)]))

                for c4 in range(4):
                    ic = 4 * blk + c4

                    @sec("vector")
                    def _(v, th=th, blk=blk, c4=c4, ic=ic):
                        acring.wait_ready(v, 16 * th + blk)
                        _li = v.scalar_tensor_tensor(ct1[:], actrd[:, c4, :],
                                               gcol_s[:, ic:ic + 1],
                                               fs2b[:, 512 * th:512 * (th + 1)],
                                               ALU.mult, ALU.mult)
                        _li = v.tensor_copy(ct2[:], ct1[:])
                        for k in range(7, -1, -1):
                            _li = v.tensor_single_scalar(cb1[:], ct2[:],
                                                   float(2 ** k) - 128.5, ALU.is_ge)
                            _li = v.scalar_tensor_tensor(ct2[:], cb1[:], -float(2 ** k),
                                                   ct2[:], ALU.mult, ALU.add)
                        _li = v.tensor_sub(ct1[:], ct1[:], ct2[:])
                        _li = v.tensor_scalar_add(n_act[:, ic, :], ct1[:], -128.0)
                        sv.inc(_li)
                        if c4 == 3:
                            acring.release(v, sv)

            for hq in range(4):
                @sec("sync")
                def _(s, th=th, hq=hq):
                    if th == 0 and hq == 0:
                        s.wait_ge(qa.h, xln1_store_mark[0])
                    xqring.produce(s)
                    xqring.mark(s.dma_start(
                        out=xq_t[:],
                        in_=xln1_scr.rearrange("(c p) t -> p c t", p=128)
                        [:, 4 * hq:4 * (hq + 1), 512 * th:512 * (th + 1)]))

                for ib8 in range(8):
                    tile_i = (th * 4 + hq) * 8 + ib8

                    @sec("sync")
                    def _(s, hq=hq, ib8=ib8):
                        wd2ring.produce(s)
                        wd2ring.mark(s.dma_start(
                            out=wdstr[:],
                            in_=wdT.rearrange("(c p) h -> p c h", p=128)
                            [:, 8 * ib8:8 * (ib8 + 1), 512 * hq:512 * (hq + 1)]))

                    @sec("vector")
                    def _(v, tile_i=tile_i):
                        wd2ring.wait_ready(v, tile_i)
                        btw.acquire(v)
                        src = wdstr[:].rearrange("p c n -> p (c n)")
                        twv = twd[:].rearrange("p c n -> p (c n)")
                        _li = v.tensor_scalar(twv, src, colA[:, 5:6], None, ALU.is_ge)
                        _li = v.tensor_scalar(qscr2[:], src, colA[:, 6:7], None, ALU.is_le)
                        sv.inc(v.tensor_sub(twv, twv, qscr2[:]))
                        wd2ring.release(v, sv)

                    @sec("tensor")
                    def _(t, ib8=ib8):
                        sv.wait(t)
                        if ib8 == 0:
                            bpd.acquire(t)
                        for ic8 in range(8):
                            ic = 8 * ib8 + ic8
                            for hc in range(4):
                                _li = t.matmul(pd[hc][:, 0:512],
                                         twd[:, ic8, 128 * hc:128 * (hc + 1)],
                                         n_act[:, ic, :],
                                         start=(ic == 0), stop=(ic == ICN - 1))
                        st.inc(_li)
                        btw.release(st)

                for hc in range(4):
                    @sec("vector")
                    def _(v, th=th, hq=hq, hc=hc):
                        st.wait(v)
                        xqring.wait_ready(v, th * 4 + hq)
                        if hc == 0:
                            by2.acquire(v)
                        _li = v.tensor_mul(ct1[:], pd[hc][:, 0:512],
                                     dsb[:, 512 * th:512 * (th + 1)])
                        _li = v.tensor_add(y2stg[:, hc, :], ct1[:], xq_t[:, hc, :])
                        sv.inc(_li)
                        if hc == 3:
                            bpd.release(sv)
                            xqring.release(v, sv)

                @sec("scalar")
                def _(sc, th=th, hq=hq):
                    sv.wait(sc)
                    qa.inc(sc.dma_start(
                        out=y2_scr.rearrange("(c p) t -> p c t", p=128)
                        [:, 4 * hq:4 * (hq + 1), 512 * th:512 * (th + 1)],
                        in_=y2stg[:]), 16)
                    by2.release(qa)

        barrier()

    with ExitStack() as stE:
        y2f = sb(stE, "y2f", (128, DC, T), F32)
        cbf1k = sb(stE, "cbf1k", (128, 1024))
        cbf1k2 = sb(stE, "cbf1k2", (128, 1024))
        bcst2 = sb(stE, "bcst2", (128, 2, 1024), F32)
        ytm = sb(stE, "ytm", (128, 2, 2048), F32)
        ctk = sb(stE, "ctk", (128, 1024), F32)
        trow3 = sb(stE, "trow3", (128, 512), F32)
        pe0 = pt(stE, "pe0", (128, 512))
        pe1 = pt(stE, "pe1", (128, 512))
        pe2 = pt(stE, "pe2", (128, 512))
        pe3 = pt(stE, "pe3", (128, 512))

        brw2 = Buf(sem("brw2"))
        bce = Buf()
        bbx2 = Buf(sem("bbx2"))
        bpe = [Buf(sem("bpe0")), Buf(sem("bpe1"))]

        @sec("sync")
        def _(s):
            qa.wait(s)
            qs.inc(s.dma_start(out=y2f[:],
                               in_=y2_scr.rearrange("(c p) t -> p c t", p=128)), 16)
        y2_mark = qs.n

        for dc in range(DC):
            @sec("scalar")
            def _(sc, dc=dc):
                if dc == 0:
                    sc.wait_ge(qs.h, y2_mark)
                bce.acquire(sc)
                _li = sc.copy(cbf1k[:], y2f[:, dc, :])
                _li = sc.activation(cbf1k2[:], y2f[:, dc, :], AF.Square)
                ss.inc(_li)

            @sec("tensor")
            def _(t, dc=dc):
                ss.wait(t)
                if dc == 0:
                    brw2.acquire(t)
                for th in range(2):
                    _li = t.matmul(pe0[32 * th:32 * th + 1, 0:512], ones_b[:, 0:1],
                             cbf1k[:, 512 * th:512 * (th + 1)],
                             start=(dc == 0), stop=(dc == DC - 1))
                    _li = t.matmul(pe1[32 * th:32 * th + 1, 0:512], ones_b[:, 0:1],
                             cbf1k2[:, 512 * th:512 * (th + 1)],
                             start=(dc == 0), stop=(dc == DC - 1))
                st.inc(_li)
                bce.release(st)

        @sec("vector")
        def _(v):
            st.wait(v)
            _li = v.tensor_copy(row(R_S1, 0), pe0[0:1, 0:512])
            _li = v.tensor_copy(row(R_S1, 1), pe0[32:33, 0:512])
            _li = v.tensor_copy(row(R_S2, 0), pe1[0:1, 0:512])
            _li = v.tensor_copy(row(R_S2, 1), pe1[32:33, 0:512])
            sv.inc(_li)
            brw2.release(sv)

        for half in range(2):
            @sec("vector")
            def _(v, half=half):
                sv.wait(v)
                _li = v.tensor_scalar_mul(row(R_MEAN, half), row(R_S1, half), 1.0 / D)
                _li = v.tensor_mul(row(R_A, half), row(R_MEAN, half), row(R_MEAN, half))
                _li = v.scalar_tensor_tensor(row(R_B, half), row(R_S2, half), 1.0 / D,
                                       row(R_A, half), ALU.mult, ALU.subtract)
                _li = v.tensor_scalar_add(row(R_B, half), row(R_B, half), LN_EPS)
                _li = v.reciprocal(row(R_C, half), row(R_B, half))
                sv.inc(_li)

            @sec("scalar")
            def _(sc, half=half):
                sv.wait(sc)
                _li = sc.activation(row(R_RSTD, half), row(R_C, half), AF.Sqrt)
                ss.inc(_li)

            @sec("vector")
            def _(v, half=half):
                ss.wait(v)
                _li = v.tensor_mul(row(R_A, half), row(R_RSTD, half), row(R_RSTD, half))
                _li = v.tensor_mul(row(R_A, half), row(R_A, half), row(R_B, half))
                _li = v.tensor_scalar(row(R_A, half), row(R_A, half), -0.5, 1.5,
                                ALU.mult, ALU.add)
                _li = v.tensor_mul(row(R_RSTD, half), row(R_RSTD, half), row(R_A, half))
                sv.inc(_li)

        broadcast_row(R_MEAN, bcst2[:, 0, :], pe2, bbx2)
        broadcast_row(R_RSTD, bcst2[:, 1, :], pe2, bbx2)

        for dc in range(DC):
            @sec("vector")
            def _(v, dc=dc):
                sv.wait(v)
                _li = v.scalar_tensor_tensor(ctk[:], y2f[:, dc, :], 0.0,
                                       bcst2[:, 0, :], ALU.add, ALU.subtract)
                sv.inc(v.tensor_mul(y2f[:, dc, :], ctk[:], bcst2[:, 1, :]))

        ytm_mark = {}
        for tb in range(8):
            for dc in range(DC):
                par = dc % 2
                pxt = pe2 if par == 0 else pe3
                pxb = bpe[par]

                @sec("tensor")
                def _(t, tb=tb, dc=dc, pxt=pxt, pxb=pxb):
                    sv.wait(t)
                    pxb.acquire(t)
                    _li = t.transpose(pxt[:, 0:128], y2f[:, dc, 128 * tb:128 * (tb + 1)],
                                iden_f[:])
                    st.inc(_li)

                @sec("vector")
                def _(v, tb=tb, dc=dc, pxt=pxt, pxb=pxb):
                    st.wait(v)
                    if dc == 0 and tb >= 2:
                        v.wait_ge(qa.h, ytm_mark[tb - 2])
                    _li = v.tensor_copy(ytm[:, tb % 2, 128 * dc:128 * (dc + 1)],
                                      pxt[:, 0:128])
                    sv.inc(_li)
                    pxb.release(sv)

            @sec("scalar")
            def _(sc, tb=tb):
                sv.wait(sc)
                qa.inc(sc.dma_start(out=y_out[128 * tb:128 * (tb + 1), :],
                                    in_=ytm[:, tb % 2, :]), 16)
            ytm_mark[tb] = qa.n
        @sec("scalar")
        def _(sc):
            qa.wait(sc)

    stRows.close()
    stQ.close()
    st0.close()
    return nc


_NC_CACHE = None


def _get_nc():
    global _NC_CACHE
    if _NC_CACHE is None:
        _NC_CACHE = build_program()
    return _NC_CACHE


def _host_tables():
    import jax
    import jax.numpy as jnp
    with jax.default_device(jax.devices("cpu")[0]):
        inv_freq = 1.0 / (10000.0 ** (jnp.arange(0, HD, 2, dtype=jnp.float32) / HD))
        ang = jnp.arange(S, dtype=jnp.float32)[:, None] * inv_freq[None, :]
        c_tab = np.asarray(jnp.cos(ang), dtype=np.float32)
        s_tab = np.asarray(jnp.sin(ang), dtype=np.float32)

    def rope_expand(positions):
        c = c_tab[positions]
        s = s_tab[positions]
        ce = np.empty((HD, len(positions)), np.float32)
        se = np.empty((HD, len(positions)), np.float32)
        ce[0::2] = c.T
        ce[1::2] = c.T
        se[0::2] = -s.T
        se[1::2] = s.T
        cf = np.vstack([ce, ce])
        sf = np.vstack([se, se])
        return np.ascontiguousarray(cf), np.ascontiguousarray(sf)

    cf_full, sf_full = rope_expand(np.arange(S))
    perm = np.zeros((128, 128), np.float32)
    for i in range(0, 128, 2):
        perm[i, i + 1] = 1.0
        perm[i + 1, i] = 1.0
    iden = np.eye(128, dtype=np.float32)
    ones = np.ones((128, 128), np.float32)
    e2 = np.zeros((128, 2), np.float32)
    e2[0:64, 0] = 1.0
    e2[64:128, 1] = 1.0
    dm = np.zeros((128, 2048), np.float32)
    q = np.arange(512)
    for kb in range(4):
        k_abs = 128 * kb + np.arange(128)
        dm[:, 512 * kb:512 * (kb + 1)] = (k_abs[:, None] <= q[None, :])
    return cf_full, sf_full, perm, iden, ones, e2, dm, rope_expand


def kernel(L0_attn_x, L0_attn_wq, L0_attn_wk, L0_attn_wv, L0_attn_wo,
           L0_ffn_w_gate, L0_ffn_w_up, L0_ffn_w_down, L0_ffn_subln_weight):
    x = np.ascontiguousarray(np.asarray(L0_attn_x, dtype=np.float32))
    wq = np.ascontiguousarray(np.asarray(L0_attn_wq, dtype=np.float32))
    wk = np.ascontiguousarray(np.asarray(L0_attn_wk, dtype=np.float32))
    wv = np.ascontiguousarray(np.asarray(L0_attn_wv, dtype=np.float32))
    wo = np.ascontiguousarray(np.asarray(L0_attn_wo, dtype=np.float32))
    wg = np.asarray(L0_ffn_w_gate, dtype=np.float32)
    wu = np.asarray(L0_ffn_w_up, dtype=np.float32)
    wd = np.asarray(L0_ffn_w_down, dtype=np.float32)
    g = np.asarray(L0_ffn_subln_weight, dtype=np.float32)

    b0heads = [h for h in range(NH) if (h // 4) % 2 == 0]
    b64heads = [h for h in range(NH) if (h // 4) % 2 == 1]
    perm_a = np.empty(D, np.int64)
    for i in range(16):
        perm_a[128 * i:128 * i + 64] = 64 * b0heads[i] + np.arange(64)
        perm_a[128 * i + 64:128 * (i + 1)] = 64 * b64heads[i] + np.arange(64)
    wq = np.ascontiguousarray(wq[:, perm_a])
    wo = np.ascontiguousarray(wo[perm_a, :])
    xT = [np.ascontiguousarray(x[b].T) for b in range(2)]
    wgT_h = np.ascontiguousarray(wg.T)
    wuT_h = np.ascontiguousarray(wu.T)
    wdT_h = np.ascontiguousarray(wd.T)
    g_col_h = np.ascontiguousarray(g.reshape(ICN, 128).T)

    cf_full, sf_full, perm, iden, ones, e2, dm, rope_expand = _host_tables()

    in_maps = []
    meta = []
    for c in range(8):
        b = c // 4
        j = c % 4
        pA, pB = j, 4 + j
        meta.append((b, pA, pB))
        pos = np.concatenate([np.arange(512 * pA, 512 * (pA + 1)),
                              np.arange(512 * pB, 512 * (pB + 1))])
        xo = np.ascontiguousarray(
            np.concatenate([xT[b][:, 512 * pA:512 * (pA + 1)],
                            xT[b][:, 512 * pB:512 * (pB + 1)]], axis=1))
        co, so = rope_expand(pos)
        pat = np.zeros((128, NGT), np.float32)
        for s_, p, ngf, base in ((0, pA, 6, 0), (1, pB, 14, NG0)):
            need = 2 * p
            for gg_ in range(ngf):
                if gg_ >= need:
                    pat[:, base + gg_] = NEG_BIG
        in_maps.append({
            "x_own": xo, "x_full": xT[b],
            "wq": wq, "wk": wk, "wv": wv, "wo": wo,
            "wgT": wgT_h, "wuT": wuT_h, "wdT": wdT_h, "g_col": g_col_h,
            "rope_cf": cf_full, "rope_sf": sf_full,
            "rope_co": co, "rope_so": so,
            "perm": perm, "iden": iden, "ones": ones, "e2": e2,
            "dmask": dm, "pat": pat,
        })

    nc = _get_nc()
    res = run_bass_kernel_spmd(nc, in_maps, core_ids=list(range(8)))

    out = np.empty((2, S, D), np.float32)
    for c in range(8):
        b, pA, pB = meta[c]
        yc = np.asarray(res.results[c]["y"])
        out[b, 512 * pA:512 * (pA + 1), :] = yc[0:512]
        out[b, 512 * pB:512 * (pB + 1), :] = yc[512:1024]
    return out


# version 08
# speedup vs baseline: 1.0582x; 1.0569x over previous
import numpy as np
from contextlib import ExitStack

import concourse.bass as bass
import concourse.mybir as mybir
from concourse.bass_utils import run_bass_kernel_spmd

F32 = mybir.dt.float32
BF16 = mybir.dt.bfloat16
I8 = mybir.dt.int8
AF = mybir.ActivationFunctionType
ALU = mybir.AluOpType
AX = mybir.AxisListType

D = 2048
S = 4096
T = 1024
NH = 32
NKV = 8
HD = 64
II = 8192
DC = D // 128
ICN = II // 128
LN_EPS = 1e-5
SUB_EPS = 1e-6
EPS_Q = 1e-5
NEG_BIG = -1.0e30
NG0, NG1 = 8, 16
NGT = NG0 + NG1


class Ctr:
    def __init__(self, h):
        self.h = h
        self.n = 0

    def inc(self, inst, k=1):
        inst.then_inc(self.h, k)
        self.n += k
        return inst

    def wait(self, eng, n=None):
        eng.wait_ge(self.h, self.n if n is None else n)


class Buf:

    def __init__(self, ctr=None, depth=1):
        self.acq = 0
        self.depth = depth
        self.marks = []

    def acquire(self, eng):
        idx = self.acq - self.depth
        if 0 <= idx < len(self.marks):
            for c, n in self.marks[idx]:
                eng.wait_ge(c.h, n)
        elif idx >= len(self.marks):
            raise RuntimeError("Buf.acquire before matching release")
        self.acq += 1

    def release(self, *ctrs):
        self.marks.append([(c, c.n) for c in ctrs])


class Ring:

    def __init__(self, rdys, fre, n):
        self.rdys = rdys
        self.fre = fre
        self.n = n
        self.p = 0

    def produce(self, eng):
        slot = self.p % self.n
        if self.p >= self.n:
            eng.wait_ge(self.fre.h, self.p - self.n + 1)
        self.p += 1
        return slot

    def mark(self, inst):
        slot = (self.p - 1) % self.n
        self.rdys[slot].inc(inst, 16)

    def wait_ready(self, eng, i):
        slot = i % self.n
        eng.wait_ge(self.rdys[slot].h, 16 * (i // self.n + 1))

    def release(self, eng, ctr):
        eng.wait_ge(ctr.h, ctr.n)
        self.fre.inc(eng.nop())


DEBUG = False
SERIAL_D1 = False
NO_M2 = False


def build_program():
    nc = bass.Bass("TRN2", target_bir_lowering=False, detect_race_conditions=False)
    dkind = "ExternalOutput" if DEBUG else "Internal"

    def din(name, shape):
        return nc.dram_tensor(name, list(shape), F32, kind="ExternalInput").ap()

    x_own = din("x_own", (D, T))
    x_full = din("x_full", (D, S))
    wq_d = din("wq", (D, D))
    wk_d = din("wk", (D, NKV * HD))
    wv_d = din("wv", (D, NKV * HD))
    wo_d = din("wo", (D, D))
    wgT = din("wgT", (D, II))
    wuT = din("wuT", (D, II))
    wdT = din("wdT", (II, D))
    g_col = din("g_col", (128, ICN))
    rope_cf = din("rope_cf", (128, S))
    rope_sf = din("rope_sf", (128, S))
    rope_co = din("rope_co", (128, T))
    rope_so = din("rope_so", (128, T))
    perm_in = din("perm", (128, 128))
    iden_in = din("iden", (128, 128))
    ones_in = din("ones", (128, 128))
    e2_in = din("e2", (128, 2))
    dmask_in = din("dmask", (128, 2048))
    pat_in = din("pat", (128, NGT))
    y_out = nc.dram_tensor("y", [T, D], F32, kind="ExternalOutput").ap()
    act_scr = nc.dram_tensor("act_scr", [II, T], F32, kind=dkind).ap()
    xln1_scr = nc.dram_tensor("xln1_scr", [D, T], F32, kind=dkind).ap()
    y2_scr = nc.dram_tensor("y2_scr", [D, T], F32, kind=dkind).ap()
    if DEBUG:
        dbg_att = nc.dram_tensor("dbg_att", [D, T], F32, kind="ExternalOutput").ap()
        dbg_misc = nc.dram_tensor("dbg_misc", [128, 64], F32, kind="ExternalOutput").ap()
        dbg_nx = nc.dram_tensor("dbg_nx", [D, T], F32, kind="ExternalOutput").ap()
        dbg_q = nc.dram_tensor("dbg_q", [D, T], F32, kind="ExternalOutput").ap()
        dbg_rg = nc.dram_tensor("dbg_rg", [II, T], F32, kind="ExternalOutput").ap()
        dbg_k = nc.dram_tensor("dbg_k", [512, S], F32, kind="ExternalOutput").ap()
        dbg_v = nc.dram_tensor("dbg_v", [128, 32 * NKV * 65], F32, kind="ExternalOutput").ap()

    st0 = ExitStack()

    def sb(stk, name, shape, dt=BF16, side=None):
        return stk.enter_context(nc.sbuf_tensor(name, list(shape), dt, side=side))

    def pt(stk, name, shape, dt=F32):
        return stk.enter_context(nc.psum_tensor(name, list(shape), dt))

    def sem(name):
        return Ctr(st0.enter_context(nc.semaphore(name)))

    iden_b = sb(st0, "iden_b", (128, 128))
    iden_f = sb(st0, "iden_f", (128, 128), F32)
    ones_f = sb(st0, "ones_f", (128, 128), F32)
    ones_b = sb(st0, "ones_b", (128, 128))
    pat_s = sb(st0, "pat_s", (128, NGT), F32)
    gcol_s = sb(st0, "gcol_s", (128, ICN), F32)
    biastbl = sb(st0, "biastbl", (128, NGT), F32)
    colA = sb(st0, "colA", (128, 16), F32)
    wabs = sb(st0, "wabs", (128, 200), F32)
    stg = sb(st0, "stg", (128, 128), F32)
    scl = sb(st0, "scl", (1, 128), F32)
    rows = None

    def row(q, half):
        return rows[32 * half:32 * half + 1, 512 * q:512 * (q + 1)]

    R_S1, R_S2, R_MEAN, R_RSTD, R_A, R_B, R_C, R_DD = range(8)

    qs = sem("qs")
    qg = sem("qg")
    qa = sem("qa")
    st = sem("st")
    sv = sem("sv")
    ss = sem("ss")
    bar = sem("bar")

    block = st0.enter_context(nc.Block())
    ENGS = ["tensor", "vector", "scalar", "gpsimd", "sync"]

    def sec(name):
        return getattr(block, name)

    def vsafe(v, inst):
        sv.inc(inst)
        v.wait_ge(sv.h, sv.n)
        return inst

    def barrier():
        for e in ENGS:
            @sec(e)
            def _(eng):
                eng.nop().then_inc(bar.h, 1)
        bar.n += 5
        tgt = bar.n
        for e in ENGS:
            @sec(e)
            def _(eng):
                eng.wait_ge(bar.h, tgt)

    def broadcast_row(slotq, dst_tile, pbank, pbuf):
        for half in range(2):
            @sec("tensor")
            def _(t, half=half):
                sv.wait(t)
                ss.wait(t)
                t.wait_ge(qs.h, qs.n)
                pbuf.acquire(t)
                _li = t.matmul(pbank[:, 0:512], ones_f[32 * half:32 * half + 1, :],
                         row(slotq, half), start=True, stop=True)
                st.inc(_li)

            @sec("vector")
            def _(v, half=half):
                st.wait(v)
                _li = v.tensor_copy(dst_tile[:, 512 * half:512 * (half + 1)],
                                  pbank[:, 0:512])
                sv.inc(_li)
                pbuf.release(sv)

    @sec("gpsimd")
    def _(g):
        for dst, src in [(iden_b[:], iden_in), (ones_b[:], ones_in)]:
            qg.inc(g.dma_start(out=dst, in_=src), 16)

    @sec("sync")
    def _(s):
        for dst, src in [(iden_f[:], iden_in), (ones_f[:], ones_in),
                         (pat_s[:], pat_in), (gcol_s[:], g_col)]:
            qs.inc(s.dma_start(out=dst, in_=src), 16)

    barrier()

    stQ = ExitStack()
    Qf = sb(stQ, "Qf", (128, DC, T), side="right")
    n_x = Qf

    stKV = ExitStack()
    Kf = sb(stKV, "Kf", (128, 4, S))
    Vt = sb(stKV, "Vt", (128, 32, NKV, 65))
    Kd = sb(stKV, "Kd", (128, 4, T))
    Vd = sb(stKV, "Vd", (128, 8, NKV, 65))

    with ExitStack() as stT:
        perm_b = sb(stT, "perm_b", (128, 128))
        e2_b = sb(stT, "e2_b", (128, 2))
        cf_b = sb(stT, "cf_b", (128, S))
        sf_b = sb(stT, "sf_b", (128, S))
        co_b = sb(stT, "co_b", (128, T))
        so_b = sb(stT, "so_b", (128, T))
        xstr = sb(stT, "xstr", (128, 2, DC, 256))
        wk_s = sb(stT, "wk_s", (128, DC, 512))
        wv_s = sb(stT, "wv_s", (128, DC, 512))
        wq_s = sb(stT, "wq_s", (128, 2, DC, 256))
        kpre = sb(stT, "kpre", (128, 2, 256))
        ktmp = sb(stT, "ktmp", (128, 256))
        ktmp2 = sb(stT, "ktmp2", (128, 256))
        sqt = sb(stT, "sqt", (128, 1024))
        pp0 = pt(stT, "pp0", (128, 512))
        pp1 = pt(stT, "pp1", (128, 512))
        pr0 = pt(stT, "pr0", (128, 256))
        pr1 = pt(stT, "pr1", (128, 256))
        pnm = pt(stT, "pnm", (128, 512))
        ptf = pt(stT, "ptf", (128, 128))

        xring = Ring([sem("xr_r0"), sem("xr_r1")], sem("rf1"), 2)
        wqring = Ring([sem("wq_r0"), sem("wq_r1")], sem("rf2"), 2)
        bpp = [Buf(sem("bpp0")), Buf(sem("bpp1"))]
        bpr = [Buf(sem("bpr0")), Buf(sem("bpr1"))]
        bkp = [Buf(sem("bkp0")), Buf(sem("bkp1"))]
        bnm = Buf(sem("bnm"))
        btf = Buf(sem("btf"))

        @sec("gpsimd")
        def _(g):
            for dst, src in [(perm_b[:], perm_in), (e2_b[:], e2_in),
                             (cf_b[:], rope_cf), (sf_b[:], rope_sf),
                             (co_b[:], rope_co), (so_b[:], rope_so)]:
                qg.inc(g.dma_start(out=dst, in_=src), 16)
            qg.inc(g.dma_start(out=wk_s[:],
                               in_=wk_d.rearrange("(c p) n -> p c n", p=128)), 16)
            qg.inc(g.dma_start(out=wv_s[:],
                               in_=wv_d.rearrange("(c p) n -> p c n", p=128)), 16)
        kvw_mark = qg.n

        NT_KV = 20

        def xsrc(i):
            if i < 16:
                return x_full.rearrange("(c p) t -> p c t", p=128)[:, :, 256 * i:256 * (i + 1)]
            j = i - 16
            return x_own.rearrange("(c p) t -> p c t", p=128)[:, :, 256 * j:256 * (j + 1)]

        @sec("gpsimd")
        def _(g):
            for i in range(NT_KV):
                slot = xring.produce(g)
                xring.mark(g.dma_start(out=xstr[:, slot, :, :], in_=xsrc(i)))

        pcnt = 0
        for ti in range(NT_KV):
            own = ti >= 16
            toff = 256 * (ti - 16) if own else 256 * ti
            cf, sf_ = (co_b, so_b) if own else (cf_b, sf_b)
            kdst = Kd if own else Kf
            for c in range(4):
                par = pcnt % 2
                pcnt += 1
                pbt, pb = (pp0, bpp[0]) if par == 0 else (pp1, bpp[1])
                prt, prb = (pr0, bpr[0]) if par == 0 else (pr1, bpr[1])
                kpb = bkp[par]

                @sec("tensor")
                def _(t, ti=ti, c=c, pbt=pbt, pb=pb):
                    xring.wait_ready(t, ti)
                    if ti == 0 and c == 0:
                        t.wait_ge(qg.h, kvw_mark)
                    pb.acquire(t)
                    for k in range(DC):
                        _li = t.matmul(pbt[:, 0:256], wk_s[:, k, 128 * c:128 * (c + 1)],
                                 xstr[:, ti % 2, k, :],
                                 start=(k == 0), stop=(k == DC - 1))
                    st.inc(_li)

                @sec("scalar")
                def _(sc, pbt=pbt, pb=pb, kpb=kpb, par=par):
                    st.wait(sc)
                    kpb.acquire(sc)
                    _li = sc.copy(kpre[:, par, :], pbt[:, 0:256])
                    ss.inc(_li)
                    pb.release(ss)

                @sec("tensor")
                def _(t, prt=prt, prb=prb, par=par):
                    ss.wait(t)
                    prb.acquire(t)
                    _li = t.matmul(prt[:], perm_b[:], kpre[:, par, :], start=True, stop=True)
                    st.inc(_li)

                @sec("vector")
                def _(v, c=c, prt=prt, prb=prb, kpb=kpb, par=par,
                      kdst=kdst, cf=cf, sf_=sf_, toff=toff):
                    st.wait(v)
                    _li = v.tensor_mul(ktmp[:], kpre[:, par, :], cf[:, toff:toff + 256])
                    _li = v.tensor_mul(ktmp2[:], prt[:], sf_[:, toff:toff + 256])
                    _li = v.tensor_add(kdst[:, c, toff:toff + 256], ktmp[:], ktmp2[:])
                    sv.inc(_li)
                    prb.release(sv)
                    kpb.release(sv)

            for m in range(2):
                par = pcnt % 2
                pcnt += 1
                pbt, pb = (pp0, bpp[0]) if par == 0 else (pp1, bpp[1])
                tc = (toff + 128 * m) // 128

                @sec("tensor")
                def _(t, ti=ti, m=m, pbt=pbt, pb=pb):
                    pb.acquire(t)
                    for k in range(DC):
                        _li = t.matmul(pbt[:], xstr[:, ti % 2, k, 128 * m:128 * (m + 1)],
                                 wv_s[:, k, :],
                                 start=(k == 0), stop=(k == DC - 1))
                    st.inc(_li)

                @sec("vector")
                def _(v, pbt=pbt, pb=pb, tc=tc, own=own):
                    st.wait(v)
                    dst = Vd if own else Vt
                    sv.inc(v.tensor_copy(
                        dst[:, tc, :, 0:64],
                        pbt[:].rearrange("p (h n) -> p h n", h=NKV)))
                    pb.release(sv)

            @sec("tensor")
            def _(t):
                sv.wait(t)
                xring.release(t, st)

        @sec("vector")
        def _(v):
            sv.wait(v)
            _li = v.memset(Vt[:, :, :, 64:65], 1.0)
            _li = v.memset(Vd[:, :, :, 64:65], 1.0)
            sv.inc(_li)

        kv_done_sv = sv.n

        @sec("gpsimd")
        def _(g):
            g.wait_ge(sv.h, kv_done_sv)
            for e8 in range(8):
                slot = wqring.produce(g)
                wqring.mark(g.dma_start(
                    out=wq_s[:, slot, :, :],
                    in_=wq_d.rearrange("(c p) n -> p c n", p=128)[:, :, 256 * e8:256 * (e8 + 1)]))
                for ot in range(4):
                    slot = xring.produce(g)
                    xring.mark(g.dma_start(out=xstr[:, slot, :, :],
                                           in_=xsrc(16 + ot)))

        for e8 in range(8):
            for ot in range(4):
                tidx = NT_KV + 4 * e8 + ot
                toff = 256 * ot
                for q2 in range(2):
                    qc = 2 * e8 + q2
                    par = pcnt % 2
                    pcnt += 1
                    pbt, pb = (pp0, bpp[0]) if par == 0 else (pp1, bpp[1])
                    prt, prb = (pr0, bpr[0]) if par == 0 else (pr1, bpr[1])
                    kpb = bkp[par]

                    @sec("tensor")
                    def _(t, e8=e8, q2=q2, tidx=tidx, pbt=pbt, pb=pb):
                        xring.wait_ready(t, tidx)
                        wqring.wait_ready(t, e8)
                        pb.acquire(t)
                        for k in range(DC):
                            _li = t.matmul(pbt[:, 0:256],
                                     wq_s[:, e8 % 2, k, 128 * q2:128 * (q2 + 1)],
                                     xstr[:, tidx % 2, k, :],
                                     start=(k == 0), stop=(k == DC - 1))
                        st.inc(_li)

                    @sec("scalar")
                    def _(sc, pbt=pbt, pb=pb, kpb=kpb, par=par):
                        st.wait(sc)
                        kpb.acquire(sc)
                        _li = sc.copy(kpre[:, par, :], pbt[:, 0:256])
                        ss.inc(_li)
                        pb.release(ss)

                    @sec("tensor")
                    def _(t, prt=prt, prb=prb, par=par):
                        ss.wait(t)
                        prb.acquire(t)
                        _li = t.matmul(prt[:], perm_b[:], kpre[:, par, :],
                                 start=True, stop=True)
                        st.inc(_li)

                    @sec("vector")
                    def _(v, qc=qc, toff=toff, prt=prt, prb=prb, kpb=kpb, par=par):
                        st.wait(v)
                        _li = v.tensor_mul(ktmp[:], kpre[:, par, :], co_b[:, toff:toff + 256])
                        _li = v.tensor_mul(ktmp2[:], prt[:], so_b[:, toff:toff + 256])
                        _li = v.tensor_add(Qf[:, qc, toff:toff + 256], ktmp[:], ktmp2[:])
                        sv.inc(_li)
                        prb.release(sv)
                        kpb.release(sv)

                @sec("tensor")
                def _(t):
                    sv.wait(t)
                    xring.release(t, st)

            @sec("tensor")
            def _(t):
                wqring.release(t, st)

        ncol = 0

        bsq = Buf()

        def normmax_tiles(tiles):
            nonlocal ncol
            first = ncol
            for ap1024 in tiles:
                @sec("scalar")
                def _(sc, ap1024=ap1024):
                    sv.wait(sc)
                    bsq.acquire(sc)
                    _li = sc.activation(sqt[:], ap1024, AF.Square)
                    ss.inc(_li)

                for j in range(2):
                    @sec("tensor")
                    def _(t, j=j):
                        ss.wait(t)
                        bnm.acquire(t)
                        _li = t.matmul(pnm[0:2, 0:512], e2_b[:],
                                 sqt[:, 512 * j:512 * (j + 1)],
                                 start=True, stop=True)
                        st.inc(_li)
                        if j == 1:
                            bsq.release(st)

                    @sec("vector")
                    def _(v, ncol=ncol):
                        st.wait(v)
                        _li = v.tensor_reduce(stg[0:2, 2 + ncol:3 + ncol], pnm[0:2, 0:512],
                                        AX.X, ALU.max)
                        sv.inc(_li)
                        bnm.release(sv)
                    ncol += 1
            return first

        @sec("vector")
        def _(v):
            sv.wait(v)
            _li = v.memset(stg[:], 0.0)
            sv.inc(_li)

        kfirst = normmax_tiles([Kf[:, c, 1024 * n:1024 * (n + 1)]
                                for c in range(4) for n in range(4)])
        qfirst = normmax_tiles([Qf[:, qc, :] for qc in range(DC)])

        for which, first, last, out_c in ((0, kfirst, qfirst, 0), (1, qfirst, ncol, 1)):
            @sec("vector")
            def _(v, first=first, last=last):
                sv.wait(v)
                vsafe(v, v.tensor_reduce(stg[0:2, 0:1], stg[0:2, 2 + first:2 + last],
                                AX.X, ALU.max))

            @sec("tensor")
            def _(t):
                sv.wait(t)
                btf.acquire(t)
                _li = t.transpose(ptf[:], stg[:], iden_f[:])
                st.inc(_li)

            @sec("vector")
            def _(v, out_c=out_c):
                st.wait(v)
                vsafe(v, v.tensor_reduce(scl[0:1, out_c:out_c + 1], ptf[0:1, 0:2],
                                AX.X, ALU.max))
                btf.release(sv)

        @sec("vector")
        def _(v):
            sv.wait(v)
            vsafe(v, v.tensor_mul(scl[0:1, 2:3], scl[0:1, 0:1], scl[0:1, 1:2]))

        @sec("scalar")
        def _(sc):
            sv.wait(sc)
            _li = sc.activation(scl[0:1, 3:4], scl[0:1, 2:3], AF.Sqrt)
            ss.inc(_li)

        @sec("vector")
        def _(v):
            ss.wait(v)
            vsafe(v, v.tensor_scalar_mul(scl[0:1, 4:5], scl[0:1, 3:4], -0.126))

        @sec("tensor")
        def _(t):
            sv.wait(t)
            bnm.acquire(t)
            _li = t.matmul(pnm[:, 0:1], ones_f[0:1, :], scl[0:1, 4:5], start=True, stop=True)
            st.inc(_li)

        @sec("vector")
        def _(v):
            st.wait(v)
            _li = v.tensor_copy(colA[:, 0:1], pnm[:, 0:1])
            sv.inc(_li)
            bnm.release(sv)
            v.wait_ge(sv.h, sv.n)
            sv.inc(v.tensor_scalar(biastbl[:], pat_s[:], colA[:, 0:1], None, ALU.add))

        if DEBUG:
            @sec("gpsimd")
            def _(g):
                g.wait_ge(sv.h, sv.n)
                qg.inc(g.dma_start(out=dbg_q.rearrange("(c p) t -> p c t", p=128),
                                   in_=Qf[:]), 16)
                qg.inc(g.dma_start(out=dbg_k.rearrange("(c p) t -> p c t", p=128),
                                   in_=Kf[:]), 16)
                qg.inc(g.dma_start(out=dbg_v[:],
                                   in_=Vt[:].rearrange("p a h n -> p (a h n)")), 16)
                qg.inc(g.dma_start(out=dbg_misc[:, 0:NGT], in_=biastbl[:]), 16)
                qg.inc(g.dma_start(out=dbg_misc[:, 56:64],
                                   in_=stg[:, 0:8]), 16)
                g.wait_ge(qg.h, qg.n)

            @sec("sync")
            def _(s):
                ss.wait(s)
                sv.wait(s)
                qs.inc(s.dma_start(out=dbg_misc[0:1, 40:45], in_=scl[0:1, 0:5]), 16)

        barrier()

    stAtt = ExitStack()
    att = sb(stAtt, "att", (128, DC, T), side="right")

    with ExitStack() as stB:
        dmask_b = sb(stB, "dmask_b", (128, 2048))
        ebuf = sb(stB, "ebuf", (128, 2, 1024))
        avsb = sb(stB, "avsb", (64, 512))
        recb = sb(stB, "recb", (1, 512))
        recf = sb(stB, "recf", (1, 512), F32)
        w1str = sb(stB, "w1str", (128, 2, 2048), F32)
        psc0 = pt(stB, "psc0", (128, 1024))
        psc1 = pt(stB, "psc1", (128, 1024))
        pav = pt(stB, "pav", (65, 512))
        pbc = pt(stB, "pbc", (128, 512))

        w1ring = Ring([sem("w1_r0"), sem("w1_r1")], sem("rf3"), 2)
        bsc = [Buf(sem("bsc0")), Buf(sem("bsc1"))]
        beb = [Buf(sem("beb0")), Buf(sem("beb1"))]
        bav = Buf(sem("bav"))
        bbc = Buf(sem("bbc"))

        @sec("gpsimd")
        def _(g):
            qg.inc(g.dma_start(out=dmask_b[:], in_=dmask_in), 16)
        dm_mark = qg.n

        NW1 = 192

        def w1_src(i):
            if i < 64:
                return wgT.rearrange("(c p) n -> p c n", p=128)[:, i // 4, 2048 * (i % 4):2048 * (i % 4 + 1)]
            if i < 128:
                j = i - 64
                return wuT.rearrange("(c p) n -> p c n", p=128)[:, j // 4, 2048 * (j % 4):2048 * (j % 4 + 1)]
            j = i - 128
            return wdT.rearrange("(c p) h -> p c h", p=128)[:, j:j + 1, :] \
                .rearrange("p c n -> p (c n)")

        @sec("sync")
        def _(s):
            for i in range(NW1):
                slot = w1ring.produce(s)
                w1ring.mark(s.dma_start(out=w1str[:, slot, :], in_=w1_src(i)))

        @sec("vector")
        def _(v):
            for i in range(NW1):
                w1ring.wait_ready(v, i)
                _li = v.tensor_reduce(wabs[:, i:i + 1], w1str[:, i % 2, :], AX.X,
                                ALU.add, apply_absolute_value=True)
                sv.inc(_li)
                w1ring.release(v, sv)

        b0heads = [h for h in range(NH) if (h // 4) % 2 == 0]
        b64heads = [h for h in range(NH) if (h // 4) % 2 == 1]
        QCH = {}
        QB = {}
        for i in range(16):
            QCH[b0heads[i]] = i
            QB[b0heads[i]] = 0
            QCH[b64heads[i]] = i
            QB[b64heads[i]] = 64
        gidx = 0
        for h in range(NH):
            kvh = h // 4
            kc = kvh // 2
            kb0 = (kvh % 2) * 64
            qc2 = QCH[h]
            qb0 = QB[h]
            for s_ in range(2):
                NG = NG0 if s_ == 0 else NG1
                qoff = 512 * s_
                for gg in range(NG):
                    par = gidx % 2
                    gidx += 1
                    psct, bscb = (psc0, bsc[0]) if par == 0 else (psc1, bsc[1])
                    bebb = beb[par]
                    diag = gg >= NG - 2
                    dg = gg - (NG - 2)
                    biascol = (NG0 if s_ == 1 else 0) + gg

                    @sec("tensor")
                    def _(t, s_=s_, gg=gg, psct=psct, bscb=bscb, diag=diag,
                          dg=dg, kc=kc, kb0=kb0, qc2=qc2, qb0=qb0, qoff=qoff,
                          gidx=gidx):
                        if gidx == 1:
                            t.wait_ge(qg.h, dm_mark)
                        bscb.acquire(t)
                        for j in range(2):
                            if diag:
                                ksl = Kd[kb0:kb0 + 64, kc,
                                         qoff + 256 * dg + 128 * j:qoff + 256 * dg + 128 * (j + 1)]
                            else:
                                ksl = Kf[kb0:kb0 + 64, kc,
                                         256 * gg + 128 * j:256 * gg + 128 * (j + 1)]
                            _li = t.matmul(psct[:, 512 * j:512 * (j + 1)], ksl,
                                     Qf[qb0:qb0 + 64, qc2, qoff:qoff + 512],
                                     start=True, stop=True)
                        st.inc(_li)

                    @sec("scalar")
                    def _(sc, par=par, psct=psct, bscb=bscb, bebb=bebb,
                          biascol=biascol):
                        st.wait(sc)
                        bebb.acquire(sc)
                        _li = sc.activation(ebuf[:, par, :], psct[:], AF.Exp,
                                      bias=biastbl[:, biascol:biascol + 1],
                                      scale=0.125)
                        ss.inc(_li)
                        bscb.release(ss)

                    if diag:
                        @sec("vector")
                        def _(v, par=par, dg=dg):
                            ss.wait(v)
                            sv.inc(v.tensor_mul(ebuf[:, par, :], ebuf[:, par, :],
                                                dmask_b[:, 1024 * dg:1024 * (dg + 1)]))

                    @sec("tensor")
                    def _(t, s_=s_, gg=gg, par=par, bebb=bebb, diag=diag,
                          dg=dg, kvh=kvh, NG=NG):
                        if diag:
                            sv.wait(t)
                        else:
                            ss.wait(t)
                        if gg == 0:
                            bav.acquire(t)
                        for j in range(2):
                            if diag:
                                vtile = Vd[:, 4 * s_ + 2 * dg + j, kvh, :]
                            else:
                                vtile = Vt[:, 2 * gg + j, kvh, :]
                            _li = t.matmul(pav[:], vtile,
                                     ebuf[:, par, 512 * j:512 * (j + 1)],
                                     start=(gg == 0 and j == 0),
                                     stop=(gg == NG - 1 and j == 1))
                        st.inc(_li)
                        bebb.release(st)

                @sec("vector")
                def _(v):
                    st.wait(v)
                    _li = v.reciprocal(recf[:], pav[64:65, :])
                    sv.inc(v.tensor_copy(recb[:], recf[:]))

                @sec("scalar")
                def _(sc):
                    st.wait(sc)
                    sv.wait(sc)
                    _li = sc.copy(avsb[:], pav[0:64, :])
                    ss.inc(_li)

                @sec("tensor")
                def _(t):
                    sv.wait(t)
                    ss.wait(t)
                    bav.release(sv, ss)
                    bbc.acquire(t)
                    _li = t.matmul(pbc[0:64, :], ones_b[0:1, 0:64], recb[:],
                             start=True, stop=True)
                    st.inc(_li)

                @sec("vector")
                def _(v, qc2=qc2, qb0=qb0, qoff=qoff):
                    st.wait(v)
                    sv.inc(v.tensor_mul(
                        att[qb0:qb0 + 64, qc2, qoff:qoff + 512],
                        avsb[:], pbc[0:64, :]))
                    bbc.release(sv)

        barrier()

        if DEBUG:
            @sec("gpsimd")
            def _(g):
                qg.inc(g.dma_start(out=dbg_att.rearrange("(c p) t -> p c t", p=128),
                                   in_=att[:]), 16)
                g.wait_ge(qg.h, qg.n)

    stKV.close()
    stRows = ExitStack()
    rows = sb(stRows, "rows", (128, 8 * 512), F32)

    with ExitStack() as stC:
        wo_s = sb(stC, "wo_s", (128, 2, DC, 256))
        xo_f = sb(stC, "xo_f", (128, DC, T), F32)
        ctile = sb(stC, "ctile", (128, 1024), F32)
        ctil2 = sb(stC, "ctil2", (128, 1024), F32)
        cbf = sb(stC, "cbf", (128, 1024))
        cbf2 = sb(stC, "cbf2", (128, 1024))
        mxf = sb(stC, "mxf", (128, 1024), F32)
        qmx = sb(stC, "qmx", (128, 16), F32)
        trow = sb(stC, "trow", (128, 512), F32)
        sxb = sb(stC, "sxb", (128, T), F32)
        bcst = sb(stC, "bcst", (128, 2, 1024), F32)
        pw0 = pt(stC, "pw0", (128, 1024))
        pw1 = pt(stC, "pw1", (128, 1024))
        prow1 = pt(stC, "prow1", (128, 512))
        prow2 = pt(stC, "prow2", (128, 512))
        pbc0 = pt(stC, "pbc0", (128, 512))
        ptf2 = pt(stC, "ptf2", (128, 512))

        woring = Ring([sem("wo_r0"), sem("wo_r1")], sem("rf4"), 2)
        bcb = Buf()
        bpw = [Buf(sem("bpw0")), Buf(sem("bpw1"))]
        brow = Buf(sem("brow"))
        bbc0 = Buf(sem("bbc0"))
        btf2 = Buf(sem("btf2"))

        @sec("gpsimd")
        def _(g):
            for e8 in range(8):
                slot = woring.produce(g)
                woring.mark(g.dma_start(
                    out=wo_s[:, slot, :, :],
                    in_=wo_d.rearrange("(c p) n -> p c n", p=128)[:, :, 256 * e8:256 * (e8 + 1)]))

        @sec("sync")
        def _(s):
            qs.inc(s.dma_start(out=xo_f[:],
                               in_=x_own.rearrange("(c p) t -> p c t", p=128)), 16)
        xo_mark = qs.n

        for dc in range(DC):
            par = dc % 2
            pwt, pwb = (pw0, bpw[0]) if par == 0 else (pw1, bpw[1])
            e8 = dc // 2
            q2 = dc % 2

            @sec("tensor")
            def _(t, dc=dc, pwt=pwt, pwb=pwb, e8=e8, q2=q2):
                woring.wait_ready(t, e8)
                pwb.acquire(t)
                for k in range(DC):
                    for th in range(2):
                        _li = t.matmul(pwt[:, 512 * th:512 * (th + 1)],
                                 wo_s[:, e8 % 2, k, 128 * q2:128 * (q2 + 1)],
                                 att[:, k, 512 * th:512 * (th + 1)],
                                 start=(k == 0), stop=(k == DC - 1))
                st.inc(_li)
                if q2 == 1:
                    woring.release(t, st)

            @sec("vector")
            def _(v, dc=dc, pwt=pwt, pwb=pwb):
                st.wait(v)
                if dc == 0:
                    v.wait_ge(qs.h, xo_mark)
                _li = v.tensor_add(xo_f[:, dc, :], pwt[:], xo_f[:, dc, :])
                sv.inc(_li)
                pwb.release(sv)

            @sec("scalar")
            def _(sc, dc=dc):
                sv.wait(sc)
                bcb.acquire(sc)
                _li = sc.copy(cbf[:], xo_f[:, dc, :])
                _li = sc.activation(cbf2[:], xo_f[:, dc, :], AF.Square)
                ss.inc(_li)

            @sec("tensor")
            def _(t, dc=dc):
                ss.wait(t)
                if dc == 0:
                    brow.acquire(t)
                for th in range(2):
                    _li = t.matmul(prow1[32 * th:32 * th + 1, 0:512], ones_b[:, 0:1],
                             cbf[:, 512 * th:512 * (th + 1)],
                             start=(dc == 0), stop=(dc == DC - 1))
                    _li = t.matmul(prow2[32 * th:32 * th + 1, 0:512], ones_b[:, 0:1],
                             cbf2[:, 512 * th:512 * (th + 1)],
                             start=(dc == 0), stop=(dc == DC - 1))
                st.inc(_li)
                bcb.release(st)

        @sec("vector")
        def _(v):
            st.wait(v)
            _li = v.tensor_copy(row(R_S1, 0), prow1[0:1, 0:512])
            _li = v.tensor_copy(row(R_S1, 1), prow1[32:33, 0:512])
            _li = v.tensor_copy(row(R_S2, 0), prow2[0:1, 0:512])
            _li = v.tensor_copy(row(R_S2, 1), prow2[32:33, 0:512])
            sv.inc(_li)
            brow.release(sv)

        def ln_rows_math():
            for half in range(2):
                @sec("vector")
                def _(v, half=half):
                    v.wait_ge(qs.h, qs.n)
                    _li = v.tensor_scalar_mul(row(R_MEAN, half), row(R_S1, half), 1.0 / D)
                    _li = v.tensor_mul(row(R_A, half), row(R_MEAN, half), row(R_MEAN, half))
                    _li = v.scalar_tensor_tensor(row(R_B, half), row(R_S2, half), 1.0 / D,
                                           row(R_A, half), ALU.mult, ALU.subtract)
                    _li = v.tensor_scalar_add(row(R_B, half), row(R_B, half), LN_EPS)
                    _li = v.reciprocal(row(R_C, half), row(R_B, half))
                    sv.inc(_li)

                @sec("scalar")
                def _(sc, half=half):
                    sv.wait(sc)
                    _li = sc.activation(row(R_RSTD, half), row(R_C, half), AF.Sqrt)
                    ss.inc(_li)

                @sec("vector")
                def _(v, half=half):
                    ss.wait(v)
                    _li = v.tensor_mul(row(R_A, half), row(R_RSTD, half), row(R_RSTD, half))
                    _li = v.tensor_mul(row(R_A, half), row(R_A, half), row(R_B, half))
                    _li = v.tensor_scalar(row(R_A, half), row(R_A, half), -0.5, 1.5,
                                    ALU.mult, ALU.add)
                    _li = v.tensor_mul(row(R_RSTD, half), row(R_RSTD, half), row(R_A, half))
                    sv.inc(_li)

        ln_rows_math()
        broadcast_row(R_MEAN, bcst[:, 0, :], pbc0, bbc0)
        broadcast_row(R_RSTD, bcst[:, 1, :], pbc0, bbc0)

        for dc in range(DC):
            @sec("vector")
            def _(v, dc=dc):
                sv.wait(v)
                _li = v.scalar_tensor_tensor(ctile[:], xo_f[:, dc, :], 0.0,
                                       bcst[:, 0, :], ALU.add, ALU.subtract)
                sv.inc(v.tensor_mul(xo_f[:, dc, :], ctile[:], bcst[:, 1, :]))

            @sec("scalar")
            def _(sc, dc=dc):
                sv.wait(sc)
                _li = sc.activation(ctil2[:], xo_f[:, dc, :], AF.Abs)
                ss.inc(_li)

            @sec("vector")
            def _(v, dc=dc):
                ss.wait(v)
                if dc == 0:
                    sv.inc(v.tensor_copy(mxf[:], ctil2[:]))
                else:
                    sv.inc(v.tensor_max(mxf[:], mxf[:], ctil2[:]))

        xln1_store_mark = [0]

        @sec("scalar")
        def _(sc):
            sv.wait(sc)
            qa.inc(sc.dma_start(out=xln1_scr.rearrange("(c p) t -> p c t", p=128),
                                in_=xo_f[:]), 16)
        xln1_store_mark[0] = qa.n

        for tb in range(8):
            @sec("tensor")
            def _(t, tb=tb):
                sv.wait(t)
                if tb % 4 == 0:
                    btf2.acquire(t)
                _li = t.transpose(ptf2[:, 128 * (tb % 4):128 * (tb % 4 + 1)],
                            mxf[:, 128 * tb:128 * (tb + 1)], iden_f[:])
                st.inc(_li)

            @sec("vector")
            def _(v, tb=tb):
                st.wait(v)
                _li = v.tensor_reduce(qmx[:, tb:tb + 1],
                                ptf2[:, 128 * (tb % 4):128 * (tb % 4 + 1)],
                                AX.X, ALU.max)
                sv.inc(_li)
                if tb % 4 == 3:
                    btf2.release(sv)

        @sec("vector")
        def _(v):
            sv.wait(v)
            vsafe(v, v.tensor_scalar_max(qmx[:, 8:16], qmx[:, 0:8], EPS_Q))
            _li = v.memset(stg[:], 0.0)
            vsafe(v, _li)
            sv.inc(v.tensor_copy(stg[:, 0:8], qmx[:, 8:16]))

        @sec("tensor")
        def _(t):
            sv.wait(t)
            btf2.acquire(t)
            _li = t.transpose(ptf2[:, 0:128], stg[:], iden_f[:])
            st.inc(_li)

        @sec("vector")
        def _(v):
            st.wait(v)
            v.wait_ge(qs.h, qs.n)
            sv.inc(v.tensor_copy(trow[0:8, 0:128], ptf2[0:8, 0:128]))
            btf2.release(sv)

        @sec("tensor")
        def _(t):
            sv.wait(t)
            btf2.acquire(t)
            for half in range(2):
                for cc in range(4):
                    _li = t.matmul(ptf2[32 * half:32 * half + 1, 128 * cc:128 * (cc + 1)],
                                   iden_f[0:8, 4 * half + cc:4 * half + cc + 1],
                                   trow[0:8, 0:128], start=True, stop=True)
            st.inc(_li)

        @sec("vector")
        def _(v):
            st.wait(v)
            _li = v.tensor_copy(row(R_C, 0), ptf2[0:1, 0:512])
            _li = v.tensor_copy(row(R_C, 1), ptf2[32:33, 0:512])
            sv.inc(_li)
            btf2.release(sv)

        for half in range(2):
            @sec("vector")
            def _(v, half=half):
                sv.wait(v)
                _li = v.reciprocal(row(R_A, half), row(R_C, half))
                sv.inc(v.tensor_scalar_mul(row(R_A, half), row(R_A, half), 127.0))

        broadcast_row(R_A, sxb[:], pbc0, bbc0)

        for dc in range(DC):
            @sec("vector")
            def _(v, dc=dc):
                sv.wait(v)
                _li = v.tensor_mul(ctile[:], xo_f[:, dc, :], sxb[:])
                _li = v.tensor_copy(ctil2[:], ctile[:])
                for k in range(7, -1, -1):
                    _li = v.tensor_single_scalar(cbf[:], ctil2[:],
                                           float(2 ** k) - 128.5, ALU.is_ge)
                    _li = v.scalar_tensor_tensor(ctil2[:], cbf[:], -float(2 ** k),
                                           ctil2[:], ALU.mult, ALU.add)
                _li = v.tensor_sub(ctile[:], ctile[:], ctil2[:])
                sv.inc(v.tensor_scalar_add(n_x[:, dc, :], ctile[:], -128.0))

        @sec("vector")
        def _(v):
            sv.wait(v)
            st.wait(v)
            v.memset(stg[:], 0.0)
            for w in range(3):
                vsafe(v, v.tensor_reduce(wabs[:, 192 + w:193 + w],
                                         wabs[:, 64 * w:64 * (w + 1)],
                                         AX.X, ALU.add))
            sv.inc(v.tensor_copy(stg[:, 0:3], wabs[:, 192:195]))

        @sec("tensor")
        def _(t):
            sv.wait(t)
            btf2.acquire(t)
            _li = t.transpose(ptf2[:, 0:128], stg[:], iden_f[:])
            st.inc(_li)

        @sec("vector")
        def _(v):
            st.wait(v)
            sv.inc(v.tensor_copy(trow[0:3, 0:128], ptf2[0:3, 0:128]))
            btf2.release(sv)

        @sec("tensor")
        def _(t):
            sv.wait(t)
            btf2.acquire(t)
            for w in range(3):
                _li = t.matmul(ptf2[0:1, 128 * w:128 * (w + 1)],
                               iden_f[0:8, w:w + 1], trow[0:8, 0:128],
                               start=True, stop=True)
            st.inc(_li)

        @sec("vector")
        def _(v):
            st.wait(v)
            _li = v.tensor_copy(rows[0:1, 512 * R_DD:512 * R_DD + 384],
                                ptf2[0:1, 0:384])
            sv.inc(_li)
            btf2.release(sv)

        @sec("vector")
        def _(v):
            sv.wait(v)
            base = 512 * R_DD
            for w in range(3):
                _li = v.tensor_reduce(scl[0:1, 8 + w:9 + w],
                                rows[0:1, base + 128 * w:base + 128 * (w + 1)],
                                AX.X, ALU.add)
            vsafe(v, _li)
            vsafe(v, v.tensor_scalar(scl[0:1, 11:14], scl[0:1, 8:11],
                                     1.0 / (II * D), EPS_Q, ALU.mult, ALU.max))
            _li = v.tensor_scalar_mul(scl[0:1, 14:17], scl[0:1, 11:14], 0.5)
            vsafe(v, v.tensor_scalar_mul(scl[0:1, 17:20], scl[0:1, 11:14], -0.5))
            vsafe(v, v.tensor_mul(scl[0:1, 20:21], scl[0:1, 11:12], scl[0:1, 11:12]))
            vsafe(v, v.tensor_mul(scl[0:1, 21:22], scl[0:1, 20:21], scl[0:1, 12:13]))

        for i, col in enumerate([14, 17, 15, 18, 16, 19]):
            @sec("tensor")
            def _(t, col=col):
                sv.wait(t)
                bbc0.acquire(t)
                _li = t.matmul(pbc0[:, 0:1], ones_f[0:1, :], scl[0:1, col:col + 1],
                         start=True, stop=True)
                st.inc(_li)

            @sec("vector")
            def _(v, i=i):
                st.wait(v)
                sv.inc(v.tensor_copy(colA[:, 1 + i:2 + i], pbc0[:, 0:1]))
                bbc0.release(sv)

        for half in range(2):
            @sec("vector")
            def _(v, half=half):
                sv.wait(v)
                _li = v.tensor_scalar_mul(row(R_B, half), row(R_C, half), 1.0 / 127.0)
                _li = v.tensor_mul(row(R_DD, half), row(R_B, half), row(R_B, half))
                _li = v.tensor_mul(row(R_DD, half), row(R_DD, half), row(R_B, half))
                sv.inc(v.tensor_scalar(row(R_DD, half), row(R_DD, half),
                                       scl[0:1, 21:22], None, ALU.mult))

        if DEBUG:
            @sec("gpsimd")
            def _(g):
                g.wait_ge(sv.h, sv.n)
                qg.inc(g.dma_start(out=dbg_nx.rearrange("(c p) t -> p c t", p=128),
                                   in_=n_x[:]), 16)
                qg.inc(g.dma_start(out=dbg_misc[:, 48:56], in_=qmx[:, 0:8]), 16)
                qg.inc(g.dma_start(out=dbg_misc[:, 32:39], in_=colA[:, 0:7]), 16)
                g.wait_ge(qg.h, qg.n)

        barrier()

    stAtt.close()

    with ExitStack() as stD:
        wstr = sb(stD, "wstr", (128, 2, DC, 256), F32)
        twg = sb(stD, "twg", (128, DC, 256))
        twu = sb(stD, "twu", (128, DC, 256))
        qscr = sb(stD, "qscr", (128, 4096))
        rg_f = sb(stD, "rg_f", (128, 1024), F32)
        rg2 = sb(stD, "rg2", (128, 1024), F32)
        rawstg = sb(stD, "rawstg", (128, 8, 1024), F32)
        raw2b = sb(stD, "raw2b", (128, 1024))
        absg = sb(stD, "absg", (128, 1024), F32)
        mxrun = sb(stD, "mxrun", (128, 1024), F32)
        trow2 = sb(stD, "trow2", (128, 512), F32)
        pg0 = pt(stD, "pg0", (128, 1024))
        pg1 = pt(stD, "pg1", (128, 1024))
        pu = pt(stD, "pu", (128, 1024))
        pm2 = pt(stD, "pm2", (128, 512))

        wdring = Ring([sem("wd_r0"), sem("wd_r1")], sem("rf5"), 2)
        btg = Buf()
        btu = Buf()
        brb = Buf()
        brg = Buf()
        bpg = [Buf(sem("bpg0")), Buf(sem("bpg1"))]
        bpu = Buf(sem("bpu"))
        bm2 = Buf(sem("bm2"))
        brs = Buf(sem("brs"), depth=2)

        @sec("sync")
        def _(s):
            for ib in range(32):
                for wsrc in (wgT, wuT):
                    slot = wdring.produce(s)
                    wdring.mark(s.dma_start(
                        out=wstr[:, slot, :, :],
                        in_=wsrc.rearrange("(c p) n -> p c n", p=128)[:, :, 256 * ib:256 * (ib + 1)]))

        for ib in range(32):
            for wi, tw, thr in ((0, twg, 1), (1, twu, 3)):
                tile_i = 2 * ib + wi

                @sec("vector")
                def _(v, tile_i=tile_i, tw=tw, thr=thr, wi=wi):
                    wdring.wait_ready(v, tile_i)
                    (btg if wi == 0 else btu).acquire(v)
                    src = wstr[:, tile_i % 2, :, :].rearrange("p c n -> p (c n)")
                    twv = tw[:].rearrange("p c n -> p (c n)")
                    _li = v.tensor_scalar(twv, src, colA[:, thr:thr + 1], None, ALU.is_ge)
                    _li = v.tensor_scalar(qscr[:], src, colA[:, thr + 1:thr + 2], None,
                                    ALU.is_le)
                    sv.inc(v.tensor_sub(twv, twv, qscr[:]))
                    wdring.release(v, sv)

            for c2 in range(2):
                ic = 2 * ib + c2
                par = ic % 2
                pgt, pgb = (pg0, bpg[0]) if par == 0 else (pg1, bpg[1])

                @sec("tensor")
                def _(t, c2=c2, pgt=pgt, pgb=pgb):
                    sv.wait(t)
                    pgb.acquire(t)
                    for k in range(DC):
                        for th in range(2):
                            _li = t.matmul(pgt[:, 512 * th:512 * (th + 1)],
                                     twg[:, k, 128 * c2:128 * (c2 + 1)],
                                     n_x[:, k, 512 * th:512 * (th + 1)],
                                     start=(k == 0), stop=(k == DC - 1))
                    st.inc(_li)
                    if c2 == 1:
                        btg.release(st)
                    bpu.acquire(t)
                    for k in range(DC):
                        for th in range(2):
                            _li = t.matmul(pu[:, 512 * th:512 * (th + 1)],
                                     twu[:, k, 128 * c2:128 * (c2 + 1)],
                                     n_x[:, k, 512 * th:512 * (th + 1)],
                                     start=(k == 0), stop=(k == DC - 1))
                    st.inc(_li)
                    if c2 == 1:
                        btu.release(st)

                @sec("scalar")
                def _(sc, pgt=pgt):
                    st.wait(sc, st.n - 1)
                    brg.acquire(sc)
                    _li = sc.activation(rg_f[:], pgt[:], AF.Relu)
                    _li = sc.activation(rg2[:], rg_f[:], AF.Square)
                    ss.inc(_li)

                @sec("vector")
                def _(v, ic=ic, pgb=pgb):
                    ss.wait(v)
                    pgb.release(ss)
                    st.wait(v)
                    if ic % 4 == 0:
                        brs.acquire(v)
                    _li = v.tensor_mul(rawstg[:, ic % 8, :], rg2[:], pu[:])
                    sv.inc(_li)
                    brg.release(sv)
                    bpu.release(sv)

                if DEBUG:
                    @sec("scalar")
                    def _(sc, ic=ic):
                        sv.wait(sc)
                        qa.inc(sc.dma_start(
                            out=dbg_rg.rearrange("(c p) t -> p c t", p=128)[:, ic:ic + 1, :],
                            in_=rg_f[:]), 16)

                @sec("scalar")
                def _(sc, ic=ic):
                    sv.wait(sc)
                    brb.acquire(sc)
                    _li = sc.activation(raw2b[:], rawstg[:, ic % 8, :], AF.Square)
                    _li = sc.activation(absg[:], rawstg[:, ic % 8, :], AF.Abs,
                                  scale=gcol_s[:, ic:ic + 1])
                    ss.inc(_li)

                @sec("tensor")
                def _(t, ic=ic):
                    ss.wait(t)
                    if ic == 0:
                        bm2.acquire(t)
                    if not NO_M2:
                        for th in range(2):
                            _li = t.matmul(pm2[32 * th:32 * th + 1, 0:512], ones_b[:, 0:1],
                                     raw2b[:, 512 * th:512 * (th + 1)],
                                     start=(ic == 0), stop=(ic == ICN - 1))
                        st.inc(_li)
                    else:
                        st.inc(t.nop())
                    brb.release(st)

                @sec("vector")
                def _(v, ic=ic):
                    ss.wait(v)
                    if ic == 0:
                        sv.inc(v.tensor_copy(mxrun[:], absg[:]))
                    else:
                        sv.inc(v.tensor_max(mxrun[:], mxrun[:], absg[:]))

                if ic % 4 == 3:
                    grp = ic // 4

                    @sec("scalar")
                    def _(sc, ic=ic, grp=grp):
                        sv.wait(sc)
                        c0 = ic - 3
                        qa.inc(sc.dma_start(
                            out=act_scr.rearrange("(c p) t -> p c t", p=128)[:, c0:c0 + 4, :],
                            in_=rawstg[:, (grp % 2) * 4:(grp % 2) * 4 + 4, :]), 16)
                        brs.release(ss, qa)
                if SERIAL_D1:
                    barrier()

        @sec("vector")
        def _(v):
            st.wait(v)
            _li = v.tensor_copy(row(R_S1, 0), pm2[0:1, 0:512])
            _li = v.tensor_copy(row(R_S1, 1), pm2[32:33, 0:512])
            sv.inc(_li)
            bm2.release(sv)

        def qmx2_col(tb):
            return stg[:, 16 + tb:17 + tb]

        for tb in range(8):
            @sec("tensor")
            def _(t, tb=tb):
                sv.wait(t)
                if tb % 4 == 0:
                    bm2.acquire(t)
                _li = t.transpose(pg0[:, 128 * (tb % 4):128 * (tb % 4 + 1)],
                            mxrun[:, 128 * tb:128 * (tb + 1)], iden_f[:])
                st.inc(_li)

            @sec("vector")
            def _(v, tb=tb):
                st.wait(v)
                _li = v.tensor_reduce(qmx2_col(tb), pg0[:, 128 * (tb % 4):128 * (tb % 4 + 1)],
                                AX.X, ALU.max)
                sv.inc(_li)
                if tb % 4 == 3:
                    bm2.release(sv)

        @sec("vector")
        def _(v):
            sv.wait(v)
            _li = v.memset(stg[:, 0:8], 0.0)
            vsafe(v, _li)
            sv.inc(v.tensor_copy(stg[:, 0:8], stg[:, 16:24]))

        @sec("tensor")
        def _(t):
            sv.wait(t)
            bm2.acquire(t)
            _li = t.transpose(pg0[:, 0:128], stg[:], iden_f[:])
            st.inc(_li)

        @sec("vector")
        def _(v):
            st.wait(v)
            v.wait_ge(qs.h, qs.n)
            _li = v.tensor_copy(trow2[0:8, 0:128], pg0[0:8, 0:128])
            sv.inc(_li)
            bm2.release(sv)

        @sec("tensor")
        def _(t):
            sv.wait(t)
            bm2.acquire(t)
            for half in range(2):
                for cc in range(4):
                    _li = t.matmul(pm2[32 * half:32 * half + 1, 128 * cc:128 * (cc + 1)],
                                   iden_f[0:8, 4 * half + cc:4 * half + cc + 1],
                                   trow2[0:8, 0:128], start=True, stop=True)
            st.inc(_li)

        @sec("vector")
        def _(v):
            st.wait(v)
            _li = v.tensor_copy(row(R_S2, 0), pm2[0:1, 0:512])
            _li = v.tensor_copy(row(R_S2, 1), pm2[32:33, 0:512])
            sv.inc(_li)
            bm2.release(sv)

        for half in range(2):
            @sec("vector")
            def _(v, half=half):
                sv.wait(v)
                _li = v.tensor_mul(row(R_A, half), row(R_DD, half), row(R_DD, half))
                _li = v.tensor_scalar_mul(row(R_B, half), row(R_S1, half), 1.0 / II)
                _li = v.tensor_mul(row(R_B, half), row(R_B, half), row(R_A, half))
                _li = v.tensor_scalar_add(row(R_B, half), row(R_B, half), SUB_EPS)
                _li = v.reciprocal(row(R_C, half), row(R_B, half))
                sv.inc(_li)

            @sec("scalar")
            def _(sc, half=half):
                sv.wait(sc)
                _li = sc.activation(row(R_RSTD, half), row(R_C, half), AF.Sqrt)
                ss.inc(_li)

            @sec("vector")
            def _(v, half=half):
                ss.wait(v)
                _li = v.tensor_mul(row(R_A, half), row(R_RSTD, half), row(R_RSTD, half))
                _li = v.tensor_mul(row(R_A, half), row(R_A, half), row(R_B, half))
                _li = v.tensor_scalar(row(R_A, half), row(R_A, half), -0.5, 1.5,
                                ALU.mult, ALU.add)
                _li = v.tensor_mul(row(R_RSTD, half), row(R_RSTD, half), row(R_A, half))
                _li = v.tensor_mul(row(R_MEAN, half), row(R_DD, half), row(R_RSTD, half))
                _li = v.tensor_mul(row(R_B, half), row(R_MEAN, half), row(R_S2, half))
                _li = v.tensor_scalar_max(row(R_B, half), row(R_B, half), EPS_Q)
                _li = v.reciprocal(row(R_C, half), row(R_B, half))
                _li = v.tensor_scalar_mul(row(R_C, half), row(R_C, half), 127.0)
                _li = v.tensor_mul(row(R_A, half), row(R_MEAN, half), row(R_C, half))
                _li = v.tensor_scalar_mul(row(R_B, half), row(R_B, half), 1.0 / 127.0)
                sv.inc(v.tensor_scalar(row(R_B, half), row(R_B, half),
                                       scl[0:1, 13:14], None, ALU.mult))

        barrier()

    stQ.close()

    with ExitStack() as stD2:
        fs2b = sb(stD2, "fs2b", (128, T), F32)
        dsb = sb(stD2, "dsb", (128, T), F32)
        wdstr = sb(stD2, "wdstr", (128, 2, 8, 512), F32)
        twd = sb(stD2, "twd", (128, 8, 512))
        qscr2 = sb(stD2, "qscr2", (128, 4096))
        n_act = sb(stD2, "n_act", (128, ICN, 512))
        actrd = sb(stD2, "actrd", (128, 4, 512), F32)
        xq_t = sb(stD2, "xq_t", (128, 4, 512), F32)
        y2stg = sb(stD2, "y2stg", (128, 4, 512), F32)
        ct1 = sb(stD2, "ct1", (128, 512), F32)
        cta = sb(stD2, "cta", (128, 2, 2048), F32)
        cti = sb(stD2, "cti", (128, 2, 2048), I8)
        ctb = sb(stD2, "ctb", (128, 2, 2048))
        ctr = sb(stD2, "ctr", (128, 2048), F32)
        pd = [pt(stD2, "pd%d" % i, (128, 512)) for i in range(4)]
        px0 = pt(stD2, "px0", (128, 1024))
        px1 = pt(stD2, "px1", (128, 1024))

        bbx = Buf(sem("bbx"))
        by2 = Buf()
        bpd = Buf(sem("bpd"))
        btw = Buf(sem("btw"))
        wd2ring = Ring([sem("wd2_r0"), sem("wd2_r1")], sem("rf6"), 2)
        acring = Ring([sem("ac_r0")], sem("rf7"), 1)
        xqring = Ring([sem("xq_r0")], sem("rf8"), 1)
        castq_mark = {}

        broadcast_row(R_A, fs2b[:], px0, bbx)
        broadcast_row(R_B, dsb[:], px0, bbx)

        for th in range(2):
            for blk in range(16):
                @sec("scalar")
                def _(sc, th=th, blk=blk):
                    acring.produce(sc)
                    acring.mark(sc.dma_start(
                        out=actrd[:],
                        in_=act_scr.rearrange("(c p) t -> p c t", p=128)
                        [:, 4 * blk:4 * (blk + 1), 512 * th:512 * (th + 1)]))

                gblk = 16 * th + blk
                par = gblk % 2

                @sec("vector")
                def _(v, th=th, blk=blk, gblk=gblk, par=par):
                    acring.wait_ready(v, gblk)
                    if gblk >= 2:
                        v.wait_ge(qg.h, castq_mark[gblk - 2])
                    for c4 in range(4):
                        ic = 4 * blk + c4
                        _li = v.scalar_tensor_tensor(
                            cta[:, par, 512 * c4:512 * (c4 + 1)], actrd[:, c4, :],
                            gcol_s[:, ic:ic + 1],
                            fs2b[:, 512 * th:512 * (th + 1)],
                            ALU.mult, ALU.mult)
                    sv.inc(_li)
                    acring.release(v, sv)

                @sec("gpsimd")
                def _(g, par=par, gblk=gblk):
                    g.wait_ge(sv.h, sv.n)
                    g.wait_ge(qg.h, qg.n)
                    qg.inc(g.dma_start(out=cti[:, par, :], in_=cta[:, par, :]), 16)
                    g.wait_ge(qg.h, qg.n)
                    qg.inc(g.dma_start(out=ctb[:, par, :], in_=cti[:, par, :]), 16)
                    castq_mark[gblk] = qg.n

                @sec("vector")
                def _(v, blk=blk, par=par):
                    v.wait_ge(qg.h, qg.n)
                    dst = n_act[:, 4 * blk:4 * (blk + 1), :].rearrange("p c n -> p (c n)")
                    src = cta[:, par, :]
                    cab = qscr2[:, 2048:4096]
                    cbb = qscr2[:, 0:2048]
                    _li = v.tensor_copy(ctr[:], ctb[:, par, :])
                    _li = v.tensor_sub(ctr[:], src, ctr[:])
                    _li = v.tensor_single_scalar(cab, ctr[:], 0.5, ALU.is_ge)
                    _li = v.tensor_single_scalar(cbb, ctr[:], -0.5, ALU.is_lt)
                    _li = v.tensor_add(cab, ctb[:, par, :], cab)
                    _li = v.tensor_sub(dst, cab, cbb)
                    sv.inc(_li)

            for hq in range(4):
                @sec("sync")
                def _(s, th=th, hq=hq):
                    if th == 0 and hq == 0:
                        s.wait_ge(qa.h, xln1_store_mark[0])
                    xqring.produce(s)
                    xqring.mark(s.dma_start(
                        out=xq_t[:],
                        in_=xln1_scr.rearrange("(c p) t -> p c t", p=128)
                        [:, 4 * hq:4 * (hq + 1), 512 * th:512 * (th + 1)]))

                for ib8 in range(8):
                    tile_i = (th * 4 + hq) * 8 + ib8

                    @sec("sync")
                    def _(s, hq=hq, ib8=ib8):
                        slot = wd2ring.produce(s)
                        wd2ring.mark(s.dma_start(
                            out=wdstr[:, slot, :, :],
                            in_=wdT.rearrange("(c p) h -> p c h", p=128)
                            [:, 8 * ib8:8 * (ib8 + 1), 512 * hq:512 * (hq + 1)]))

                    @sec("vector")
                    def _(v, tile_i=tile_i):
                        wd2ring.wait_ready(v, tile_i)
                        btw.acquire(v)
                        src = wdstr[:, tile_i % 2, :, :].rearrange("p c n -> p (c n)")
                        twv = twd[:].rearrange("p c n -> p (c n)")
                        _li = v.tensor_scalar(twv, src, colA[:, 5:6], None, ALU.is_ge)
                        _li = v.tensor_scalar(qscr2[:], src, colA[:, 6:7], None, ALU.is_le)
                        sv.inc(v.tensor_sub(twv, twv, qscr2[:]))
                        wd2ring.release(v, sv)

                    @sec("tensor")
                    def _(t, ib8=ib8):
                        sv.wait(t)
                        if ib8 == 0:
                            bpd.acquire(t)
                        for ic8 in range(8):
                            ic = 8 * ib8 + ic8
                            for hc in range(4):
                                _li = t.matmul(pd[hc][:, 0:512],
                                         twd[:, ic8, 128 * hc:128 * (hc + 1)],
                                         n_act[:, ic, :],
                                         start=(ic == 0), stop=(ic == ICN - 1))
                        st.inc(_li)
                        btw.release(st)

                for hc in range(4):
                    @sec("vector")
                    def _(v, th=th, hq=hq, hc=hc):
                        st.wait(v)
                        xqring.wait_ready(v, th * 4 + hq)
                        if hc == 0:
                            by2.acquire(v)
                        _li = v.tensor_mul(ct1[:], pd[hc][:, 0:512],
                                     dsb[:, 512 * th:512 * (th + 1)])
                        _li = v.tensor_add(y2stg[:, hc, :], ct1[:], xq_t[:, hc, :])
                        sv.inc(_li)
                        if hc == 3:
                            bpd.release(sv)
                            xqring.release(v, sv)

                @sec("scalar")
                def _(sc, th=th, hq=hq):
                    sv.wait(sc)
                    qa.inc(sc.dma_start(
                        out=y2_scr.rearrange("(c p) t -> p c t", p=128)
                        [:, 4 * hq:4 * (hq + 1), 512 * th:512 * (th + 1)],
                        in_=y2stg[:]), 16)
                    by2.release(qa)

        barrier()

    with ExitStack() as stE:
        y2f = sb(stE, "y2f", (128, DC, T), F32)
        cbf1k = sb(stE, "cbf1k", (128, 1024))
        cbf1k2 = sb(stE, "cbf1k2", (128, 1024))
        bcst2 = sb(stE, "bcst2", (128, 2, 1024), F32)
        ytm = sb(stE, "ytm", (128, 2, 2048), F32)
        ctk = sb(stE, "ctk", (128, 1024), F32)
        trow3 = sb(stE, "trow3", (128, 512), F32)
        pe0 = pt(stE, "pe0", (128, 512))
        pe1 = pt(stE, "pe1", (128, 512))
        pe2 = pt(stE, "pe2", (128, 512))
        pe3 = pt(stE, "pe3", (128, 512))

        brw2 = Buf(sem("brw2"))
        bce = Buf()
        bbx2 = Buf(sem("bbx2"))
        bpe = [Buf(sem("bpe0")), Buf(sem("bpe1"))]

        @sec("sync")
        def _(s):
            qa.wait(s)
            qs.inc(s.dma_start(out=y2f[:],
                               in_=y2_scr.rearrange("(c p) t -> p c t", p=128)), 16)
        y2_mark = qs.n

        for dc in range(DC):
            @sec("scalar")
            def _(sc, dc=dc):
                if dc == 0:
                    sc.wait_ge(qs.h, y2_mark)
                bce.acquire(sc)
                _li = sc.copy(cbf1k[:], y2f[:, dc, :])
                _li = sc.activation(cbf1k2[:], y2f[:, dc, :], AF.Square)
                ss.inc(_li)

            @sec("tensor")
            def _(t, dc=dc):
                ss.wait(t)
                if dc == 0:
                    brw2.acquire(t)
                for th in range(2):
                    _li = t.matmul(pe0[32 * th:32 * th + 1, 0:512], ones_b[:, 0:1],
                             cbf1k[:, 512 * th:512 * (th + 1)],
                             start=(dc == 0), stop=(dc == DC - 1))
                    _li = t.matmul(pe1[32 * th:32 * th + 1, 0:512], ones_b[:, 0:1],
                             cbf1k2[:, 512 * th:512 * (th + 1)],
                             start=(dc == 0), stop=(dc == DC - 1))
                st.inc(_li)
                bce.release(st)

        @sec("vector")
        def _(v):
            st.wait(v)
            _li = v.tensor_copy(row(R_S1, 0), pe0[0:1, 0:512])
            _li = v.tensor_copy(row(R_S1, 1), pe0[32:33, 0:512])
            _li = v.tensor_copy(row(R_S2, 0), pe1[0:1, 0:512])
            _li = v.tensor_copy(row(R_S2, 1), pe1[32:33, 0:512])
            sv.inc(_li)
            brw2.release(sv)

        for half in range(2):
            @sec("vector")
            def _(v, half=half):
                sv.wait(v)
                _li = v.tensor_scalar_mul(row(R_MEAN, half), row(R_S1, half), 1.0 / D)
                _li = v.tensor_mul(row(R_A, half), row(R_MEAN, half), row(R_MEAN, half))
                _li = v.scalar_tensor_tensor(row(R_B, half), row(R_S2, half), 1.0 / D,
                                       row(R_A, half), ALU.mult, ALU.subtract)
                _li = v.tensor_scalar_add(row(R_B, half), row(R_B, half), LN_EPS)
                _li = v.reciprocal(row(R_C, half), row(R_B, half))
                sv.inc(_li)

            @sec("scalar")
            def _(sc, half=half):
                sv.wait(sc)
                _li = sc.activation(row(R_RSTD, half), row(R_C, half), AF.Sqrt)
                ss.inc(_li)

            @sec("vector")
            def _(v, half=half):
                ss.wait(v)
                _li = v.tensor_mul(row(R_A, half), row(R_RSTD, half), row(R_RSTD, half))
                _li = v.tensor_mul(row(R_A, half), row(R_A, half), row(R_B, half))
                _li = v.tensor_scalar(row(R_A, half), row(R_A, half), -0.5, 1.5,
                                ALU.mult, ALU.add)
                _li = v.tensor_mul(row(R_RSTD, half), row(R_RSTD, half), row(R_A, half))
                sv.inc(_li)

        broadcast_row(R_MEAN, bcst2[:, 0, :], pe2, bbx2)
        broadcast_row(R_RSTD, bcst2[:, 1, :], pe2, bbx2)

        for dc in range(DC):
            @sec("vector")
            def _(v, dc=dc):
                sv.wait(v)
                _li = v.scalar_tensor_tensor(ctk[:], y2f[:, dc, :], 0.0,
                                       bcst2[:, 0, :], ALU.add, ALU.subtract)
                sv.inc(v.tensor_mul(y2f[:, dc, :], ctk[:], bcst2[:, 1, :]))

        ytm_mark = {}
        for tb in range(8):
            for dc in range(DC):
                par = dc % 2
                pxt = pe2 if par == 0 else pe3
                pxb = bpe[par]

                @sec("tensor")
                def _(t, tb=tb, dc=dc, pxt=pxt, pxb=pxb):
                    sv.wait(t)
                    pxb.acquire(t)
                    _li = t.transpose(pxt[:, 0:128], y2f[:, dc, 128 * tb:128 * (tb + 1)],
                                iden_f[:])
                    st.inc(_li)

                @sec("vector")
                def _(v, tb=tb, dc=dc, pxt=pxt, pxb=pxb):
                    st.wait(v)
                    if dc == 0 and tb >= 2:
                        v.wait_ge(qa.h, ytm_mark[tb - 2])
                    _li = v.tensor_copy(ytm[:, tb % 2, 128 * dc:128 * (dc + 1)],
                                      pxt[:, 0:128])
                    sv.inc(_li)
                    pxb.release(sv)

            @sec("scalar")
            def _(sc, tb=tb):
                sv.wait(sc)
                qa.inc(sc.dma_start(out=y_out[128 * tb:128 * (tb + 1), :],
                                    in_=ytm[:, tb % 2, :]), 16)
            ytm_mark[tb] = qa.n
        @sec("scalar")
        def _(sc):
            qa.wait(sc)

    stRows.close()
    st0.close()
    return nc


_NC_CACHE = None


def _get_nc():
    global _NC_CACHE
    if _NC_CACHE is None:
        _NC_CACHE = build_program()
    return _NC_CACHE


def _host_tables():
    import jax
    import jax.numpy as jnp
    with jax.default_device(jax.devices("cpu")[0]):
        inv_freq = 1.0 / (10000.0 ** (jnp.arange(0, HD, 2, dtype=jnp.float32) / HD))
        ang = jnp.arange(S, dtype=jnp.float32)[:, None] * inv_freq[None, :]
        c_tab = np.asarray(jnp.cos(ang), dtype=np.float32)
        s_tab = np.asarray(jnp.sin(ang), dtype=np.float32)

    def rope_expand(positions):
        c = c_tab[positions]
        s = s_tab[positions]
        ce = np.empty((HD, len(positions)), np.float32)
        se = np.empty((HD, len(positions)), np.float32)
        ce[0::2] = c.T
        ce[1::2] = c.T
        se[0::2] = -s.T
        se[1::2] = s.T
        cf = np.vstack([ce, ce])
        sf = np.vstack([se, se])
        return np.ascontiguousarray(cf), np.ascontiguousarray(sf)

    cf_full, sf_full = rope_expand(np.arange(S))
    perm = np.zeros((128, 128), np.float32)
    for i in range(0, 128, 2):
        perm[i, i + 1] = 1.0
        perm[i + 1, i] = 1.0
    iden = np.eye(128, dtype=np.float32)
    ones = np.ones((128, 128), np.float32)
    e2 = np.zeros((128, 2), np.float32)
    e2[0:64, 0] = 1.0
    e2[64:128, 1] = 1.0
    dm = np.zeros((128, 2048), np.float32)
    q = np.arange(512)
    for kb in range(4):
        k_abs = 128 * kb + np.arange(128)
        dm[:, 512 * kb:512 * (kb + 1)] = (k_abs[:, None] <= q[None, :])
    return cf_full, sf_full, perm, iden, ones, e2, dm, rope_expand


def kernel(L0_attn_x, L0_attn_wq, L0_attn_wk, L0_attn_wv, L0_attn_wo,
           L0_ffn_w_gate, L0_ffn_w_up, L0_ffn_w_down, L0_ffn_subln_weight):
    x = np.ascontiguousarray(np.asarray(L0_attn_x, dtype=np.float32))
    wq = np.ascontiguousarray(np.asarray(L0_attn_wq, dtype=np.float32))
    wk = np.ascontiguousarray(np.asarray(L0_attn_wk, dtype=np.float32))
    wv = np.ascontiguousarray(np.asarray(L0_attn_wv, dtype=np.float32))
    wo = np.ascontiguousarray(np.asarray(L0_attn_wo, dtype=np.float32))
    wg = np.asarray(L0_ffn_w_gate, dtype=np.float32)
    wu = np.asarray(L0_ffn_w_up, dtype=np.float32)
    wd = np.asarray(L0_ffn_w_down, dtype=np.float32)
    g = np.asarray(L0_ffn_subln_weight, dtype=np.float32)

    b0heads = [h for h in range(NH) if (h // 4) % 2 == 0]
    b64heads = [h for h in range(NH) if (h // 4) % 2 == 1]
    perm_a = np.empty(D, np.int64)
    for i in range(16):
        perm_a[128 * i:128 * i + 64] = 64 * b0heads[i] + np.arange(64)
        perm_a[128 * i + 64:128 * (i + 1)] = 64 * b64heads[i] + np.arange(64)
    wq = np.ascontiguousarray(wq[:, perm_a])
    wo = np.ascontiguousarray(wo[perm_a, :])
    xT = [np.ascontiguousarray(x[b].T) for b in range(2)]
    wgT_h = np.ascontiguousarray(wg.T)
    wuT_h = np.ascontiguousarray(wu.T)
    wdT_h = np.ascontiguousarray(wd.T)
    g_col_h = np.ascontiguousarray(g.reshape(ICN, 128).T)

    cf_full, sf_full, perm, iden, ones, e2, dm, rope_expand = _host_tables()

    in_maps = []
    meta = []
    for c in range(8):
        b = c // 4
        j = c % 4
        pA, pB = j, 4 + j
        meta.append((b, pA, pB))
        pos = np.concatenate([np.arange(512 * pA, 512 * (pA + 1)),
                              np.arange(512 * pB, 512 * (pB + 1))])
        xo = np.ascontiguousarray(
            np.concatenate([xT[b][:, 512 * pA:512 * (pA + 1)],
                            xT[b][:, 512 * pB:512 * (pB + 1)]], axis=1))
        co, so = rope_expand(pos)
        pat = np.zeros((128, NGT), np.float32)
        for s_, p, ngf, base in ((0, pA, 6, 0), (1, pB, 14, NG0)):
            need = 2 * p
            for gg_ in range(ngf):
                if gg_ >= need:
                    pat[:, base + gg_] = NEG_BIG
        in_maps.append({
            "x_own": xo, "x_full": xT[b],
            "wq": wq, "wk": wk, "wv": wv, "wo": wo,
            "wgT": wgT_h, "wuT": wuT_h, "wdT": wdT_h, "g_col": g_col_h,
            "rope_cf": cf_full, "rope_sf": sf_full,
            "rope_co": co, "rope_so": so,
            "perm": perm, "iden": iden, "ones": ones, "e2": e2,
            "dmask": dm, "pat": pat,
        })

    nc = _get_nc()
    res = run_bass_kernel_spmd(nc, in_maps, core_ids=list(range(8)))

    out = np.empty((2, S, D), np.float32)
    for c in range(8):
        b, pA, pB = meta[c]
        yc = np.asarray(res.results[c]["y"])
        out[b, 512 * pA:512 * (pA + 1), :] = yc[0:512]
        out[b, 512 * pB:512 * (pB + 1), :] = yc[512:1024]
    return out


# version 09
# speedup vs baseline: 1.0609x; 1.0026x over previous
import numpy as np
from contextlib import ExitStack

import concourse.bass as bass
import concourse.mybir as mybir
from concourse.bass_utils import run_bass_kernel_spmd

F32 = mybir.dt.float32
BF16 = mybir.dt.bfloat16
I8 = mybir.dt.int8
AF = mybir.ActivationFunctionType
ALU = mybir.AluOpType
AX = mybir.AxisListType

D = 2048
S = 4096
T = 1024
NH = 32
NKV = 8
HD = 64
II = 8192
DC = D // 128
ICN = II // 128
LN_EPS = 1e-5
SUB_EPS = 1e-6
EPS_Q = 1e-5
NEG_BIG = -1.0e30
NG0, NG1 = 8, 16
NGT = NG0 + NG1


class Ctr:
    def __init__(self, h):
        self.h = h
        self.n = 0

    def inc(self, inst, k=1):
        inst.then_inc(self.h, k)
        self.n += k
        return inst

    def wait(self, eng, n=None):
        eng.wait_ge(self.h, self.n if n is None else n)


class Buf:

    def __init__(self, ctr=None, depth=1):
        self.acq = 0
        self.depth = depth
        self.marks = []

    def acquire(self, eng):
        idx = self.acq - self.depth
        if 0 <= idx < len(self.marks):
            for c, n in self.marks[idx]:
                eng.wait_ge(c.h, n)
        elif idx >= len(self.marks):
            raise RuntimeError("Buf.acquire before matching release")
        self.acq += 1

    def release(self, *ctrs):
        self.marks.append([(c, c.n) for c in ctrs])


class Ring:

    def __init__(self, rdys, fre, n):
        self.rdys = rdys
        self.fre = fre
        self.n = n
        self.p = 0

    def produce(self, eng):
        slot = self.p % self.n
        if self.p >= self.n:
            eng.wait_ge(self.fre.h, self.p - self.n + 1)
        self.p += 1
        return slot

    def mark(self, inst):
        slot = (self.p - 1) % self.n
        self.rdys[slot].inc(inst, 16)

    def wait_ready(self, eng, i):
        slot = i % self.n
        eng.wait_ge(self.rdys[slot].h, 16 * (i // self.n + 1))

    def release(self, eng, ctr):
        eng.wait_ge(ctr.h, ctr.n)
        self.fre.inc(eng.nop())


DEBUG = False
SERIAL_D1 = False
NO_M2 = False


def build_program():
    nc = bass.Bass("TRN2", target_bir_lowering=False, detect_race_conditions=False)
    dkind = "ExternalOutput" if DEBUG else "Internal"

    def din(name, shape):
        return nc.dram_tensor(name, list(shape), F32, kind="ExternalInput").ap()

    x_own = din("x_own", (D, T))
    x_full = din("x_full", (D, S))
    wq_d = din("wq", (D, D))
    wk_d = din("wk", (D, NKV * HD))
    wv_d = din("wv", (D, NKV * HD))
    wo_d = din("wo", (D, D))
    wgT = din("wgT", (D, II))
    wuT = din("wuT", (D, II))
    wdT = din("wdT", (II, D))
    g_col = din("g_col", (128, ICN))
    rope_cf = din("rope_cf", (128, S))
    rope_sf = din("rope_sf", (128, S))
    rope_co = din("rope_co", (128, T))
    rope_so = din("rope_so", (128, T))
    perm_in = din("perm", (128, 128))
    iden_in = din("iden", (128, 128))
    ones_in = din("ones", (128, 128))
    e2_in = din("e2", (128, 2))
    dmask_in = din("dmask", (128, 2048))
    pat_in = din("pat", (128, NGT))
    y_out = nc.dram_tensor("y", [T, D], F32, kind="ExternalOutput").ap()
    act_scr = nc.dram_tensor("act_scr", [II, T], F32, kind=dkind).ap()
    xln1_scr = nc.dram_tensor("xln1_scr", [D, T], F32, kind=dkind).ap()
    y2_scr = nc.dram_tensor("y2_scr", [D, T], F32, kind=dkind).ap()
    if DEBUG:
        dbg_att = nc.dram_tensor("dbg_att", [D, T], F32, kind="ExternalOutput").ap()
        dbg_misc = nc.dram_tensor("dbg_misc", [128, 64], F32, kind="ExternalOutput").ap()
        dbg_nx = nc.dram_tensor("dbg_nx", [D, T], F32, kind="ExternalOutput").ap()
        dbg_q = nc.dram_tensor("dbg_q", [D, T], F32, kind="ExternalOutput").ap()
        dbg_rg = nc.dram_tensor("dbg_rg", [II, T], F32, kind="ExternalOutput").ap()
        dbg_k = nc.dram_tensor("dbg_k", [512, S], F32, kind="ExternalOutput").ap()
        dbg_v = nc.dram_tensor("dbg_v", [128, 32 * NKV * 65], F32, kind="ExternalOutput").ap()

    st0 = ExitStack()

    def sb(stk, name, shape, dt=BF16, side=None):
        return stk.enter_context(nc.sbuf_tensor(name, list(shape), dt, side=side))

    def pt(stk, name, shape, dt=F32):
        return stk.enter_context(nc.psum_tensor(name, list(shape), dt))

    def sem(name):
        return Ctr(st0.enter_context(nc.semaphore(name)))

    iden_b = sb(st0, "iden_b", (128, 128))
    iden_f = sb(st0, "iden_f", (128, 128), F32)
    ones_f = sb(st0, "ones_f", (128, 128), F32)
    ones_b = sb(st0, "ones_b", (128, 128))
    pat_s = sb(st0, "pat_s", (128, NGT), F32)
    gcol_s = sb(st0, "gcol_s", (128, ICN), F32)
    biastbl = sb(st0, "biastbl", (128, NGT), F32)
    colA = sb(st0, "colA", (128, 16), F32)
    wabs = sb(st0, "wabs", (128, 200), F32)
    stg = sb(st0, "stg", (128, 128), F32)
    scl = sb(st0, "scl", (1, 128), F32)
    rows = None

    def row(q, half):
        return rows[32 * half:32 * half + 1, 512 * q:512 * (q + 1)]

    R_S1, R_S2, R_MEAN, R_RSTD, R_A, R_B, R_C, R_DD = range(8)

    qs = sem("qs")
    qg = sem("qg")
    qa = sem("qa")
    st = sem("st")
    sv = sem("sv")
    ss = sem("ss")
    bar = sem("bar")

    block = st0.enter_context(nc.Block())
    ENGS = ["tensor", "vector", "scalar", "gpsimd", "sync"]

    def sec(name):
        return getattr(block, name)

    def vsafe(v, inst):
        sv.inc(inst)
        v.wait_ge(sv.h, sv.n)
        return inst

    def barrier():
        for e in ENGS:
            @sec(e)
            def _(eng):
                eng.nop().then_inc(bar.h, 1)
        bar.n += 5
        tgt = bar.n
        for e in ENGS:
            @sec(e)
            def _(eng):
                eng.wait_ge(bar.h, tgt)

    def broadcast_row(slotq, dst_tile, pbank, pbuf):
        for half in range(2):
            @sec("tensor")
            def _(t, half=half):
                sv.wait(t)
                ss.wait(t)
                t.wait_ge(qs.h, qs.n)
                pbuf.acquire(t)
                _li = t.matmul(pbank[:, 0:512], ones_f[32 * half:32 * half + 1, :],
                         row(slotq, half), start=True, stop=True)
                st.inc(_li)

            @sec("vector")
            def _(v, half=half):
                st.wait(v)
                _li = v.tensor_copy(dst_tile[:, 512 * half:512 * (half + 1)],
                                  pbank[:, 0:512])
                sv.inc(_li)
                pbuf.release(sv)

    @sec("gpsimd")
    def _(g):
        for dst, src in [(iden_b[:], iden_in), (ones_b[:], ones_in)]:
            qg.inc(g.dma_start(out=dst, in_=src), 16)

    @sec("sync")
    def _(s):
        for dst, src in [(iden_f[:], iden_in), (ones_f[:], ones_in),
                         (pat_s[:], pat_in), (gcol_s[:], g_col)]:
            qs.inc(s.dma_start(out=dst, in_=src), 16)

    barrier()

    stQ = ExitStack()
    Qf = sb(stQ, "Qf", (128, DC, T), side="right")
    n_x = Qf

    stKV = ExitStack()
    Kf = sb(stKV, "Kf", (128, 4, S))
    Vt = sb(stKV, "Vt", (128, 32, NKV, 65))
    Kd = sb(stKV, "Kd", (128, 4, T))
    Vd = sb(stKV, "Vd", (128, 8, NKV, 65))

    with ExitStack() as stT:
        perm_b = sb(stT, "perm_b", (128, 128))
        e2_b = sb(stT, "e2_b", (128, 2))
        cf_b = sb(stT, "cf_b", (128, S))
        sf_b = sb(stT, "sf_b", (128, S))
        co_b = sb(stT, "co_b", (128, T))
        so_b = sb(stT, "so_b", (128, T))
        xstr = sb(stT, "xstr", (128, 2, DC, 256))
        wk_s = sb(stT, "wk_s", (128, DC, 512))
        wv_s = sb(stT, "wv_s", (128, DC, 512))
        wq_s = sb(stT, "wq_s", (128, 2, DC, 256))
        kpre = sb(stT, "kpre", (128, 2, 256))
        ktmp = sb(stT, "ktmp", (128, 256))
        ktmp2 = sb(stT, "ktmp2", (128, 256))
        sqt = sb(stT, "sqt", (128, 1024))
        pp0 = pt(stT, "pp0", (128, 512))
        pp1 = pt(stT, "pp1", (128, 512))
        pr0 = pt(stT, "pr0", (128, 256))
        pr1 = pt(stT, "pr1", (128, 256))
        pnm = pt(stT, "pnm", (128, 512))
        ptf = pt(stT, "ptf", (128, 128))

        xring = Ring([sem("xr_r0"), sem("xr_r1")], sem("rf1"), 2)
        wqring = Ring([sem("wq_r0"), sem("wq_r1")], sem("rf2"), 2)
        bpp = [Buf(sem("bpp0")), Buf(sem("bpp1"))]
        bpr = [Buf(sem("bpr0")), Buf(sem("bpr1"))]
        bkp = [Buf(sem("bkp0")), Buf(sem("bkp1"))]
        bnm = Buf(sem("bnm"))
        btf = Buf(sem("btf"))

        @sec("gpsimd")
        def _(g):
            for dst, src in [(perm_b[:], perm_in), (e2_b[:], e2_in),
                             (cf_b[:], rope_cf), (sf_b[:], rope_sf),
                             (co_b[:], rope_co), (so_b[:], rope_so)]:
                qg.inc(g.dma_start(out=dst, in_=src), 16)
            qg.inc(g.dma_start(out=wk_s[:],
                               in_=wk_d.rearrange("(c p) n -> p c n", p=128)), 16)
            qg.inc(g.dma_start(out=wv_s[:],
                               in_=wv_d.rearrange("(c p) n -> p c n", p=128)), 16)
        kvw_mark = qg.n

        NT_KV = 20

        def xsrc(i):
            if i < 16:
                return x_full.rearrange("(c p) t -> p c t", p=128)[:, :, 256 * i:256 * (i + 1)]
            j = i - 16
            return x_own.rearrange("(c p) t -> p c t", p=128)[:, :, 256 * j:256 * (j + 1)]

        @sec("gpsimd")
        def _(g):
            for i in range(NT_KV):
                slot = xring.produce(g)
                xring.mark(g.dma_start(out=xstr[:, slot, :, :], in_=xsrc(i)))

        pcnt = 0
        for ti in range(NT_KV):
            own = ti >= 16
            toff = 256 * (ti - 16) if own else 256 * ti
            cf, sf_ = (co_b, so_b) if own else (cf_b, sf_b)
            kdst = Kd if own else Kf
            for c in range(4):
                par = pcnt % 2
                pcnt += 1
                pbt, pb = (pp0, bpp[0]) if par == 0 else (pp1, bpp[1])
                prt, prb = (pr0, bpr[0]) if par == 0 else (pr1, bpr[1])
                kpb = bkp[par]

                @sec("tensor")
                def _(t, ti=ti, c=c, pbt=pbt, pb=pb):
                    xring.wait_ready(t, ti)
                    if ti == 0 and c == 0:
                        t.wait_ge(qg.h, kvw_mark)
                    pb.acquire(t)
                    for k in range(DC):
                        _li = t.matmul(pbt[:, 0:256], wk_s[:, k, 128 * c:128 * (c + 1)],
                                 xstr[:, ti % 2, k, :],
                                 start=(k == 0), stop=(k == DC - 1))
                    st.inc(_li)

                @sec("scalar")
                def _(sc, pbt=pbt, pb=pb, kpb=kpb, par=par):
                    st.wait(sc)
                    kpb.acquire(sc)
                    _li = sc.copy(kpre[:, par, :], pbt[:, 0:256])
                    ss.inc(_li)
                    pb.release(ss)

                @sec("tensor")
                def _(t, prt=prt, prb=prb, par=par):
                    ss.wait(t)
                    prb.acquire(t)
                    _li = t.matmul(prt[:], perm_b[:], kpre[:, par, :], start=True, stop=True)
                    st.inc(_li)

                @sec("vector")
                def _(v, c=c, prt=prt, prb=prb, kpb=kpb, par=par,
                      kdst=kdst, cf=cf, sf_=sf_, toff=toff):
                    st.wait(v)
                    _li = v.tensor_mul(ktmp[:], kpre[:, par, :], cf[:, toff:toff + 256])
                    _li = v.tensor_mul(ktmp2[:], prt[:], sf_[:, toff:toff + 256])
                    _li = v.tensor_add(kdst[:, c, toff:toff + 256], ktmp[:], ktmp2[:])
                    sv.inc(_li)
                    prb.release(sv)
                    kpb.release(sv)

            for m in range(2):
                par = pcnt % 2
                pcnt += 1
                pbt, pb = (pp0, bpp[0]) if par == 0 else (pp1, bpp[1])
                tc = (toff + 128 * m) // 128

                @sec("tensor")
                def _(t, ti=ti, m=m, pbt=pbt, pb=pb):
                    pb.acquire(t)
                    for k in range(DC):
                        _li = t.matmul(pbt[:], xstr[:, ti % 2, k, 128 * m:128 * (m + 1)],
                                 wv_s[:, k, :],
                                 start=(k == 0), stop=(k == DC - 1))
                    st.inc(_li)

                @sec("vector")
                def _(v, pbt=pbt, pb=pb, tc=tc, own=own):
                    st.wait(v)
                    dst = Vd if own else Vt
                    sv.inc(v.tensor_copy(
                        dst[:, tc, :, 0:64],
                        pbt[:].rearrange("p (h n) -> p h n", h=NKV)))
                    pb.release(sv)

            @sec("tensor")
            def _(t):
                sv.wait(t)
                xring.release(t, st)

        @sec("vector")
        def _(v):
            sv.wait(v)
            _li = v.memset(Vt[:, :, :, 64:65], 1.0)
            _li = v.memset(Vd[:, :, :, 64:65], 1.0)
            sv.inc(_li)

        kv_done_sv = sv.n

        @sec("gpsimd")
        def _(g):
            g.wait_ge(sv.h, kv_done_sv)
            for e8 in range(8):
                slot = wqring.produce(g)
                wqring.mark(g.dma_start(
                    out=wq_s[:, slot, :, :],
                    in_=wq_d.rearrange("(c p) n -> p c n", p=128)[:, :, 256 * e8:256 * (e8 + 1)]))
                for ot in range(4):
                    slot = xring.produce(g)
                    xring.mark(g.dma_start(out=xstr[:, slot, :, :],
                                           in_=xsrc(16 + ot)))

        for e8 in range(8):
            for ot in range(4):
                tidx = NT_KV + 4 * e8 + ot
                toff = 256 * ot
                for q2 in range(2):
                    qc = 2 * e8 + q2
                    par = pcnt % 2
                    pcnt += 1
                    pbt, pb = (pp0, bpp[0]) if par == 0 else (pp1, bpp[1])
                    prt, prb = (pr0, bpr[0]) if par == 0 else (pr1, bpr[1])
                    kpb = bkp[par]

                    @sec("tensor")
                    def _(t, e8=e8, q2=q2, tidx=tidx, pbt=pbt, pb=pb):
                        xring.wait_ready(t, tidx)
                        wqring.wait_ready(t, e8)
                        pb.acquire(t)
                        for k in range(DC):
                            _li = t.matmul(pbt[:, 0:256],
                                     wq_s[:, e8 % 2, k, 128 * q2:128 * (q2 + 1)],
                                     xstr[:, tidx % 2, k, :],
                                     start=(k == 0), stop=(k == DC - 1))
                        st.inc(_li)

                    @sec("scalar")
                    def _(sc, pbt=pbt, pb=pb, kpb=kpb, par=par):
                        st.wait(sc)
                        kpb.acquire(sc)
                        _li = sc.copy(kpre[:, par, :], pbt[:, 0:256])
                        ss.inc(_li)
                        pb.release(ss)

                    @sec("tensor")
                    def _(t, prt=prt, prb=prb, par=par):
                        ss.wait(t)
                        prb.acquire(t)
                        _li = t.matmul(prt[:], perm_b[:], kpre[:, par, :],
                                 start=True, stop=True)
                        st.inc(_li)

                    @sec("vector")
                    def _(v, qc=qc, toff=toff, prt=prt, prb=prb, kpb=kpb, par=par):
                        st.wait(v)
                        _li = v.tensor_mul(ktmp[:], kpre[:, par, :], co_b[:, toff:toff + 256])
                        _li = v.tensor_mul(ktmp2[:], prt[:], so_b[:, toff:toff + 256])
                        _li = v.tensor_add(Qf[:, qc, toff:toff + 256], ktmp[:], ktmp2[:])
                        sv.inc(_li)
                        prb.release(sv)
                        kpb.release(sv)

                @sec("tensor")
                def _(t):
                    sv.wait(t)
                    xring.release(t, st)

            @sec("tensor")
            def _(t):
                wqring.release(t, st)

        ncol = 0

        bsq = Buf()

        def normmax_tiles(tiles):
            nonlocal ncol
            first = ncol
            for ap1024 in tiles:
                @sec("scalar")
                def _(sc, ap1024=ap1024):
                    sv.wait(sc)
                    bsq.acquire(sc)
                    _li = sc.activation(sqt[:], ap1024, AF.Square)
                    ss.inc(_li)

                for j in range(2):
                    @sec("tensor")
                    def _(t, j=j):
                        ss.wait(t)
                        bnm.acquire(t)
                        _li = t.matmul(pnm[0:2, 0:512], e2_b[:],
                                 sqt[:, 512 * j:512 * (j + 1)],
                                 start=True, stop=True)
                        st.inc(_li)
                        if j == 1:
                            bsq.release(st)

                    @sec("vector")
                    def _(v, ncol=ncol):
                        st.wait(v)
                        _li = v.tensor_reduce(stg[0:2, 2 + ncol:3 + ncol], pnm[0:2, 0:512],
                                        AX.X, ALU.max)
                        sv.inc(_li)
                        bnm.release(sv)
                    ncol += 1
            return first

        @sec("vector")
        def _(v):
            sv.wait(v)
            _li = v.memset(stg[:], 0.0)
            sv.inc(_li)

        kfirst = normmax_tiles([Kf[:, c, 1024 * n:1024 * (n + 1)]
                                for c in range(4) for n in range(4)])
        qfirst = normmax_tiles([Qf[:, qc, :] for qc in range(DC)])

        for which, first, last, out_c in ((0, kfirst, qfirst, 0), (1, qfirst, ncol, 1)):
            @sec("vector")
            def _(v, first=first, last=last):
                sv.wait(v)
                vsafe(v, v.tensor_reduce(stg[0:2, 0:1], stg[0:2, 2 + first:2 + last],
                                AX.X, ALU.max))

            @sec("tensor")
            def _(t):
                sv.wait(t)
                btf.acquire(t)
                _li = t.transpose(ptf[:], stg[:], iden_f[:])
                st.inc(_li)

            @sec("vector")
            def _(v, out_c=out_c):
                st.wait(v)
                vsafe(v, v.tensor_reduce(scl[0:1, out_c:out_c + 1], ptf[0:1, 0:2],
                                AX.X, ALU.max))
                btf.release(sv)

        @sec("vector")
        def _(v):
            sv.wait(v)
            vsafe(v, v.tensor_mul(scl[0:1, 2:3], scl[0:1, 0:1], scl[0:1, 1:2]))

        @sec("scalar")
        def _(sc):
            sv.wait(sc)
            _li = sc.activation(scl[0:1, 3:4], scl[0:1, 2:3], AF.Sqrt)
            ss.inc(_li)

        @sec("vector")
        def _(v):
            ss.wait(v)
            vsafe(v, v.tensor_scalar_mul(scl[0:1, 4:5], scl[0:1, 3:4], -0.126))

        @sec("tensor")
        def _(t):
            sv.wait(t)
            bnm.acquire(t)
            _li = t.matmul(pnm[:, 0:1], ones_f[0:1, :], scl[0:1, 4:5], start=True, stop=True)
            st.inc(_li)

        @sec("vector")
        def _(v):
            st.wait(v)
            _li = v.tensor_copy(colA[:, 0:1], pnm[:, 0:1])
            sv.inc(_li)
            bnm.release(sv)
            v.wait_ge(sv.h, sv.n)
            sv.inc(v.tensor_scalar(biastbl[:], pat_s[:], colA[:, 0:1], None, ALU.add))

        if DEBUG:
            @sec("gpsimd")
            def _(g):
                g.wait_ge(sv.h, sv.n)
                qg.inc(g.dma_start(out=dbg_q.rearrange("(c p) t -> p c t", p=128),
                                   in_=Qf[:]), 16)
                qg.inc(g.dma_start(out=dbg_k.rearrange("(c p) t -> p c t", p=128),
                                   in_=Kf[:]), 16)
                qg.inc(g.dma_start(out=dbg_v[:],
                                   in_=Vt[:].rearrange("p a h n -> p (a h n)")), 16)
                qg.inc(g.dma_start(out=dbg_misc[:, 0:NGT], in_=biastbl[:]), 16)
                qg.inc(g.dma_start(out=dbg_misc[:, 56:64],
                                   in_=stg[:, 0:8]), 16)
                g.wait_ge(qg.h, qg.n)

            @sec("sync")
            def _(s):
                ss.wait(s)
                sv.wait(s)
                qs.inc(s.dma_start(out=dbg_misc[0:1, 40:45], in_=scl[0:1, 0:5]), 16)

        barrier()

    stAtt = ExitStack()
    att = sb(stAtt, "att", (128, DC, T), side="right")

    with ExitStack() as stB:
        dmask_b = sb(stB, "dmask_b", (128, 2048))
        ebuf = sb(stB, "ebuf", (128, 2, 1024))
        avsb = sb(stB, "avsb", (64, 512))
        recb = sb(stB, "recb", (1, 512))
        recf = sb(stB, "recf", (1, 512), F32)
        w1str = sb(stB, "w1str", (128, 2, 2048), F32)
        psc0 = pt(stB, "psc0", (128, 1024))
        psc1 = pt(stB, "psc1", (128, 1024))
        pav = pt(stB, "pav", (65, 512))
        pbc = pt(stB, "pbc", (128, 512))

        w1ring = Ring([sem("w1_r0"), sem("w1_r1")], sem("rf3"), 2)
        bsc = [Buf(sem("bsc0")), Buf(sem("bsc1"))]
        beb = [Buf(sem("beb0")), Buf(sem("beb1"))]
        bav = Buf(sem("bav"))
        bbc = Buf(sem("bbc"))

        @sec("gpsimd")
        def _(g):
            qg.inc(g.dma_start(out=dmask_b[:], in_=dmask_in), 16)
        dm_mark = qg.n

        NW1 = 192

        def w1_src(i):
            if i < 64:
                return wgT.rearrange("(c p) n -> p c n", p=128)[:, i // 4, 2048 * (i % 4):2048 * (i % 4 + 1)]
            if i < 128:
                j = i - 64
                return wuT.rearrange("(c p) n -> p c n", p=128)[:, j // 4, 2048 * (j % 4):2048 * (j % 4 + 1)]
            j = i - 128
            return wdT.rearrange("(c p) h -> p c h", p=128)[:, j:j + 1, :] \
                .rearrange("p c n -> p (c n)")

        @sec("sync")
        def _(s):
            for i in range(NW1):
                slot = w1ring.produce(s)
                w1ring.mark(s.dma_start(out=w1str[:, slot, :], in_=w1_src(i)))

        @sec("vector")
        def _(v):
            for i in range(NW1):
                w1ring.wait_ready(v, i)
                _li = v.tensor_reduce(wabs[:, i:i + 1], w1str[:, i % 2, :], AX.X,
                                ALU.add, apply_absolute_value=True)
                sv.inc(_li)
                w1ring.release(v, sv)

        b0heads = [h for h in range(NH) if (h // 4) % 2 == 0]
        b64heads = [h for h in range(NH) if (h // 4) % 2 == 1]
        QCH = {}
        QB = {}
        for i in range(16):
            QCH[b0heads[i]] = i
            QB[b0heads[i]] = 0
            QCH[b64heads[i]] = i
            QB[b64heads[i]] = 64
        gidx = 0
        for h in range(NH):
            kvh = h // 4
            kc = kvh // 2
            kb0 = (kvh % 2) * 64
            qc2 = QCH[h]
            qb0 = QB[h]
            for s_ in range(2):
                NG = NG0 if s_ == 0 else NG1
                qoff = 512 * s_
                for gg in range(NG):
                    par = gidx % 2
                    gidx += 1
                    psct, bscb = (psc0, bsc[0]) if par == 0 else (psc1, bsc[1])
                    bebb = beb[par]
                    diag = gg >= NG - 2
                    dg = gg - (NG - 2)
                    biascol = (NG0 if s_ == 1 else 0) + gg

                    @sec("tensor")
                    def _(t, s_=s_, gg=gg, psct=psct, bscb=bscb, diag=diag,
                          dg=dg, kc=kc, kb0=kb0, qc2=qc2, qb0=qb0, qoff=qoff,
                          gidx=gidx):
                        if gidx == 1:
                            t.wait_ge(qg.h, dm_mark)
                        bscb.acquire(t)
                        for j in range(2):
                            if diag:
                                ksl = Kd[kb0:kb0 + 64, kc,
                                         qoff + 256 * dg + 128 * j:qoff + 256 * dg + 128 * (j + 1)]
                            else:
                                ksl = Kf[kb0:kb0 + 64, kc,
                                         256 * gg + 128 * j:256 * gg + 128 * (j + 1)]
                            _li = t.matmul(psct[:, 512 * j:512 * (j + 1)], ksl,
                                     Qf[qb0:qb0 + 64, qc2, qoff:qoff + 512],
                                     start=True, stop=True)
                        st.inc(_li)

                    @sec("scalar")
                    def _(sc, par=par, psct=psct, bscb=bscb, bebb=bebb,
                          biascol=biascol):
                        st.wait(sc)
                        bebb.acquire(sc)
                        _li = sc.activation(ebuf[:, par, :], psct[:], AF.Exp,
                                      bias=biastbl[:, biascol:biascol + 1],
                                      scale=0.125)
                        ss.inc(_li)
                        bscb.release(ss)

                    if diag:
                        @sec("vector")
                        def _(v, par=par, dg=dg):
                            ss.wait(v)
                            sv.inc(v.tensor_mul(ebuf[:, par, :], ebuf[:, par, :],
                                                dmask_b[:, 1024 * dg:1024 * (dg + 1)]))

                    @sec("tensor")
                    def _(t, s_=s_, gg=gg, par=par, bebb=bebb, diag=diag,
                          dg=dg, kvh=kvh, NG=NG):
                        if diag:
                            sv.wait(t)
                        else:
                            ss.wait(t)
                        if gg == 0:
                            bav.acquire(t)
                        for j in range(2):
                            if diag:
                                vtile = Vd[:, 4 * s_ + 2 * dg + j, kvh, :]
                            else:
                                vtile = Vt[:, 2 * gg + j, kvh, :]
                            _li = t.matmul(pav[:], vtile,
                                     ebuf[:, par, 512 * j:512 * (j + 1)],
                                     start=(gg == 0 and j == 0),
                                     stop=(gg == NG - 1 and j == 1))
                        st.inc(_li)
                        bebb.release(st)

                @sec("vector")
                def _(v):
                    st.wait(v)
                    _li = v.reciprocal(recf[:], pav[64:65, :])
                    sv.inc(v.tensor_copy(recb[:], recf[:]))

                @sec("scalar")
                def _(sc):
                    st.wait(sc)
                    sv.wait(sc)
                    _li = sc.copy(avsb[:], pav[0:64, :])
                    ss.inc(_li)

                @sec("tensor")
                def _(t):
                    sv.wait(t)
                    ss.wait(t)
                    bav.release(sv, ss)
                    bbc.acquire(t)
                    _li = t.matmul(pbc[0:64, :], ones_b[0:1, 0:64], recb[:],
                             start=True, stop=True)
                    st.inc(_li)

                @sec("vector")
                def _(v, qc2=qc2, qb0=qb0, qoff=qoff):
                    st.wait(v)
                    sv.inc(v.tensor_mul(
                        att[qb0:qb0 + 64, qc2, qoff:qoff + 512],
                        avsb[:], pbc[0:64, :]))
                    bbc.release(sv)

        barrier()

        if DEBUG:
            @sec("gpsimd")
            def _(g):
                qg.inc(g.dma_start(out=dbg_att.rearrange("(c p) t -> p c t", p=128),
                                   in_=att[:]), 16)
                g.wait_ge(qg.h, qg.n)

    stKV.close()
    stRows = ExitStack()
    rows = sb(stRows, "rows", (128, 8 * 512), F32)

    with ExitStack() as stC:
        wo_s = sb(stC, "wo_s", (128, 2, DC, 256))
        xo_f = sb(stC, "xo_f", (128, DC, T), F32)
        ctile = sb(stC, "ctile", (128, 2, 1024), F32)
        ctil2 = sb(stC, "ctil2", (128, 1024), F32)
        cci8 = sb(stC, "cci8", (128, 2, 1024), I8)
        ccb0 = sb(stC, "ccb0", (128, 2, 1024))
        cbf = sb(stC, "cbf", (128, 1024))
        cbf2 = sb(stC, "cbf2", (128, 1024))
        mxf = sb(stC, "mxf", (128, 1024), F32)
        qmx = sb(stC, "qmx", (128, 16), F32)
        trow = sb(stC, "trow", (128, 512), F32)
        sxb = sb(stC, "sxb", (128, T), F32)
        bcst = sb(stC, "bcst", (128, 2, 1024), F32)
        pw0 = pt(stC, "pw0", (128, 1024))
        pw1 = pt(stC, "pw1", (128, 1024))
        prow1 = pt(stC, "prow1", (128, 512))
        prow2 = pt(stC, "prow2", (128, 512))
        pbc0 = pt(stC, "pbc0", (128, 512))
        ptf2 = pt(stC, "ptf2", (128, 512))

        woring = Ring([sem("wo_r0"), sem("wo_r1")], sem("rf4"), 2)
        bcb = Buf()
        bpw = [Buf(sem("bpw0")), Buf(sem("bpw1"))]
        brow = Buf(sem("brow"))
        bbc0 = Buf(sem("bbc0"))
        btf2 = Buf(sem("btf2"))

        @sec("gpsimd")
        def _(g):
            for e8 in range(8):
                slot = woring.produce(g)
                woring.mark(g.dma_start(
                    out=wo_s[:, slot, :, :],
                    in_=wo_d.rearrange("(c p) n -> p c n", p=128)[:, :, 256 * e8:256 * (e8 + 1)]))

        @sec("sync")
        def _(s):
            qs.inc(s.dma_start(out=xo_f[:],
                               in_=x_own.rearrange("(c p) t -> p c t", p=128)), 16)
        xo_mark = qs.n

        for dc in range(DC):
            par = dc % 2
            pwt, pwb = (pw0, bpw[0]) if par == 0 else (pw1, bpw[1])
            e8 = dc // 2
            q2 = dc % 2

            @sec("tensor")
            def _(t, dc=dc, pwt=pwt, pwb=pwb, e8=e8, q2=q2):
                woring.wait_ready(t, e8)
                pwb.acquire(t)
                for k in range(DC):
                    for th in range(2):
                        _li = t.matmul(pwt[:, 512 * th:512 * (th + 1)],
                                 wo_s[:, e8 % 2, k, 128 * q2:128 * (q2 + 1)],
                                 att[:, k, 512 * th:512 * (th + 1)],
                                 start=(k == 0), stop=(k == DC - 1))
                st.inc(_li)
                if q2 == 1:
                    woring.release(t, st)

            @sec("vector")
            def _(v, dc=dc, pwt=pwt, pwb=pwb):
                st.wait(v)
                if dc == 0:
                    v.wait_ge(qs.h, xo_mark)
                _li = v.tensor_add(xo_f[:, dc, :], pwt[:], xo_f[:, dc, :])
                sv.inc(_li)
                pwb.release(sv)

            @sec("scalar")
            def _(sc, dc=dc):
                sv.wait(sc)
                bcb.acquire(sc)
                _li = sc.copy(cbf[:], xo_f[:, dc, :])
                _li = sc.activation(cbf2[:], xo_f[:, dc, :], AF.Square)
                ss.inc(_li)

            @sec("tensor")
            def _(t, dc=dc):
                ss.wait(t)
                if dc == 0:
                    brow.acquire(t)
                for th in range(2):
                    _li = t.matmul(prow1[32 * th:32 * th + 1, 0:512], ones_b[:, 0:1],
                             cbf[:, 512 * th:512 * (th + 1)],
                             start=(dc == 0), stop=(dc == DC - 1))
                    _li = t.matmul(prow2[32 * th:32 * th + 1, 0:512], ones_b[:, 0:1],
                             cbf2[:, 512 * th:512 * (th + 1)],
                             start=(dc == 0), stop=(dc == DC - 1))
                st.inc(_li)
                bcb.release(st)

        @sec("vector")
        def _(v):
            st.wait(v)
            _li = v.tensor_copy(row(R_S1, 0), prow1[0:1, 0:512])
            _li = v.tensor_copy(row(R_S1, 1), prow1[32:33, 0:512])
            _li = v.tensor_copy(row(R_S2, 0), prow2[0:1, 0:512])
            _li = v.tensor_copy(row(R_S2, 1), prow2[32:33, 0:512])
            sv.inc(_li)
            brow.release(sv)

        def ln_rows_math():
            for half in range(2):
                @sec("vector")
                def _(v, half=half):
                    v.wait_ge(qs.h, qs.n)
                    _li = v.tensor_scalar_mul(row(R_MEAN, half), row(R_S1, half), 1.0 / D)
                    _li = v.tensor_mul(row(R_A, half), row(R_MEAN, half), row(R_MEAN, half))
                    _li = v.scalar_tensor_tensor(row(R_B, half), row(R_S2, half), 1.0 / D,
                                           row(R_A, half), ALU.mult, ALU.subtract)
                    _li = v.tensor_scalar_add(row(R_B, half), row(R_B, half), LN_EPS)
                    _li = v.reciprocal(row(R_C, half), row(R_B, half))
                    sv.inc(_li)

                @sec("scalar")
                def _(sc, half=half):
                    sv.wait(sc)
                    _li = sc.activation(row(R_RSTD, half), row(R_C, half), AF.Sqrt)
                    ss.inc(_li)

                @sec("vector")
                def _(v, half=half):
                    ss.wait(v)
                    _li = v.tensor_mul(row(R_A, half), row(R_RSTD, half), row(R_RSTD, half))
                    _li = v.tensor_mul(row(R_A, half), row(R_A, half), row(R_B, half))
                    _li = v.tensor_scalar(row(R_A, half), row(R_A, half), -0.5, 1.5,
                                    ALU.mult, ALU.add)
                    _li = v.tensor_mul(row(R_RSTD, half), row(R_RSTD, half), row(R_A, half))
                    sv.inc(_li)

        ln_rows_math()
        broadcast_row(R_MEAN, bcst[:, 0, :], pbc0, bbc0)
        broadcast_row(R_RSTD, bcst[:, 1, :], pbc0, bbc0)

        for dc in range(DC):
            @sec("vector")
            def _(v, dc=dc):
                sv.wait(v)
                _li = v.scalar_tensor_tensor(ctile[:, 0, :], xo_f[:, dc, :], 0.0,
                                       bcst[:, 0, :], ALU.add, ALU.subtract)
                sv.inc(v.tensor_mul(xo_f[:, dc, :], ctile[:, 0, :], bcst[:, 1, :]))

            @sec("scalar")
            def _(sc, dc=dc):
                sv.wait(sc)
                _li = sc.activation(ctil2[:], xo_f[:, dc, :], AF.Abs)
                ss.inc(_li)

            @sec("vector")
            def _(v, dc=dc):
                ss.wait(v)
                if dc == 0:
                    sv.inc(v.tensor_copy(mxf[:], ctil2[:]))
                else:
                    sv.inc(v.tensor_max(mxf[:], mxf[:], ctil2[:]))

        xln1_store_mark = [0]

        @sec("scalar")
        def _(sc):
            sv.wait(sc)
            qa.inc(sc.dma_start(out=xln1_scr.rearrange("(c p) t -> p c t", p=128),
                                in_=xo_f[:]), 16)
        xln1_store_mark[0] = qa.n

        for tb in range(8):
            @sec("tensor")
            def _(t, tb=tb):
                sv.wait(t)
                if tb % 4 == 0:
                    btf2.acquire(t)
                _li = t.transpose(ptf2[:, 128 * (tb % 4):128 * (tb % 4 + 1)],
                            mxf[:, 128 * tb:128 * (tb + 1)], iden_f[:])
                st.inc(_li)

            @sec("vector")
            def _(v, tb=tb):
                st.wait(v)
                _li = v.tensor_reduce(qmx[:, tb:tb + 1],
                                ptf2[:, 128 * (tb % 4):128 * (tb % 4 + 1)],
                                AX.X, ALU.max)
                sv.inc(_li)
                if tb % 4 == 3:
                    btf2.release(sv)

        @sec("vector")
        def _(v):
            sv.wait(v)
            vsafe(v, v.tensor_scalar_max(qmx[:, 8:16], qmx[:, 0:8], EPS_Q))
            _li = v.memset(stg[:], 0.0)
            vsafe(v, _li)
            sv.inc(v.tensor_copy(stg[:, 0:8], qmx[:, 8:16]))

        @sec("tensor")
        def _(t):
            sv.wait(t)
            btf2.acquire(t)
            _li = t.transpose(ptf2[:, 0:128], stg[:], iden_f[:])
            st.inc(_li)

        @sec("vector")
        def _(v):
            st.wait(v)
            v.wait_ge(qs.h, qs.n)
            sv.inc(v.tensor_copy(trow[0:8, 0:128], ptf2[0:8, 0:128]))
            btf2.release(sv)

        @sec("tensor")
        def _(t):
            sv.wait(t)
            btf2.acquire(t)
            for half in range(2):
                for cc in range(4):
                    _li = t.matmul(ptf2[32 * half:32 * half + 1, 128 * cc:128 * (cc + 1)],
                                   iden_f[0:8, 4 * half + cc:4 * half + cc + 1],
                                   trow[0:8, 0:128], start=True, stop=True)
            st.inc(_li)

        @sec("vector")
        def _(v):
            st.wait(v)
            _li = v.tensor_copy(row(R_C, 0), ptf2[0:1, 0:512])
            _li = v.tensor_copy(row(R_C, 1), ptf2[32:33, 0:512])
            sv.inc(_li)
            btf2.release(sv)

        for half in range(2):
            @sec("vector")
            def _(v, half=half):
                sv.wait(v)
                _li = v.reciprocal(row(R_A, half), row(R_C, half))
                sv.inc(v.tensor_scalar_mul(row(R_A, half), row(R_A, half), 127.0))

        broadcast_row(R_A, sxb[:], pbc0, bbc0)

        ccq_mark = {}
        for dc in range(DC):
            par = dc % 2

            @sec("vector")
            def _(v, dc=dc, par=par):
                sv.wait(v)
                if dc >= 2:
                    v.wait_ge(qg.h, ccq_mark[dc - 2])
                _li = v.tensor_mul(ctile[:, par, :], xo_f[:, dc, :], sxb[:])
                sv.inc(_li)

            @sec("gpsimd")
            def _(g, dc=dc, par=par):
                g.wait_ge(sv.h, sv.n)
                g.wait_ge(qg.h, qg.n)
                qg.inc(g.dma_start(out=cci8[:, par, :], in_=ctile[:, par, :]), 16)
                g.wait_ge(qg.h, qg.n)
                qg.inc(g.dma_start(out=ccb0[:, par, :], in_=cci8[:, par, :]), 16)
                ccq_mark[dc] = qg.n

            @sec("vector")
            def _(v, dc=dc, par=par):
                v.wait_ge(qg.h, qg.n)
                _li = v.tensor_copy(ctil2[:], ccb0[:, par, :])
                _li = v.tensor_sub(ctil2[:], ctile[:, par, :], ctil2[:])
                _li = v.tensor_single_scalar(cbf[:], ctil2[:], 0.5, ALU.is_ge)
                _li = v.tensor_single_scalar(cbf2[:], ctil2[:], -0.5, ALU.is_lt)
                _li = v.tensor_add(cbf[:], ccb0[:, par, :], cbf[:])
                sv.inc(v.tensor_sub(n_x[:, dc, :], cbf[:], cbf2[:]))

        @sec("vector")
        def _(v):
            sv.wait(v)
            st.wait(v)
            v.memset(stg[:], 0.0)
            for w in range(3):
                vsafe(v, v.tensor_reduce(wabs[:, 192 + w:193 + w],
                                         wabs[:, 64 * w:64 * (w + 1)],
                                         AX.X, ALU.add))
            sv.inc(v.tensor_copy(stg[:, 0:3], wabs[:, 192:195]))

        @sec("tensor")
        def _(t):
            sv.wait(t)
            btf2.acquire(t)
            _li = t.transpose(ptf2[:, 0:128], stg[:], iden_f[:])
            st.inc(_li)

        @sec("vector")
        def _(v):
            st.wait(v)
            sv.inc(v.tensor_copy(trow[0:3, 0:128], ptf2[0:3, 0:128]))
            btf2.release(sv)

        @sec("tensor")
        def _(t):
            sv.wait(t)
            btf2.acquire(t)
            for w in range(3):
                _li = t.matmul(ptf2[0:1, 128 * w:128 * (w + 1)],
                               iden_f[0:8, w:w + 1], trow[0:8, 0:128],
                               start=True, stop=True)
            st.inc(_li)

        @sec("vector")
        def _(v):
            st.wait(v)
            _li = v.tensor_copy(rows[0:1, 512 * R_DD:512 * R_DD + 384],
                                ptf2[0:1, 0:384])
            sv.inc(_li)
            btf2.release(sv)

        @sec("vector")
        def _(v):
            sv.wait(v)
            base = 512 * R_DD
            for w in range(3):
                _li = v.tensor_reduce(scl[0:1, 8 + w:9 + w],
                                rows[0:1, base + 128 * w:base + 128 * (w + 1)],
                                AX.X, ALU.add)
            vsafe(v, _li)
            vsafe(v, v.tensor_scalar(scl[0:1, 11:14], scl[0:1, 8:11],
                                     1.0 / (II * D), EPS_Q, ALU.mult, ALU.max))
            _li = v.tensor_scalar_mul(scl[0:1, 14:17], scl[0:1, 11:14], 0.5)
            vsafe(v, v.tensor_scalar_mul(scl[0:1, 17:20], scl[0:1, 11:14], -0.5))
            vsafe(v, v.tensor_mul(scl[0:1, 20:21], scl[0:1, 11:12], scl[0:1, 11:12]))
            vsafe(v, v.tensor_mul(scl[0:1, 21:22], scl[0:1, 20:21], scl[0:1, 12:13]))

        for i, col in enumerate([14, 17, 15, 18, 16, 19]):
            @sec("tensor")
            def _(t, col=col):
                sv.wait(t)
                bbc0.acquire(t)
                _li = t.matmul(pbc0[:, 0:1], ones_f[0:1, :], scl[0:1, col:col + 1],
                         start=True, stop=True)
                st.inc(_li)

            @sec("vector")
            def _(v, i=i):
                st.wait(v)
                sv.inc(v.tensor_copy(colA[:, 1 + i:2 + i], pbc0[:, 0:1]))
                bbc0.release(sv)

        for half in range(2):
            @sec("vector")
            def _(v, half=half):
                sv.wait(v)
                _li = v.tensor_scalar_mul(row(R_B, half), row(R_C, half), 1.0 / 127.0)
                _li = v.tensor_mul(row(R_DD, half), row(R_B, half), row(R_B, half))
                _li = v.tensor_mul(row(R_DD, half), row(R_DD, half), row(R_B, half))
                sv.inc(v.tensor_scalar(row(R_DD, half), row(R_DD, half),
                                       scl[0:1, 21:22], None, ALU.mult))

        if DEBUG:
            @sec("gpsimd")
            def _(g):
                g.wait_ge(sv.h, sv.n)
                qg.inc(g.dma_start(out=dbg_nx.rearrange("(c p) t -> p c t", p=128),
                                   in_=n_x[:]), 16)
                qg.inc(g.dma_start(out=dbg_misc[:, 48:56], in_=qmx[:, 0:8]), 16)
                qg.inc(g.dma_start(out=dbg_misc[:, 32:39], in_=colA[:, 0:7]), 16)
                g.wait_ge(qg.h, qg.n)

        barrier()

    stAtt.close()

    with ExitStack() as stD:
        wstr = sb(stD, "wstr", (128, 2, DC, 256), F32)
        twg = sb(stD, "twg", (128, DC, 256))
        twu = sb(stD, "twu", (128, DC, 256))
        qscr = sb(stD, "qscr", (128, 4096))
        rg_f = sb(stD, "rg_f", (128, 1024), F32)
        rg2 = sb(stD, "rg2", (128, 1024), F32)
        rawstg = sb(stD, "rawstg", (128, 8, 1024), F32)
        raw2b = sb(stD, "raw2b", (128, 1024))
        absg = sb(stD, "absg", (128, 1024), F32)
        mxrun = sb(stD, "mxrun", (128, 1024), F32)
        trow2 = sb(stD, "trow2", (128, 512), F32)
        pg0 = pt(stD, "pg0", (128, 1024))
        pg1 = pt(stD, "pg1", (128, 1024))
        pu = pt(stD, "pu", (128, 1024))
        pm2 = pt(stD, "pm2", (128, 512))

        wdring = Ring([sem("wd_r0"), sem("wd_r1")], sem("rf5"), 2)
        btg = Buf()
        btu = Buf()
        brb = Buf()
        brg = Buf()
        bpg = [Buf(sem("bpg0")), Buf(sem("bpg1"))]
        bpu = Buf(sem("bpu"))
        bm2 = Buf(sem("bm2"))
        brs = Buf(sem("brs"), depth=2)

        @sec("sync")
        def _(s):
            for ib in range(32):
                for wsrc in (wgT, wuT):
                    slot = wdring.produce(s)
                    wdring.mark(s.dma_start(
                        out=wstr[:, slot, :, :],
                        in_=wsrc.rearrange("(c p) n -> p c n", p=128)[:, :, 256 * ib:256 * (ib + 1)]))

        for ib in range(32):
            for wi, tw, thr in ((0, twg, 1), (1, twu, 3)):
                tile_i = 2 * ib + wi

                @sec("vector")
                def _(v, tile_i=tile_i, tw=tw, thr=thr, wi=wi):
                    wdring.wait_ready(v, tile_i)
                    (btg if wi == 0 else btu).acquire(v)
                    src = wstr[:, tile_i % 2, :, :].rearrange("p c n -> p (c n)")
                    twv = tw[:].rearrange("p c n -> p (c n)")
                    _li = v.tensor_scalar(twv, src, colA[:, thr:thr + 1], None, ALU.is_ge)
                    _li = v.tensor_scalar(qscr[:], src, colA[:, thr + 1:thr + 2], None,
                                    ALU.is_le)
                    sv.inc(v.tensor_sub(twv, twv, qscr[:]))
                    wdring.release(v, sv)

            for c2 in range(2):
                ic = 2 * ib + c2
                par = ic % 2
                pgt, pgb = (pg0, bpg[0]) if par == 0 else (pg1, bpg[1])

                @sec("tensor")
                def _(t, c2=c2, pgt=pgt, pgb=pgb):
                    sv.wait(t)
                    pgb.acquire(t)
                    for k in range(DC):
                        for th in range(2):
                            _li = t.matmul(pgt[:, 512 * th:512 * (th + 1)],
                                     twg[:, k, 128 * c2:128 * (c2 + 1)],
                                     n_x[:, k, 512 * th:512 * (th + 1)],
                                     start=(k == 0), stop=(k == DC - 1))
                    st.inc(_li)
                    if c2 == 1:
                        btg.release(st)
                    bpu.acquire(t)
                    for k in range(DC):
                        for th in range(2):
                            _li = t.matmul(pu[:, 512 * th:512 * (th + 1)],
                                     twu[:, k, 128 * c2:128 * (c2 + 1)],
                                     n_x[:, k, 512 * th:512 * (th + 1)],
                                     start=(k == 0), stop=(k == DC - 1))
                    st.inc(_li)
                    if c2 == 1:
                        btu.release(st)

                @sec("scalar")
                def _(sc, pgt=pgt):
                    st.wait(sc, st.n - 1)
                    brg.acquire(sc)
                    _li = sc.activation(rg_f[:], pgt[:], AF.Relu)
                    _li = sc.activation(rg2[:], rg_f[:], AF.Square)
                    ss.inc(_li)

                @sec("vector")
                def _(v, ic=ic, pgb=pgb):
                    ss.wait(v)
                    pgb.release(ss)
                    st.wait(v)
                    if ic % 4 == 0:
                        brs.acquire(v)
                    _li = v.tensor_mul(rawstg[:, ic % 8, :], rg2[:], pu[:])
                    sv.inc(_li)
                    brg.release(sv)
                    bpu.release(sv)

                if DEBUG:
                    @sec("scalar")
                    def _(sc, ic=ic):
                        sv.wait(sc)
                        qa.inc(sc.dma_start(
                            out=dbg_rg.rearrange("(c p) t -> p c t", p=128)[:, ic:ic + 1, :],
                            in_=rg_f[:]), 16)

                @sec("scalar")
                def _(sc, ic=ic):
                    sv.wait(sc)
                    brb.acquire(sc)
                    _li = sc.activation(raw2b[:], rawstg[:, ic % 8, :], AF.Square)
                    _li = sc.activation(absg[:], rawstg[:, ic % 8, :], AF.Abs,
                                  scale=gcol_s[:, ic:ic + 1])
                    ss.inc(_li)

                @sec("tensor")
                def _(t, ic=ic):
                    ss.wait(t)
                    if ic == 0:
                        bm2.acquire(t)
                    if not NO_M2:
                        for th in range(2):
                            _li = t.matmul(pm2[32 * th:32 * th + 1, 0:512], ones_b[:, 0:1],
                                     raw2b[:, 512 * th:512 * (th + 1)],
                                     start=(ic == 0), stop=(ic == ICN - 1))
                        st.inc(_li)
                    else:
                        st.inc(t.nop())
                    brb.release(st)

                @sec("vector")
                def _(v, ic=ic):
                    ss.wait(v)
                    if ic == 0:
                        sv.inc(v.tensor_copy(mxrun[:], absg[:]))
                    else:
                        sv.inc(v.tensor_max(mxrun[:], mxrun[:], absg[:]))

                if ic % 4 == 3:
                    grp = ic // 4

                    @sec("scalar")
                    def _(sc, ic=ic, grp=grp):
                        sv.wait(sc)
                        c0 = ic - 3
                        qa.inc(sc.dma_start(
                            out=act_scr.rearrange("(c p) t -> p c t", p=128)[:, c0:c0 + 4, :],
                            in_=rawstg[:, (grp % 2) * 4:(grp % 2) * 4 + 4, :]), 16)
                        brs.release(ss, qa)
                if SERIAL_D1:
                    barrier()

        @sec("vector")
        def _(v):
            st.wait(v)
            _li = v.tensor_copy(row(R_S1, 0), pm2[0:1, 0:512])
            _li = v.tensor_copy(row(R_S1, 1), pm2[32:33, 0:512])
            sv.inc(_li)
            bm2.release(sv)

        def qmx2_col(tb):
            return stg[:, 16 + tb:17 + tb]

        for tb in range(8):
            @sec("tensor")
            def _(t, tb=tb):
                sv.wait(t)
                if tb % 4 == 0:
                    bm2.acquire(t)
                _li = t.transpose(pg0[:, 128 * (tb % 4):128 * (tb % 4 + 1)],
                            mxrun[:, 128 * tb:128 * (tb + 1)], iden_f[:])
                st.inc(_li)

            @sec("vector")
            def _(v, tb=tb):
                st.wait(v)
                _li = v.tensor_reduce(qmx2_col(tb), pg0[:, 128 * (tb % 4):128 * (tb % 4 + 1)],
                                AX.X, ALU.max)
                sv.inc(_li)
                if tb % 4 == 3:
                    bm2.release(sv)

        @sec("vector")
        def _(v):
            sv.wait(v)
            _li = v.memset(stg[:, 0:8], 0.0)
            vsafe(v, _li)
            sv.inc(v.tensor_copy(stg[:, 0:8], stg[:, 16:24]))

        @sec("tensor")
        def _(t):
            sv.wait(t)
            bm2.acquire(t)
            _li = t.transpose(pg0[:, 0:128], stg[:], iden_f[:])
            st.inc(_li)

        @sec("vector")
        def _(v):
            st.wait(v)
            v.wait_ge(qs.h, qs.n)
            _li = v.tensor_copy(trow2[0:8, 0:128], pg0[0:8, 0:128])
            sv.inc(_li)
            bm2.release(sv)

        @sec("tensor")
        def _(t):
            sv.wait(t)
            bm2.acquire(t)
            for half in range(2):
                for cc in range(4):
                    _li = t.matmul(pm2[32 * half:32 * half + 1, 128 * cc:128 * (cc + 1)],
                                   iden_f[0:8, 4 * half + cc:4 * half + cc + 1],
                                   trow2[0:8, 0:128], start=True, stop=True)
            st.inc(_li)

        @sec("vector")
        def _(v):
            st.wait(v)
            _li = v.tensor_copy(row(R_S2, 0), pm2[0:1, 0:512])
            _li = v.tensor_copy(row(R_S2, 1), pm2[32:33, 0:512])
            sv.inc(_li)
            bm2.release(sv)

        for half in range(2):
            @sec("vector")
            def _(v, half=half):
                sv.wait(v)
                _li = v.tensor_mul(row(R_A, half), row(R_DD, half), row(R_DD, half))
                _li = v.tensor_scalar_mul(row(R_B, half), row(R_S1, half), 1.0 / II)
                _li = v.tensor_mul(row(R_B, half), row(R_B, half), row(R_A, half))
                _li = v.tensor_scalar_add(row(R_B, half), row(R_B, half), SUB_EPS)
                _li = v.reciprocal(row(R_C, half), row(R_B, half))
                sv.inc(_li)

            @sec("scalar")
            def _(sc, half=half):
                sv.wait(sc)
                _li = sc.activation(row(R_RSTD, half), row(R_C, half), AF.Sqrt)
                ss.inc(_li)

            @sec("vector")
            def _(v, half=half):
                ss.wait(v)
                _li = v.tensor_mul(row(R_A, half), row(R_RSTD, half), row(R_RSTD, half))
                _li = v.tensor_mul(row(R_A, half), row(R_A, half), row(R_B, half))
                _li = v.tensor_scalar(row(R_A, half), row(R_A, half), -0.5, 1.5,
                                ALU.mult, ALU.add)
                _li = v.tensor_mul(row(R_RSTD, half), row(R_RSTD, half), row(R_A, half))
                _li = v.tensor_mul(row(R_MEAN, half), row(R_DD, half), row(R_RSTD, half))
                _li = v.tensor_mul(row(R_B, half), row(R_MEAN, half), row(R_S2, half))
                _li = v.tensor_scalar_max(row(R_B, half), row(R_B, half), EPS_Q)
                _li = v.reciprocal(row(R_C, half), row(R_B, half))
                _li = v.tensor_scalar_mul(row(R_C, half), row(R_C, half), 127.0)
                _li = v.tensor_mul(row(R_A, half), row(R_MEAN, half), row(R_C, half))
                _li = v.tensor_scalar_mul(row(R_B, half), row(R_B, half), 1.0 / 127.0)
                sv.inc(v.tensor_scalar(row(R_B, half), row(R_B, half),
                                       scl[0:1, 13:14], None, ALU.mult))

        barrier()

    stQ.close()

    with ExitStack() as stD2:
        fs2b = sb(stD2, "fs2b", (128, T), F32)
        dsb = sb(stD2, "dsb", (128, T), F32)
        wdstr = sb(stD2, "wdstr", (128, 2, 8, 512), F32)
        twd = sb(stD2, "twd", (128, 8, 512))
        qscr2 = sb(stD2, "qscr2", (128, 4096))
        n_act = sb(stD2, "n_act", (128, ICN, 512))
        actrd = sb(stD2, "actrd", (128, 4, 512), F32)
        xq_t = sb(stD2, "xq_t", (128, 4, 512), F32)
        y2stg = sb(stD2, "y2stg", (128, 4, 512), F32)
        ct1 = sb(stD2, "ct1", (128, 512), F32)
        cta = sb(stD2, "cta", (128, 2, 2048), F32)
        cti = sb(stD2, "cti", (128, 2, 2048), I8)
        ctb = sb(stD2, "ctb", (128, 2, 2048))
        ctr = sb(stD2, "ctr", (128, 2048), F32)
        pd = [pt(stD2, "pd%d" % i, (128, 512)) for i in range(4)]
        px0 = pt(stD2, "px0", (128, 1024))
        px1 = pt(stD2, "px1", (128, 1024))

        bbx = Buf(sem("bbx"))
        by2 = Buf()
        bpd = Buf(sem("bpd"))
        btw = Buf(sem("btw"))
        wd2ring = Ring([sem("wd2_r0"), sem("wd2_r1")], sem("rf6"), 2)
        acring = Ring([sem("ac_r0")], sem("rf7"), 1)
        xqring = Ring([sem("xq_r0")], sem("rf8"), 1)
        castq_mark = {}

        broadcast_row(R_A, fs2b[:], px0, bbx)
        broadcast_row(R_B, dsb[:], px0, bbx)

        for th in range(2):
            for blk in range(16):
                @sec("scalar")
                def _(sc, th=th, blk=blk):
                    acring.produce(sc)
                    acring.mark(sc.dma_start(
                        out=actrd[:],
                        in_=act_scr.rearrange("(c p) t -> p c t", p=128)
                        [:, 4 * blk:4 * (blk + 1), 512 * th:512 * (th + 1)]))

                gblk = 16 * th + blk
                par = gblk % 2

                @sec("vector")
                def _(v, th=th, blk=blk, gblk=gblk, par=par):
                    acring.wait_ready(v, gblk)
                    if gblk >= 2:
                        v.wait_ge(qg.h, castq_mark[gblk - 2])
                    for c4 in range(4):
                        ic = 4 * blk + c4
                        _li = v.scalar_tensor_tensor(
                            cta[:, par, 512 * c4:512 * (c4 + 1)], actrd[:, c4, :],
                            gcol_s[:, ic:ic + 1],
                            fs2b[:, 512 * th:512 * (th + 1)],
                            ALU.mult, ALU.mult)
                    sv.inc(_li)
                    acring.release(v, sv)

                @sec("gpsimd")
                def _(g, par=par, gblk=gblk):
                    g.wait_ge(sv.h, sv.n)
                    g.wait_ge(qg.h, qg.n)
                    qg.inc(g.dma_start(out=cti[:, par, :], in_=cta[:, par, :]), 16)
                    g.wait_ge(qg.h, qg.n)
                    qg.inc(g.dma_start(out=ctb[:, par, :], in_=cti[:, par, :]), 16)
                    castq_mark[gblk] = qg.n

                @sec("vector")
                def _(v, blk=blk, par=par):
                    v.wait_ge(qg.h, qg.n)
                    dst = n_act[:, 4 * blk:4 * (blk + 1), :].rearrange("p c n -> p (c n)")
                    src = cta[:, par, :]
                    cab = qscr2[:, 2048:4096]
                    cbb = qscr2[:, 0:2048]
                    _li = v.tensor_copy(ctr[:], ctb[:, par, :])
                    _li = v.tensor_sub(ctr[:], src, ctr[:])
                    _li = v.tensor_single_scalar(cab, ctr[:], 0.5, ALU.is_ge)
                    _li = v.tensor_single_scalar(cbb, ctr[:], -0.5, ALU.is_lt)
                    _li = v.tensor_add(cab, ctb[:, par, :], cab)
                    _li = v.tensor_sub(dst, cab, cbb)
                    sv.inc(_li)

            for hq in range(4):
                @sec("sync")
                def _(s, th=th, hq=hq):
                    if th == 0 and hq == 0:
                        s.wait_ge(qa.h, xln1_store_mark[0])
                    xqring.produce(s)
                    xqring.mark(s.dma_start(
                        out=xq_t[:],
                        in_=xln1_scr.rearrange("(c p) t -> p c t", p=128)
                        [:, 4 * hq:4 * (hq + 1), 512 * th:512 * (th + 1)]))

                for ib8 in range(8):
                    tile_i = (th * 4 + hq) * 8 + ib8

                    @sec("sync")
                    def _(s, hq=hq, ib8=ib8):
                        slot = wd2ring.produce(s)
                        wd2ring.mark(s.dma_start(
                            out=wdstr[:, slot, :, :],
                            in_=wdT.rearrange("(c p) h -> p c h", p=128)
                            [:, 8 * ib8:8 * (ib8 + 1), 512 * hq:512 * (hq + 1)]))

                    @sec("vector")
                    def _(v, tile_i=tile_i):
                        wd2ring.wait_ready(v, tile_i)
                        btw.acquire(v)
                        src = wdstr[:, tile_i % 2, :, :].rearrange("p c n -> p (c n)")
                        twv = twd[:].rearrange("p c n -> p (c n)")
                        _li = v.tensor_scalar(twv, src, colA[:, 5:6], None, ALU.is_ge)
                        _li = v.tensor_scalar(qscr2[:], src, colA[:, 6:7], None, ALU.is_le)
                        sv.inc(v.tensor_sub(twv, twv, qscr2[:]))
                        wd2ring.release(v, sv)

                    @sec("tensor")
                    def _(t, ib8=ib8):
                        sv.wait(t)
                        if ib8 == 0:
                            bpd.acquire(t)
                        for ic8 in range(8):
                            ic = 8 * ib8 + ic8
                            for hc in range(4):
                                _li = t.matmul(pd[hc][:, 0:512],
                                         twd[:, ic8, 128 * hc:128 * (hc + 1)],
                                         n_act[:, ic, :],
                                         start=(ic == 0), stop=(ic == ICN - 1))
                        st.inc(_li)
                        btw.release(st)

                for hc in range(4):
                    @sec("vector")
                    def _(v, th=th, hq=hq, hc=hc):
                        st.wait(v)
                        xqring.wait_ready(v, th * 4 + hq)
                        if hc == 0:
                            by2.acquire(v)
                        _li = v.tensor_mul(ct1[:], pd[hc][:, 0:512],
                                     dsb[:, 512 * th:512 * (th + 1)])
                        _li = v.tensor_add(y2stg[:, hc, :], ct1[:], xq_t[:, hc, :])
                        sv.inc(_li)
                        if hc == 3:
                            bpd.release(sv)
                            xqring.release(v, sv)

                @sec("scalar")
                def _(sc, th=th, hq=hq):
                    sv.wait(sc)
                    qa.inc(sc.dma_start(
                        out=y2_scr.rearrange("(c p) t -> p c t", p=128)
                        [:, 4 * hq:4 * (hq + 1), 512 * th:512 * (th + 1)],
                        in_=y2stg[:]), 16)
                    by2.release(qa)

        barrier()

    with ExitStack() as stE:
        y2f = sb(stE, "y2f", (128, DC, T), F32)
        cbf1k = sb(stE, "cbf1k", (128, 1024))
        cbf1k2 = sb(stE, "cbf1k2", (128, 1024))
        bcst2 = sb(stE, "bcst2", (128, 2, 1024), F32)
        ytm = sb(stE, "ytm", (128, 2, 2048), F32)
        ctk = sb(stE, "ctk", (128, 1024), F32)
        trow3 = sb(stE, "trow3", (128, 512), F32)
        pe0 = pt(stE, "pe0", (128, 512))
        pe1 = pt(stE, "pe1", (128, 512))
        pe2 = pt(stE, "pe2", (128, 512))
        pe3 = pt(stE, "pe3", (128, 512))

        brw2 = Buf(sem("brw2"))
        bce = Buf()
        bbx2 = Buf(sem("bbx2"))
        bpe = [Buf(sem("bpe0")), Buf(sem("bpe1"))]

        @sec("sync")
        def _(s):
            qa.wait(s)
            qs.inc(s.dma_start(out=y2f[:],
                               in_=y2_scr.rearrange("(c p) t -> p c t", p=128)), 16)
        y2_mark = qs.n

        for dc in range(DC):
            @sec("scalar")
            def _(sc, dc=dc):
                if dc == 0:
                    sc.wait_ge(qs.h, y2_mark)
                bce.acquire(sc)
                _li = sc.copy(cbf1k[:], y2f[:, dc, :])
                _li = sc.activation(cbf1k2[:], y2f[:, dc, :], AF.Square)
                ss.inc(_li)

            @sec("tensor")
            def _(t, dc=dc):
                ss.wait(t)
                if dc == 0:
                    brw2.acquire(t)
                for th in range(2):
                    _li = t.matmul(pe0[32 * th:32 * th + 1, 0:512], ones_b[:, 0:1],
                             cbf1k[:, 512 * th:512 * (th + 1)],
                             start=(dc == 0), stop=(dc == DC - 1))
                    _li = t.matmul(pe1[32 * th:32 * th + 1, 0:512], ones_b[:, 0:1],
                             cbf1k2[:, 512 * th:512 * (th + 1)],
                             start=(dc == 0), stop=(dc == DC - 1))
                st.inc(_li)
                bce.release(st)

        @sec("vector")
        def _(v):
            st.wait(v)
            _li = v.tensor_copy(row(R_S1, 0), pe0[0:1, 0:512])
            _li = v.tensor_copy(row(R_S1, 1), pe0[32:33, 0:512])
            _li = v.tensor_copy(row(R_S2, 0), pe1[0:1, 0:512])
            _li = v.tensor_copy(row(R_S2, 1), pe1[32:33, 0:512])
            sv.inc(_li)
            brw2.release(sv)

        for half in range(2):
            @sec("vector")
            def _(v, half=half):
                sv.wait(v)
                _li = v.tensor_scalar_mul(row(R_MEAN, half), row(R_S1, half), 1.0 / D)
                _li = v.tensor_mul(row(R_A, half), row(R_MEAN, half), row(R_MEAN, half))
                _li = v.scalar_tensor_tensor(row(R_B, half), row(R_S2, half), 1.0 / D,
                                       row(R_A, half), ALU.mult, ALU.subtract)
                _li = v.tensor_scalar_add(row(R_B, half), row(R_B, half), LN_EPS)
                _li = v.reciprocal(row(R_C, half), row(R_B, half))
                sv.inc(_li)

            @sec("scalar")
            def _(sc, half=half):
                sv.wait(sc)
                _li = sc.activation(row(R_RSTD, half), row(R_C, half), AF.Sqrt)
                ss.inc(_li)

            @sec("vector")
            def _(v, half=half):
                ss.wait(v)
                _li = v.tensor_mul(row(R_A, half), row(R_RSTD, half), row(R_RSTD, half))
                _li = v.tensor_mul(row(R_A, half), row(R_A, half), row(R_B, half))
                _li = v.tensor_scalar(row(R_A, half), row(R_A, half), -0.5, 1.5,
                                ALU.mult, ALU.add)
                _li = v.tensor_mul(row(R_RSTD, half), row(R_RSTD, half), row(R_A, half))
                sv.inc(_li)

        broadcast_row(R_MEAN, bcst2[:, 0, :], pe2, bbx2)
        broadcast_row(R_RSTD, bcst2[:, 1, :], pe2, bbx2)

        for dc in range(DC):
            @sec("vector")
            def _(v, dc=dc):
                sv.wait(v)
                _li = v.scalar_tensor_tensor(ctk[:], y2f[:, dc, :], 0.0,
                                       bcst2[:, 0, :], ALU.add, ALU.subtract)
                sv.inc(v.tensor_mul(y2f[:, dc, :], ctk[:], bcst2[:, 1, :]))

        ytm_mark = {}
        for tb in range(8):
            for dc in range(DC):
                par = dc % 2
                pxt = pe2 if par == 0 else pe3
                pxb = bpe[par]

                @sec("tensor")
                def _(t, tb=tb, dc=dc, pxt=pxt, pxb=pxb):
                    sv.wait(t)
                    pxb.acquire(t)
                    _li = t.transpose(pxt[:, 0:128], y2f[:, dc, 128 * tb:128 * (tb + 1)],
                                iden_f[:])
                    st.inc(_li)

                @sec("vector")
                def _(v, tb=tb, dc=dc, pxt=pxt, pxb=pxb):
                    st.wait(v)
                    if dc == 0 and tb >= 2:
                        v.wait_ge(qa.h, ytm_mark[tb - 2])
                    _li = v.tensor_copy(ytm[:, tb % 2, 128 * dc:128 * (dc + 1)],
                                      pxt[:, 0:128])
                    sv.inc(_li)
                    pxb.release(sv)

            @sec("scalar")
            def _(sc, tb=tb):
                sv.wait(sc)
                qa.inc(sc.dma_start(out=y_out[128 * tb:128 * (tb + 1), :],
                                    in_=ytm[:, tb % 2, :]), 16)
            ytm_mark[tb] = qa.n
        @sec("scalar")
        def _(sc):
            qa.wait(sc)

    stRows.close()
    st0.close()
    return nc


_NC_CACHE = None


def _get_nc():
    global _NC_CACHE
    if _NC_CACHE is None:
        _NC_CACHE = build_program()
    return _NC_CACHE


def _host_tables():
    import jax
    import jax.numpy as jnp
    with jax.default_device(jax.devices("cpu")[0]):
        inv_freq = 1.0 / (10000.0 ** (jnp.arange(0, HD, 2, dtype=jnp.float32) / HD))
        ang = jnp.arange(S, dtype=jnp.float32)[:, None] * inv_freq[None, :]
        c_tab = np.asarray(jnp.cos(ang), dtype=np.float32)
        s_tab = np.asarray(jnp.sin(ang), dtype=np.float32)

    def rope_expand(positions):
        c = c_tab[positions]
        s = s_tab[positions]
        ce = np.empty((HD, len(positions)), np.float32)
        se = np.empty((HD, len(positions)), np.float32)
        ce[0::2] = c.T
        ce[1::2] = c.T
        se[0::2] = -s.T
        se[1::2] = s.T
        cf = np.vstack([ce, ce])
        sf = np.vstack([se, se])
        return np.ascontiguousarray(cf), np.ascontiguousarray(sf)

    cf_full, sf_full = rope_expand(np.arange(S))
    perm = np.zeros((128, 128), np.float32)
    for i in range(0, 128, 2):
        perm[i, i + 1] = 1.0
        perm[i + 1, i] = 1.0
    iden = np.eye(128, dtype=np.float32)
    ones = np.ones((128, 128), np.float32)
    e2 = np.zeros((128, 2), np.float32)
    e2[0:64, 0] = 1.0
    e2[64:128, 1] = 1.0
    dm = np.zeros((128, 2048), np.float32)
    q = np.arange(512)
    for kb in range(4):
        k_abs = 128 * kb + np.arange(128)
        dm[:, 512 * kb:512 * (kb + 1)] = (k_abs[:, None] <= q[None, :])
    return cf_full, sf_full, perm, iden, ones, e2, dm, rope_expand


def kernel(L0_attn_x, L0_attn_wq, L0_attn_wk, L0_attn_wv, L0_attn_wo,
           L0_ffn_w_gate, L0_ffn_w_up, L0_ffn_w_down, L0_ffn_subln_weight):
    x = np.ascontiguousarray(np.asarray(L0_attn_x, dtype=np.float32))
    wq = np.ascontiguousarray(np.asarray(L0_attn_wq, dtype=np.float32))
    wk = np.ascontiguousarray(np.asarray(L0_attn_wk, dtype=np.float32))
    wv = np.ascontiguousarray(np.asarray(L0_attn_wv, dtype=np.float32))
    wo = np.ascontiguousarray(np.asarray(L0_attn_wo, dtype=np.float32))
    wg = np.asarray(L0_ffn_w_gate, dtype=np.float32)
    wu = np.asarray(L0_ffn_w_up, dtype=np.float32)
    wd = np.asarray(L0_ffn_w_down, dtype=np.float32)
    g = np.asarray(L0_ffn_subln_weight, dtype=np.float32)

    b0heads = [h for h in range(NH) if (h // 4) % 2 == 0]
    b64heads = [h for h in range(NH) if (h // 4) % 2 == 1]
    perm_a = np.empty(D, np.int64)
    for i in range(16):
        perm_a[128 * i:128 * i + 64] = 64 * b0heads[i] + np.arange(64)
        perm_a[128 * i + 64:128 * (i + 1)] = 64 * b64heads[i] + np.arange(64)
    wq = np.ascontiguousarray(wq[:, perm_a])
    wo = np.ascontiguousarray(wo[perm_a, :])
    xT = [np.ascontiguousarray(x[b].T) for b in range(2)]
    wgT_h = np.ascontiguousarray(wg.T)
    wuT_h = np.ascontiguousarray(wu.T)
    wdT_h = np.ascontiguousarray(wd.T)
    g_col_h = np.ascontiguousarray(g.reshape(ICN, 128).T)

    cf_full, sf_full, perm, iden, ones, e2, dm, rope_expand = _host_tables()

    in_maps = []
    meta = []
    for c in range(8):
        b = c // 4
        j = c % 4
        pA, pB = j, 4 + j
        meta.append((b, pA, pB))
        pos = np.concatenate([np.arange(512 * pA, 512 * (pA + 1)),
                              np.arange(512 * pB, 512 * (pB + 1))])
        xo = np.ascontiguousarray(
            np.concatenate([xT[b][:, 512 * pA:512 * (pA + 1)],
                            xT[b][:, 512 * pB:512 * (pB + 1)]], axis=1))
        co, so = rope_expand(pos)
        pat = np.zeros((128, NGT), np.float32)
        for s_, p, ngf, base in ((0, pA, 6, 0), (1, pB, 14, NG0)):
            need = 2 * p
            for gg_ in range(ngf):
                if gg_ >= need:
                    pat[:, base + gg_] = NEG_BIG
        in_maps.append({
            "x_own": xo, "x_full": xT[b],
            "wq": wq, "wk": wk, "wv": wv, "wo": wo,
            "wgT": wgT_h, "wuT": wuT_h, "wdT": wdT_h, "g_col": g_col_h,
            "rope_cf": cf_full, "rope_sf": sf_full,
            "rope_co": co, "rope_so": so,
            "perm": perm, "iden": iden, "ones": ones, "e2": e2,
            "dmask": dm, "pat": pat,
        })

    nc = _get_nc()
    res = run_bass_kernel_spmd(nc, in_maps, core_ids=list(range(8)))

    out = np.empty((2, S, D), np.float32)
    for c in range(8):
        b, pA, pB = meta[c]
        yc = np.asarray(res.results[c]["y"])
        out[b, 512 * pA:512 * (pA + 1), :] = yc[0:512]
        out[b, 512 * pB:512 * (pB + 1), :] = yc[512:1024]
    return out


# version 10
# speedup vs baseline: 1.3277x; 1.2514x over previous
import numpy as np
from contextlib import ExitStack

import concourse.bass as bass
import concourse.mybir as mybir
from concourse.bass_utils import run_bass_kernel_spmd

F32 = mybir.dt.float32
BF16 = mybir.dt.bfloat16
I8 = mybir.dt.int8
AF = mybir.ActivationFunctionType
ALU = mybir.AluOpType
AX = mybir.AxisListType

D = 2048
S = 4096
T = 1024
NH = 32
NKV = 8
HD = 64
II = 8192
DC = D // 128
ICN = II // 128
LN_EPS = 1e-5
SUB_EPS = 1e-6
EPS_Q = 1e-5
NEG_BIG = -1.0e30
NG0, NG1 = 8, 16
NGT = NG0 + NG1


class Ctr:
    def __init__(self, h):
        self.h = h
        self.n = 0

    def inc(self, inst, k=1):
        inst.then_inc(self.h, k)
        self.n += k
        return inst

    def wait(self, eng, n=None):
        eng.wait_ge(self.h, self.n if n is None else n)


class Buf:

    def __init__(self, ctr=None, depth=1):
        self.acq = 0
        self.depth = depth
        self.marks = []

    def acquire(self, eng):
        idx = self.acq - self.depth
        if 0 <= idx < len(self.marks):
            for c, n in self.marks[idx]:
                eng.wait_ge(c.h, n)
        elif idx >= len(self.marks):
            raise RuntimeError("Buf.acquire before matching release")
        self.acq += 1

    def release(self, *ctrs):
        self.marks.append([(c, c.n) for c in ctrs])


class Ring:

    def __init__(self, rdys, fre, n):
        self.rdys = rdys
        self.fre = fre
        self.n = n
        self.p = 0

    def produce(self, eng):
        slot = self.p % self.n
        if self.p >= self.n:
            eng.wait_ge(self.fre.h, self.p - self.n + 1)
        self.p += 1
        return slot

    def mark(self, inst):
        slot = (self.p - 1) % self.n
        self.rdys[slot].inc(inst, 16)

    def wait_ready(self, eng, i):
        slot = i % self.n
        eng.wait_ge(self.rdys[slot].h, 16 * (i // self.n + 1))

    def release(self, eng, ctr):
        eng.wait_ge(ctr.h, ctr.n)
        self.fre.inc(eng.nop())


DEBUG = False
SERIAL_D1 = False
NO_M2 = False


def build_program():
    nc = bass.Bass("TRN2", target_bir_lowering=False, detect_race_conditions=False)
    dkind = "ExternalOutput" if DEBUG else "Internal"

    def din(name, shape):
        return nc.dram_tensor(name, list(shape), F32, kind="ExternalInput").ap()

    x_own = din("x_own", (D, T))
    x_full = din("x_full", (D, S))
    wq_d = din("wq", (D, D))
    wk_d = din("wk", (D, NKV * HD))
    wv_d = din("wv", (D, NKV * HD))
    wo_d = din("wo", (D, D))
    wgT = din("wgT", (D, II))
    wuT = din("wuT", (D, II))
    wdT = din("wdT", (II, D))
    g_col = din("g_col", (128, ICN))
    rope_cf = din("rope_cf", (128, S))
    rope_sf = din("rope_sf", (128, S))
    rope_co = din("rope_co", (128, T))
    rope_so = din("rope_so", (128, T))
    perm_in = din("perm", (128, 128))
    iden_in = din("iden", (128, 128))
    ones_in = din("ones", (128, 128))
    e2_in = din("e2", (128, 2))
    dmask_in = din("dmask", (128, 2048))
    pat_in = din("pat", (128, NGT))
    y_out = nc.dram_tensor("y", [T, D], F32, kind="ExternalOutput").ap()
    act_scr = nc.dram_tensor("act_scr", [II, T], F32, kind=dkind).ap()
    xln1_scr = nc.dram_tensor("xln1_scr", [D, T], F32, kind=dkind).ap()
    y2_scr = nc.dram_tensor("y2_scr", [D, T], F32, kind=dkind).ap()
    if DEBUG:
        dbg_att = nc.dram_tensor("dbg_att", [D, T], F32, kind="ExternalOutput").ap()
        dbg_misc = nc.dram_tensor("dbg_misc", [128, 64], F32, kind="ExternalOutput").ap()
        dbg_nx = nc.dram_tensor("dbg_nx", [D, T], F32, kind="ExternalOutput").ap()
        dbg_q = nc.dram_tensor("dbg_q", [D, T], F32, kind="ExternalOutput").ap()
        dbg_rg = nc.dram_tensor("dbg_rg", [II, T], F32, kind="ExternalOutput").ap()
        dbg_k = nc.dram_tensor("dbg_k", [512, S], F32, kind="ExternalOutput").ap()
        dbg_v = nc.dram_tensor("dbg_v", [128, 32 * NKV * 65], F32, kind="ExternalOutput").ap()

    st0 = ExitStack()

    def sb(stk, name, shape, dt=BF16, side=None):
        return stk.enter_context(nc.sbuf_tensor(name, list(shape), dt, side=side))

    def pt(stk, name, shape, dt=F32):
        return stk.enter_context(nc.psum_tensor(name, list(shape), dt))

    def sem(name):
        return Ctr(st0.enter_context(nc.semaphore(name)))

    iden_b = sb(st0, "iden_b", (128, 128))
    iden_f = sb(st0, "iden_f", (128, 128), F32)
    ones_f = sb(st0, "ones_f", (128, 128), F32)
    ones_b = sb(st0, "ones_b", (128, 128))
    pat_s = sb(st0, "pat_s", (128, NGT), F32)
    gcol_s = sb(st0, "gcol_s", (128, ICN), F32)
    biastbl = sb(st0, "biastbl", (128, NGT), F32)
    colA = sb(st0, "colA", (128, 16), F32)
    wabs = sb(st0, "wabs", (128, 200), F32)
    stg = sb(st0, "stg", (128, 128), F32)
    scl = sb(st0, "scl", (1, 128), F32)
    rows = None

    def row(q, half):
        return rows[32 * half:32 * half + 1, 512 * q:512 * (q + 1)]

    R_S1, R_S2, R_MEAN, R_RSTD, R_A, R_B, R_C, R_DD = range(8)

    qs = sem("qs")
    qg = sem("qg")
    qa = sem("qa")
    st = sem("st")
    sv = sem("sv")
    ss = sem("ss")
    bar = sem("bar")

    blkstack = ExitStack()
    block = blkstack.enter_context(nc.Block())
    ENGS = ["tensor", "vector", "scalar", "gpsimd", "sync"]
    RAWE = {"tensor": nc.tensor, "vector": nc.vector, "scalar": nc.scalar,
            "gpsimd": nc.gpsimd, "sync": nc.sync}
    raw_mode = [False]

    def sec(name):
        if raw_mode[0]:
            eng = RAWE[name]
            return lambda fn: fn(eng)
        return getattr(block, name)

    def vsafe(v, inst):
        sv.inc(inst)
        v.wait_ge(sv.h, sv.n)
        return inst

    def barrier():
        for e in ENGS:
            @sec(e)
            def _(eng):
                eng.nop().then_inc(bar.h, 1)
        bar.n += 5
        tgt = bar.n
        for e in ENGS:
            @sec(e)
            def _(eng):
                eng.wait_ge(bar.h, tgt)

    def broadcast_row(slotq, dst_tile, pbank, pbuf):
        for half in range(2):
            @sec("tensor")
            def _(t, half=half):
                sv.wait(t)
                ss.wait(t)
                t.wait_ge(qs.h, qs.n)
                pbuf.acquire(t)
                _li = t.matmul(pbank[:, 0:512], ones_f[32 * half:32 * half + 1, :],
                         row(slotq, half), start=True, stop=True)
                st.inc(_li)

            @sec("vector")
            def _(v, half=half):
                st.wait(v)
                _li = v.tensor_copy(dst_tile[:, 512 * half:512 * (half + 1)],
                                  pbank[:, 0:512])
                sv.inc(_li)
                pbuf.release(sv)

    @sec("gpsimd")
    def _(g):
        for dst, src in [(iden_b[:], iden_in), (ones_b[:], ones_in)]:
            qg.inc(g.dma_start(out=dst, in_=src), 16)

    @sec("sync")
    def _(s):
        for dst, src in [(iden_f[:], iden_in), (ones_f[:], ones_in),
                         (pat_s[:], pat_in), (gcol_s[:], g_col)]:
            qs.inc(s.dma_start(out=dst, in_=src), 16)

    barrier()

    stQ = ExitStack()
    Qf = sb(stQ, "Qf", (128, DC, T), side="right")
    n_x = Qf

    stKV = ExitStack()
    Kf = sb(stKV, "Kf", (128, 4, S))
    Vt = sb(stKV, "Vt", (128, 32, NKV, 65))
    Kd = sb(stKV, "Kd", (128, 4, T))
    Vd = sb(stKV, "Vd", (128, 8, NKV, 65))

    with ExitStack() as stT:
        perm_b = sb(stT, "perm_b", (128, 128))
        e2_b = sb(stT, "e2_b", (128, 2))
        cf_b = sb(stT, "cf_b", (128, S))
        sf_b = sb(stT, "sf_b", (128, S))
        co_b = sb(stT, "co_b", (128, T))
        so_b = sb(stT, "so_b", (128, T))
        xstr = sb(stT, "xstr", (128, 2, DC, 256))
        wk_s = sb(stT, "wk_s", (128, DC, 512))
        wv_s = sb(stT, "wv_s", (128, DC, 512))
        wq_s = sb(stT, "wq_s", (128, 2, DC, 256))
        kpre = sb(stT, "kpre", (128, 2, 256))
        ktmp = sb(stT, "ktmp", (128, 256))
        ktmp2 = sb(stT, "ktmp2", (128, 256))
        sqt = sb(stT, "sqt", (128, 1024))
        pp0 = pt(stT, "pp0", (128, 512))
        pp1 = pt(stT, "pp1", (128, 512))
        pr0 = pt(stT, "pr0", (128, 256))
        pr1 = pt(stT, "pr1", (128, 256))
        pnm = pt(stT, "pnm", (128, 512))
        ptf = pt(stT, "ptf", (128, 128))

        xring = Ring([sem("xr_r0"), sem("xr_r1")], sem("rf1"), 2)
        wqring = Ring([sem("wq_r0"), sem("wq_r1")], sem("rf2"), 2)
        bpp = [Buf(sem("bpp0")), Buf(sem("bpp1"))]
        bpr = [Buf(sem("bpr0")), Buf(sem("bpr1"))]
        bkp = [Buf(sem("bkp0")), Buf(sem("bkp1"))]
        bnm = Buf(sem("bnm"))
        btf = Buf(sem("btf"))

        @sec("gpsimd")
        def _(g):
            for dst, src in [(perm_b[:], perm_in), (e2_b[:], e2_in),
                             (cf_b[:], rope_cf), (sf_b[:], rope_sf),
                             (co_b[:], rope_co), (so_b[:], rope_so)]:
                qg.inc(g.dma_start(out=dst, in_=src), 16)
            qg.inc(g.dma_start(out=wk_s[:],
                               in_=wk_d.rearrange("(c p) n -> p c n", p=128)), 16)
            qg.inc(g.dma_start(out=wv_s[:],
                               in_=wv_d.rearrange("(c p) n -> p c n", p=128)), 16)
        kvw_mark = qg.n

        NT_KV = 20

        def xsrc(i):
            if i < 16:
                return x_full.rearrange("(c p) t -> p c t", p=128)[:, :, 256 * i:256 * (i + 1)]
            j = i - 16
            return x_own.rearrange("(c p) t -> p c t", p=128)[:, :, 256 * j:256 * (j + 1)]

        @sec("gpsimd")
        def _(g):
            for i in range(NT_KV):
                slot = xring.produce(g)
                xring.mark(g.dma_start(out=xstr[:, slot, :, :], in_=xsrc(i)))

        pcnt = 0
        for ti in range(NT_KV):
            own = ti >= 16
            toff = 256 * (ti - 16) if own else 256 * ti
            cf, sf_ = (co_b, so_b) if own else (cf_b, sf_b)
            kdst = Kd if own else Kf
            for c in range(4):
                par = pcnt % 2
                pcnt += 1
                pbt, pb = (pp0, bpp[0]) if par == 0 else (pp1, bpp[1])
                prt, prb = (pr0, bpr[0]) if par == 0 else (pr1, bpr[1])
                kpb = bkp[par]

                @sec("tensor")
                def _(t, ti=ti, c=c, pbt=pbt, pb=pb):
                    xring.wait_ready(t, ti)
                    if ti == 0 and c == 0:
                        t.wait_ge(qg.h, kvw_mark)
                    pb.acquire(t)
                    for k in range(DC):
                        _li = t.matmul(pbt[:, 0:256], wk_s[:, k, 128 * c:128 * (c + 1)],
                                 xstr[:, ti % 2, k, :],
                                 start=(k == 0), stop=(k == DC - 1))
                    st.inc(_li)

                @sec("scalar")
                def _(sc, pbt=pbt, pb=pb, kpb=kpb, par=par):
                    st.wait(sc)
                    kpb.acquire(sc)
                    _li = sc.copy(kpre[:, par, :], pbt[:, 0:256])
                    ss.inc(_li)
                    pb.release(ss)

                @sec("tensor")
                def _(t, prt=prt, prb=prb, par=par):
                    ss.wait(t)
                    prb.acquire(t)
                    _li = t.matmul(prt[:], perm_b[:], kpre[:, par, :], start=True, stop=True)
                    st.inc(_li)

                @sec("vector")
                def _(v, c=c, prt=prt, prb=prb, kpb=kpb, par=par,
                      kdst=kdst, cf=cf, sf_=sf_, toff=toff):
                    st.wait(v)
                    _li = v.tensor_mul(ktmp[:], kpre[:, par, :], cf[:, toff:toff + 256])
                    _li = v.tensor_mul(ktmp2[:], prt[:], sf_[:, toff:toff + 256])
                    _li = v.tensor_add(kdst[:, c, toff:toff + 256], ktmp[:], ktmp2[:])
                    sv.inc(_li)
                    prb.release(sv)
                    kpb.release(sv)

            for m in range(2):
                par = pcnt % 2
                pcnt += 1
                pbt, pb = (pp0, bpp[0]) if par == 0 else (pp1, bpp[1])
                tc = (toff + 128 * m) // 128

                @sec("tensor")
                def _(t, ti=ti, m=m, pbt=pbt, pb=pb):
                    pb.acquire(t)
                    for k in range(DC):
                        _li = t.matmul(pbt[:], xstr[:, ti % 2, k, 128 * m:128 * (m + 1)],
                                 wv_s[:, k, :],
                                 start=(k == 0), stop=(k == DC - 1))
                    st.inc(_li)

                @sec("vector")
                def _(v, pbt=pbt, pb=pb, tc=tc, own=own):
                    st.wait(v)
                    dst = Vd if own else Vt
                    sv.inc(v.tensor_copy(
                        dst[:, tc, :, 0:64],
                        pbt[:].rearrange("p (h n) -> p h n", h=NKV)))
                    pb.release(sv)

            @sec("tensor")
            def _(t):
                sv.wait(t)
                xring.release(t, st)

        @sec("vector")
        def _(v):
            sv.wait(v)
            _li = v.memset(Vt[:, :, :, 64:65], 1.0)
            _li = v.memset(Vd[:, :, :, 64:65], 1.0)
            sv.inc(_li)

        kv_done_sv = sv.n

        @sec("gpsimd")
        def _(g):
            g.wait_ge(sv.h, kv_done_sv)
            for e8 in range(8):
                slot = wqring.produce(g)
                wqring.mark(g.dma_start(
                    out=wq_s[:, slot, :, :],
                    in_=wq_d.rearrange("(c p) n -> p c n", p=128)[:, :, 256 * e8:256 * (e8 + 1)]))
                for ot in range(4):
                    slot = xring.produce(g)
                    xring.mark(g.dma_start(out=xstr[:, slot, :, :],
                                           in_=xsrc(16 + ot)))

        for e8 in range(8):
            for ot in range(4):
                tidx = NT_KV + 4 * e8 + ot
                toff = 256 * ot
                for q2 in range(2):
                    qc = 2 * e8 + q2
                    par = pcnt % 2
                    pcnt += 1
                    pbt, pb = (pp0, bpp[0]) if par == 0 else (pp1, bpp[1])
                    prt, prb = (pr0, bpr[0]) if par == 0 else (pr1, bpr[1])
                    kpb = bkp[par]

                    @sec("tensor")
                    def _(t, e8=e8, q2=q2, tidx=tidx, pbt=pbt, pb=pb):
                        xring.wait_ready(t, tidx)
                        wqring.wait_ready(t, e8)
                        pb.acquire(t)
                        for k in range(DC):
                            _li = t.matmul(pbt[:, 0:256],
                                     wq_s[:, e8 % 2, k, 128 * q2:128 * (q2 + 1)],
                                     xstr[:, tidx % 2, k, :],
                                     start=(k == 0), stop=(k == DC - 1))
                        st.inc(_li)

                    @sec("scalar")
                    def _(sc, pbt=pbt, pb=pb, kpb=kpb, par=par):
                        st.wait(sc)
                        kpb.acquire(sc)
                        _li = sc.copy(kpre[:, par, :], pbt[:, 0:256])
                        ss.inc(_li)
                        pb.release(ss)

                    @sec("tensor")
                    def _(t, prt=prt, prb=prb, par=par):
                        ss.wait(t)
                        prb.acquire(t)
                        _li = t.matmul(prt[:], perm_b[:], kpre[:, par, :],
                                 start=True, stop=True)
                        st.inc(_li)

                    @sec("vector")
                    def _(v, qc=qc, toff=toff, prt=prt, prb=prb, kpb=kpb, par=par):
                        st.wait(v)
                        _li = v.tensor_mul(ktmp[:], kpre[:, par, :], co_b[:, toff:toff + 256])
                        _li = v.tensor_mul(ktmp2[:], prt[:], so_b[:, toff:toff + 256])
                        _li = v.tensor_add(Qf[:, qc, toff:toff + 256], ktmp[:], ktmp2[:])
                        sv.inc(_li)
                        prb.release(sv)
                        kpb.release(sv)

                @sec("tensor")
                def _(t):
                    sv.wait(t)
                    xring.release(t, st)

            @sec("tensor")
            def _(t):
                wqring.release(t, st)

        ncol = 0

        bsq = Buf()

        def normmax_tiles(tiles):
            nonlocal ncol
            first = ncol
            for ap1024 in tiles:
                @sec("scalar")
                def _(sc, ap1024=ap1024):
                    sv.wait(sc)
                    bsq.acquire(sc)
                    _li = sc.activation(sqt[:], ap1024, AF.Square)
                    ss.inc(_li)

                for j in range(2):
                    @sec("tensor")
                    def _(t, j=j):
                        ss.wait(t)
                        bnm.acquire(t)
                        _li = t.matmul(pnm[0:2, 0:512], e2_b[:],
                                 sqt[:, 512 * j:512 * (j + 1)],
                                 start=True, stop=True)
                        st.inc(_li)
                        if j == 1:
                            bsq.release(st)

                    @sec("vector")
                    def _(v, ncol=ncol):
                        st.wait(v)
                        _li = v.tensor_reduce(stg[0:2, 2 + ncol:3 + ncol], pnm[0:2, 0:512],
                                        AX.X, ALU.max)
                        sv.inc(_li)
                        bnm.release(sv)
                    ncol += 1
            return first

        @sec("vector")
        def _(v):
            sv.wait(v)
            _li = v.memset(stg[:], 0.0)
            sv.inc(_li)

        kfirst = normmax_tiles([Kf[:, c, 1024 * n:1024 * (n + 1)]
                                for c in range(4) for n in range(4)])
        qfirst = normmax_tiles([Qf[:, qc, :] for qc in range(DC)])

        for which, first, last, out_c in ((0, kfirst, qfirst, 0), (1, qfirst, ncol, 1)):
            @sec("vector")
            def _(v, first=first, last=last):
                sv.wait(v)
                vsafe(v, v.tensor_reduce(stg[0:2, 0:1], stg[0:2, 2 + first:2 + last],
                                AX.X, ALU.max))

            @sec("tensor")
            def _(t):
                sv.wait(t)
                btf.acquire(t)
                _li = t.transpose(ptf[:], stg[:], iden_f[:])
                st.inc(_li)

            @sec("vector")
            def _(v, out_c=out_c):
                st.wait(v)
                vsafe(v, v.tensor_reduce(scl[0:1, out_c:out_c + 1], ptf[0:1, 0:2],
                                AX.X, ALU.max))
                btf.release(sv)

        @sec("vector")
        def _(v):
            sv.wait(v)
            vsafe(v, v.tensor_mul(scl[0:1, 2:3], scl[0:1, 0:1], scl[0:1, 1:2]))

        @sec("scalar")
        def _(sc):
            sv.wait(sc)
            _li = sc.activation(scl[0:1, 3:4], scl[0:1, 2:3], AF.Sqrt)
            ss.inc(_li)

        @sec("vector")
        def _(v):
            ss.wait(v)
            vsafe(v, v.tensor_scalar_mul(scl[0:1, 4:5], scl[0:1, 3:4], -0.126))

        @sec("tensor")
        def _(t):
            sv.wait(t)
            bnm.acquire(t)
            _li = t.matmul(pnm[:, 0:1], ones_f[0:1, :], scl[0:1, 4:5], start=True, stop=True)
            st.inc(_li)

        @sec("vector")
        def _(v):
            st.wait(v)
            _li = v.tensor_copy(colA[:, 0:1], pnm[:, 0:1])
            sv.inc(_li)
            bnm.release(sv)
            v.wait_ge(sv.h, sv.n)
            sv.inc(v.tensor_scalar(biastbl[:], pat_s[:], colA[:, 0:1], None, ALU.add))

        if DEBUG:
            @sec("gpsimd")
            def _(g):
                g.wait_ge(sv.h, sv.n)
                qg.inc(g.dma_start(out=dbg_q.rearrange("(c p) t -> p c t", p=128),
                                   in_=Qf[:]), 16)
                qg.inc(g.dma_start(out=dbg_k.rearrange("(c p) t -> p c t", p=128),
                                   in_=Kf[:]), 16)
                qg.inc(g.dma_start(out=dbg_v[:],
                                   in_=Vt[:].rearrange("p a h n -> p (a h n)")), 16)
                qg.inc(g.dma_start(out=dbg_misc[:, 0:NGT], in_=biastbl[:]), 16)
                qg.inc(g.dma_start(out=dbg_misc[:, 56:64],
                                   in_=stg[:, 0:8]), 16)
                g.wait_ge(qg.h, qg.n)

            @sec("sync")
            def _(s):
                ss.wait(s)
                sv.wait(s)
                qs.inc(s.dma_start(out=dbg_misc[0:1, 40:45], in_=scl[0:1, 0:5]), 16)

        barrier()
    blkstack.close()
    raw_mode[0] = True

    stAtt = ExitStack()
    att = sb(stAtt, "att", (128, DC, T), side="right")

    with ExitStack() as stB:
        dmask_b = sb(stB, "dmask_b", (128, 2048))
        ebuf = sb(stB, "ebuf", (128, 2, 1024))
        avsb = sb(stB, "avsb", (64, 512))
        recb = sb(stB, "recb", (1, 512))
        recf = sb(stB, "recf", (1, 512), F32)
        w1str = sb(stB, "w1str", (128, 2, 2048), F32)
        psc0 = pt(stB, "psc0", (128, 1024))
        psc1 = pt(stB, "psc1", (128, 1024))
        pav = pt(stB, "pav", (65, 512))
        pbc = pt(stB, "pbc", (128, 512))

        w1ring = Ring([sem("w1_r0"), sem("w1_r1")], sem("rf3"), 2)

        @sec("gpsimd")
        def _(g):
            qg.inc(g.dma_start(out=dmask_b[:], in_=dmask_in), 16)
        dm_mark = qg.n

        NW1 = 192

        def w1_src(i):
            if i < 64:
                return wgT.rearrange("(c p) n -> p c n", p=128)[:, i // 4, 2048 * (i % 4):2048 * (i % 4 + 1)]
            if i < 128:
                j = i - 64
                return wuT.rearrange("(c p) n -> p c n", p=128)[:, j // 4, 2048 * (j % 4):2048 * (j % 4 + 1)]
            j = i - 128
            return wdT.rearrange("(c p) h -> p c h", p=128)[:, j:j + 1, :] \
                .rearrange("p c n -> p (c n)")

        @sec("sync")
        def _(s):
            for i in range(NW1):
                slot = w1ring.produce(s)
                w1ring.mark(s.dma_start(out=w1str[:, slot, :], in_=w1_src(i)))

        def emit_w1_reduce(i):
            v = RAWE["vector"]
            w1ring.wait_ready(v, i)
            _li = v.tensor_reduce(wabs[:, i:i + 1], w1str[:, i % 2, :], AX.X,
                            ALU.add, apply_absolute_value=True)
            sv.inc(_li)
            w1ring.release(v, sv)

        b0heads = [h for h in range(NH) if (h // 4) % 2 == 0]
        b64heads = [h for h in range(NH) if (h // 4) % 2 == 1]
        QCH = {}
        QB = {}
        for i in range(16):
            QCH[b0heads[i]] = i
            QB[b0heads[i]] = 0
            QCH[b64heads[i]] = i
            QB[b64heads[i]] = 64

        def emit_attention(p):
            NGA = 2 * p + 2
            NGB = 16 - 2 * p
            bsc = [Buf(), Buf()]
            beb = [Buf(), Buf()]
            bav = Buf()
            bbc = Buf()
            gidx = [0]
            w1i = [0]
            pending = [None]
            recip_mark = [0]
            attmult_mark = [0]
            for h in range(NH):
                kvh = h // 4
                kc = kvh // 2
                kb0 = (kvh % 2) * 64
                qc2 = QCH[h]
                qb0 = QB[h]
                for s_ in range(2):
                    NG = NGA if s_ == 0 else NGB
                    qoff = 512 * s_
                    for gg in range(NG):
                        par = gidx[0] % 2
                        gidx[0] += 1
                        psct, bscb = (psc0, bsc[0]) if par == 0 else (psc1, bsc[1])
                        bebb = beb[par]
                        diag = gg >= NG - 2
                        dg = gg - (NG - 2)
                        biascol = (NGA if s_ == 1 else 0) + gg

                        @sec("tensor")
                        def _(t, s_=s_, gg=gg, psct=psct, bscb=bscb, diag=diag,
                              dg=dg, kc=kc, kb0=kb0, qc2=qc2, qb0=qb0, qoff=qoff,
                              first=(gidx[0] == 1)):
                            if first:
                                t.wait_ge(qg.h, dm_mark)
                            bscb.acquire(t)
                            for j in range(2):
                                if diag:
                                    ksl = Kd[kb0:kb0 + 64, kc,
                                             qoff + 256 * dg + 128 * j:qoff + 256 * dg + 128 * (j + 1)]
                                else:
                                    ksl = Kf[kb0:kb0 + 64, kc,
                                             256 * gg + 128 * j:256 * gg + 128 * (j + 1)]
                                _li = t.matmul(psct[:, 512 * j:512 * (j + 1)], ksl,
                                         Qf[qb0:qb0 + 64, qc2, qoff:qoff + 512],
                                         start=True, stop=True)
                            st.inc(_li)

                        if gg == 0 and pending[0] is not None:
                            pending[0]()
                            pending[0] = None

                        @sec("scalar")
                        def _(sc, par=par, psct=psct, bscb=bscb, bebb=bebb,
                              biascol=biascol):
                            st.wait(sc)
                            bebb.acquire(sc)
                            _li = sc.activation(ebuf[:, par, :], psct[:], AF.Exp,
                                          bias=biastbl[:, biascol:biascol + 1],
                                          scale=0.125)
                            ss.inc(_li)
                            bscb.release(ss)

                        dmk = [0]
                        if diag:
                            @sec("vector")
                            def _(v, par=par, dg=dg):
                                ss.wait(v)
                                sv.inc(v.tensor_mul(ebuf[:, par, :], ebuf[:, par, :],
                                                    dmask_b[:, 1024 * dg:1024 * (dg + 1)]))
                                dmk[0] = sv.n

                        @sec("tensor")
                        def _(t, s_=s_, gg=gg, par=par, bebb=bebb, diag=diag,
                              dg=dg, kvh=kvh, NG=NG, dmk=dmk):
                            if diag:
                                t.wait_ge(sv.h, dmk[0])
                            else:
                                ss.wait(t)
                            if gg == 0:
                                bav.acquire(t)
                            for j in range(2):
                                if diag:
                                    vtile = Vd[:, 4 * s_ + 2 * dg + j, kvh, :]
                                else:
                                    vtile = Vt[:, 2 * gg + j, kvh, :]
                                _li = t.matmul(pav[:], vtile,
                                         ebuf[:, par, 512 * j:512 * (j + 1)],
                                         start=(gg == 0 and j == 0),
                                         stop=(gg == NG - 1 and j == 1))
                            st.inc(_li)
                            bebb.release(st)

                    @sec("vector")
                    def _(v):
                        st.wait(v)
                        _li = v.reciprocal(recf[:], pav[64:65, :])
                        sv.inc(v.tensor_copy(recb[:], recf[:]))
                        recip_mark[0] = sv.n

                    @sec("scalar")
                    def _(sc, am=attmult_mark[0]):
                        st.wait(sc)
                        sc.wait_ge(sv.h, am)
                        _li = sc.copy(avsb[:], pav[0:64, :])
                        ss.inc(_li)

                    bav.release(sv, ss)

                    def tail2(qc2=qc2, qb0=qb0, qoff=qoff, rm=recip_mark[0]):
                        @sec("tensor")
                        def _(t):
                            t.wait_ge(sv.h, rm)
                            bbc.acquire(t)
                            _li = t.matmul(pbc[0:64, :], ones_b[0:1, 0:64], recb[:],
                                     start=True, stop=True)
                            st.inc(_li)

                        @sec("vector")
                        def _(v):
                            st.wait(v)
                            sv.inc(v.tensor_mul(
                                att[qb0:qb0 + 64, qc2, qoff:qoff + 512],
                                avsb[:], pbc[0:64, :]))
                            attmult_mark[0] = sv.n
                            bbc.release(sv)
                        for _k in range(3):
                            emit_w1_reduce(w1i[0])
                            w1i[0] += 1
                    pending[0] = tail2
            pending[0]()
            pending[0] = None
            assert w1i[0] == NW1, w1i[0]

        ET = mybir.EngineType
        pid = nc.partition_id(engines=[ET.PE, ET.DVE, ET.Activation])
        pid4 = pid % 4
        base_state = (st.n, sv.n, ss.n, w1ring.fre.n)
        end_states = []
        for body in nc.Switch(engines=[RAWE["tensor"], RAWE["vector"], RAWE["scalar"]],
                              index=[pid4, pid4, pid4], n=4):
            st.n, sv.n, ss.n, w1ring.fre.n = base_state
            emit_attention(body)
            end_states.append((st.n, sv.n, ss.n, w1ring.fre.n))
        assert len(set(end_states)) == 1, end_states
        st.n, sv.n, ss.n, w1ring.fre.n = end_states[0]

        barrier()

        if DEBUG:
            @sec("gpsimd")
            def _(g):
                qg.inc(g.dma_start(out=dbg_att.rearrange("(c p) t -> p c t", p=128),
                                   in_=att[:]), 16)
                g.wait_ge(qg.h, qg.n)

    stKV.close()
    raw_mode[0] = False
    block = blkstack.enter_context(nc.Block())
    stRows = ExitStack()
    rows = sb(stRows, "rows", (128, 8 * 512), F32)

    with ExitStack() as stC:
        wo_s = sb(stC, "wo_s", (128, 2, DC, 256))
        xo_f = sb(stC, "xo_f", (128, DC, T), F32)
        ctile = sb(stC, "ctile", (128, 2, 1024), F32)
        ctil2 = sb(stC, "ctil2", (128, 1024), F32)
        cci8 = sb(stC, "cci8", (128, 2, 1024), I8)
        ccb0 = sb(stC, "ccb0", (128, 2, 1024))
        cbf = sb(stC, "cbf", (128, 1024))
        cbf2 = sb(stC, "cbf2", (128, 1024))
        mxf = sb(stC, "mxf", (128, 1024), F32)
        qmx = sb(stC, "qmx", (128, 16), F32)
        trow = sb(stC, "trow", (128, 512), F32)
        sxb = sb(stC, "sxb", (128, T), F32)
        bcst = sb(stC, "bcst", (128, 2, 1024), F32)
        pw0 = pt(stC, "pw0", (128, 1024))
        pw1 = pt(stC, "pw1", (128, 1024))
        prow1 = pt(stC, "prow1", (128, 512))
        prow2 = pt(stC, "prow2", (128, 512))
        pbc0 = pt(stC, "pbc0", (128, 512))
        ptf2 = pt(stC, "ptf2", (128, 512))

        woring = Ring([sem("wo_r0"), sem("wo_r1")], sem("rf4"), 2)
        bcb = Buf()
        bpw = [Buf(sem("bpw0")), Buf(sem("bpw1"))]
        brow = Buf(sem("brow"))
        bbc0 = Buf(sem("bbc0"))
        btf2 = Buf(sem("btf2"))

        @sec("gpsimd")
        def _(g):
            for e8 in range(8):
                slot = woring.produce(g)
                woring.mark(g.dma_start(
                    out=wo_s[:, slot, :, :],
                    in_=wo_d.rearrange("(c p) n -> p c n", p=128)[:, :, 256 * e8:256 * (e8 + 1)]))

        @sec("sync")
        def _(s):
            qs.inc(s.dma_start(out=xo_f[:],
                               in_=x_own.rearrange("(c p) t -> p c t", p=128)), 16)
        xo_mark = qs.n

        for dc in range(DC):
            par = dc % 2
            pwt, pwb = (pw0, bpw[0]) if par == 0 else (pw1, bpw[1])
            e8 = dc // 2
            q2 = dc % 2

            @sec("tensor")
            def _(t, dc=dc, pwt=pwt, pwb=pwb, e8=e8, q2=q2):
                woring.wait_ready(t, e8)
                pwb.acquire(t)
                for k in range(DC):
                    for th in range(2):
                        _li = t.matmul(pwt[:, 512 * th:512 * (th + 1)],
                                 wo_s[:, e8 % 2, k, 128 * q2:128 * (q2 + 1)],
                                 att[:, k, 512 * th:512 * (th + 1)],
                                 start=(k == 0), stop=(k == DC - 1))
                st.inc(_li)
                if q2 == 1:
                    woring.release(t, st)

            @sec("vector")
            def _(v, dc=dc, pwt=pwt, pwb=pwb):
                st.wait(v)
                if dc == 0:
                    v.wait_ge(qs.h, xo_mark)
                _li = v.tensor_add(xo_f[:, dc, :], pwt[:], xo_f[:, dc, :])
                sv.inc(_li)
                pwb.release(sv)

            @sec("scalar")
            def _(sc, dc=dc):
                sv.wait(sc)
                bcb.acquire(sc)
                _li = sc.copy(cbf[:], xo_f[:, dc, :])
                _li = sc.activation(cbf2[:], xo_f[:, dc, :], AF.Square)
                ss.inc(_li)

            @sec("tensor")
            def _(t, dc=dc):
                ss.wait(t)
                if dc == 0:
                    brow.acquire(t)
                for th in range(2):
                    _li = t.matmul(prow1[32 * th:32 * th + 1, 0:512], ones_b[:, 0:1],
                             cbf[:, 512 * th:512 * (th + 1)],
                             start=(dc == 0), stop=(dc == DC - 1))
                    _li = t.matmul(prow2[32 * th:32 * th + 1, 0:512], ones_b[:, 0:1],
                             cbf2[:, 512 * th:512 * (th + 1)],
                             start=(dc == 0), stop=(dc == DC - 1))
                st.inc(_li)
                bcb.release(st)

        @sec("vector")
        def _(v):
            st.wait(v)
            _li = v.tensor_copy(row(R_S1, 0), prow1[0:1, 0:512])
            _li = v.tensor_copy(row(R_S1, 1), prow1[32:33, 0:512])
            _li = v.tensor_copy(row(R_S2, 0), prow2[0:1, 0:512])
            _li = v.tensor_copy(row(R_S2, 1), prow2[32:33, 0:512])
            sv.inc(_li)
            brow.release(sv)

        def ln_rows_math():
            for half in range(2):
                @sec("vector")
                def _(v, half=half):
                    v.wait_ge(qs.h, qs.n)
                    _li = v.tensor_scalar_mul(row(R_MEAN, half), row(R_S1, half), 1.0 / D)
                    _li = v.tensor_mul(row(R_A, half), row(R_MEAN, half), row(R_MEAN, half))
                    _li = v.scalar_tensor_tensor(row(R_B, half), row(R_S2, half), 1.0 / D,
                                           row(R_A, half), ALU.mult, ALU.subtract)
                    _li = v.tensor_scalar_add(row(R_B, half), row(R_B, half), LN_EPS)
                    _li = v.reciprocal(row(R_C, half), row(R_B, half))
                    sv.inc(_li)

                @sec("scalar")
                def _(sc, half=half):
                    sv.wait(sc)
                    _li = sc.activation(row(R_RSTD, half), row(R_C, half), AF.Sqrt)
                    ss.inc(_li)

                @sec("vector")
                def _(v, half=half):
                    ss.wait(v)
                    _li = v.tensor_mul(row(R_A, half), row(R_RSTD, half), row(R_RSTD, half))
                    _li = v.tensor_mul(row(R_A, half), row(R_A, half), row(R_B, half))
                    _li = v.tensor_scalar(row(R_A, half), row(R_A, half), -0.5, 1.5,
                                    ALU.mult, ALU.add)
                    _li = v.tensor_mul(row(R_RSTD, half), row(R_RSTD, half), row(R_A, half))
                    sv.inc(_li)

        ln_rows_math()
        broadcast_row(R_MEAN, bcst[:, 0, :], pbc0, bbc0)
        broadcast_row(R_RSTD, bcst[:, 1, :], pbc0, bbc0)

        for dc in range(DC):
            @sec("vector")
            def _(v, dc=dc):
                sv.wait(v)
                _li = v.scalar_tensor_tensor(ctile[:, 0, :], xo_f[:, dc, :], 0.0,
                                       bcst[:, 0, :], ALU.add, ALU.subtract)
                sv.inc(v.tensor_mul(xo_f[:, dc, :], ctile[:, 0, :], bcst[:, 1, :]))

            @sec("scalar")
            def _(sc, dc=dc):
                sv.wait(sc)
                _li = sc.activation(ctil2[:], xo_f[:, dc, :], AF.Abs)
                ss.inc(_li)

            @sec("vector")
            def _(v, dc=dc):
                ss.wait(v)
                if dc == 0:
                    sv.inc(v.tensor_copy(mxf[:], ctil2[:]))
                else:
                    sv.inc(v.tensor_max(mxf[:], mxf[:], ctil2[:]))

        xln1_store_mark = [0]

        @sec("scalar")
        def _(sc):
            sv.wait(sc)
            qa.inc(sc.dma_start(out=xln1_scr.rearrange("(c p) t -> p c t", p=128),
                                in_=xo_f[:]), 16)
        xln1_store_mark[0] = qa.n

        for tb in range(8):
            @sec("tensor")
            def _(t, tb=tb):
                sv.wait(t)
                if tb % 4 == 0:
                    btf2.acquire(t)
                _li = t.transpose(ptf2[:, 128 * (tb % 4):128 * (tb % 4 + 1)],
                            mxf[:, 128 * tb:128 * (tb + 1)], iden_f[:])
                st.inc(_li)

            @sec("vector")
            def _(v, tb=tb):
                st.wait(v)
                _li = v.tensor_reduce(qmx[:, tb:tb + 1],
                                ptf2[:, 128 * (tb % 4):128 * (tb % 4 + 1)],
                                AX.X, ALU.max)
                sv.inc(_li)
                if tb % 4 == 3:
                    btf2.release(sv)

        @sec("vector")
        def _(v):
            sv.wait(v)
            vsafe(v, v.tensor_scalar_max(qmx[:, 8:16], qmx[:, 0:8], EPS_Q))
            _li = v.memset(stg[:], 0.0)
            vsafe(v, _li)
            sv.inc(v.tensor_copy(stg[:, 0:8], qmx[:, 8:16]))

        @sec("tensor")
        def _(t):
            sv.wait(t)
            btf2.acquire(t)
            _li = t.transpose(ptf2[:, 0:128], stg[:], iden_f[:])
            st.inc(_li)

        @sec("vector")
        def _(v):
            st.wait(v)
            v.wait_ge(qs.h, qs.n)
            sv.inc(v.tensor_copy(trow[0:8, 0:128], ptf2[0:8, 0:128]))
            btf2.release(sv)

        @sec("tensor")
        def _(t):
            sv.wait(t)
            btf2.acquire(t)
            for half in range(2):
                for cc in range(4):
                    _li = t.matmul(ptf2[32 * half:32 * half + 1, 128 * cc:128 * (cc + 1)],
                                   iden_f[0:8, 4 * half + cc:4 * half + cc + 1],
                                   trow[0:8, 0:128], start=True, stop=True)
            st.inc(_li)

        @sec("vector")
        def _(v):
            st.wait(v)
            _li = v.tensor_copy(row(R_C, 0), ptf2[0:1, 0:512])
            _li = v.tensor_copy(row(R_C, 1), ptf2[32:33, 0:512])
            sv.inc(_li)
            btf2.release(sv)

        for half in range(2):
            @sec("vector")
            def _(v, half=half):
                sv.wait(v)
                _li = v.reciprocal(row(R_A, half), row(R_C, half))
                sv.inc(v.tensor_scalar_mul(row(R_A, half), row(R_A, half), 127.0))

        broadcast_row(R_A, sxb[:], pbc0, bbc0)

        ccq_mark = {}
        for dc in range(DC):
            par = dc % 2

            @sec("vector")
            def _(v, dc=dc, par=par):
                sv.wait(v)
                if dc >= 2:
                    v.wait_ge(qg.h, ccq_mark[dc - 2])
                _li = v.tensor_mul(ctile[:, par, :], xo_f[:, dc, :], sxb[:])
                sv.inc(_li)

            @sec("gpsimd")
            def _(g, dc=dc, par=par):
                g.wait_ge(sv.h, sv.n)
                g.wait_ge(qg.h, qg.n)
                qg.inc(g.dma_start(out=cci8[:, par, :], in_=ctile[:, par, :]), 16)
                g.wait_ge(qg.h, qg.n)
                qg.inc(g.dma_start(out=ccb0[:, par, :], in_=cci8[:, par, :]), 16)
                ccq_mark[dc] = qg.n

            @sec("vector")
            def _(v, dc=dc, par=par):
                v.wait_ge(qg.h, qg.n)
                _li = v.tensor_copy(ctil2[:], ccb0[:, par, :])
                _li = v.tensor_sub(ctil2[:], ctile[:, par, :], ctil2[:])
                _li = v.tensor_single_scalar(cbf[:], ctil2[:], 0.5, ALU.is_ge)
                _li = v.tensor_single_scalar(cbf2[:], ctil2[:], -0.5, ALU.is_lt)
                _li = v.tensor_add(cbf[:], ccb0[:, par, :], cbf[:])
                sv.inc(v.tensor_sub(n_x[:, dc, :], cbf[:], cbf2[:]))

        @sec("vector")
        def _(v):
            sv.wait(v)
            st.wait(v)
            v.memset(stg[:], 0.0)
            for w in range(3):
                vsafe(v, v.tensor_reduce(wabs[:, 192 + w:193 + w],
                                         wabs[:, 64 * w:64 * (w + 1)],
                                         AX.X, ALU.add))
            sv.inc(v.tensor_copy(stg[:, 0:3], wabs[:, 192:195]))

        @sec("tensor")
        def _(t):
            sv.wait(t)
            btf2.acquire(t)
            _li = t.transpose(ptf2[:, 0:128], stg[:], iden_f[:])
            st.inc(_li)

        @sec("vector")
        def _(v):
            st.wait(v)
            sv.inc(v.tensor_copy(trow[0:3, 0:128], ptf2[0:3, 0:128]))
            btf2.release(sv)

        @sec("tensor")
        def _(t):
            sv.wait(t)
            btf2.acquire(t)
            for w in range(3):
                _li = t.matmul(ptf2[0:1, 128 * w:128 * (w + 1)],
                               iden_f[0:8, w:w + 1], trow[0:8, 0:128],
                               start=True, stop=True)
            st.inc(_li)

        @sec("vector")
        def _(v):
            st.wait(v)
            _li = v.tensor_copy(rows[0:1, 512 * R_DD:512 * R_DD + 384],
                                ptf2[0:1, 0:384])
            sv.inc(_li)
            btf2.release(sv)

        @sec("vector")
        def _(v):
            sv.wait(v)
            base = 512 * R_DD
            for w in range(3):
                _li = v.tensor_reduce(scl[0:1, 8 + w:9 + w],
                                rows[0:1, base + 128 * w:base + 128 * (w + 1)],
                                AX.X, ALU.add)
            vsafe(v, _li)
            vsafe(v, v.tensor_scalar(scl[0:1, 11:14], scl[0:1, 8:11],
                                     1.0 / (II * D), EPS_Q, ALU.mult, ALU.max))
            _li = v.tensor_scalar_mul(scl[0:1, 14:17], scl[0:1, 11:14], 0.5)
            vsafe(v, v.tensor_scalar_mul(scl[0:1, 17:20], scl[0:1, 11:14], -0.5))
            vsafe(v, v.tensor_mul(scl[0:1, 20:21], scl[0:1, 11:12], scl[0:1, 11:12]))
            vsafe(v, v.tensor_mul(scl[0:1, 21:22], scl[0:1, 20:21], scl[0:1, 12:13]))

        for i, col in enumerate([14, 17, 15, 18, 16, 19]):
            @sec("tensor")
            def _(t, col=col):
                sv.wait(t)
                bbc0.acquire(t)
                _li = t.matmul(pbc0[:, 0:1], ones_f[0:1, :], scl[0:1, col:col + 1],
                         start=True, stop=True)
                st.inc(_li)

            @sec("vector")
            def _(v, i=i):
                st.wait(v)
                sv.inc(v.tensor_copy(colA[:, 1 + i:2 + i], pbc0[:, 0:1]))
                bbc0.release(sv)

        for half in range(2):
            @sec("vector")
            def _(v, half=half):
                sv.wait(v)
                _li = v.tensor_scalar_mul(row(R_B, half), row(R_C, half), 1.0 / 127.0)
                _li = v.tensor_mul(row(R_DD, half), row(R_B, half), row(R_B, half))
                _li = v.tensor_mul(row(R_DD, half), row(R_DD, half), row(R_B, half))
                sv.inc(v.tensor_scalar(row(R_DD, half), row(R_DD, half),
                                       scl[0:1, 21:22], None, ALU.mult))

        if DEBUG:
            @sec("gpsimd")
            def _(g):
                g.wait_ge(sv.h, sv.n)
                qg.inc(g.dma_start(out=dbg_nx.rearrange("(c p) t -> p c t", p=128),
                                   in_=n_x[:]), 16)
                qg.inc(g.dma_start(out=dbg_misc[:, 48:56], in_=qmx[:, 0:8]), 16)
                qg.inc(g.dma_start(out=dbg_misc[:, 32:39], in_=colA[:, 0:7]), 16)
                g.wait_ge(qg.h, qg.n)

        barrier()

    stAtt.close()

    with ExitStack() as stD:
        wstr = sb(stD, "wstr", (128, 2, DC, 256), F32)
        twg = sb(stD, "twg", (128, DC, 256))
        twu = sb(stD, "twu", (128, DC, 256))
        qscr = sb(stD, "qscr", (128, 4096))
        rg_f = sb(stD, "rg_f", (128, 1024), F32)
        rg2 = sb(stD, "rg2", (128, 1024), F32)
        rawstg = sb(stD, "rawstg", (128, 8, 1024), F32)
        raw2b = sb(stD, "raw2b", (128, 1024))
        absg = sb(stD, "absg", (128, 1024), F32)
        mxrun = sb(stD, "mxrun", (128, 1024), F32)
        trow2 = sb(stD, "trow2", (128, 512), F32)
        pg0 = pt(stD, "pg0", (128, 1024))
        pg1 = pt(stD, "pg1", (128, 1024))
        pu = pt(stD, "pu", (128, 1024))
        pm2 = pt(stD, "pm2", (128, 512))

        wdring = Ring([sem("wd_r0"), sem("wd_r1")], sem("rf5"), 2)
        btg = Buf()
        btu = Buf()
        brb = Buf()
        brg = Buf()
        bpg = [Buf(sem("bpg0")), Buf(sem("bpg1"))]
        bpu = Buf(sem("bpu"))
        bm2 = Buf(sem("bm2"))
        brs = Buf(sem("brs"), depth=2)

        @sec("sync")
        def _(s):
            for ib in range(32):
                for wsrc in (wgT, wuT):
                    slot = wdring.produce(s)
                    wdring.mark(s.dma_start(
                        out=wstr[:, slot, :, :],
                        in_=wsrc.rearrange("(c p) n -> p c n", p=128)[:, :, 256 * ib:256 * (ib + 1)]))

        for ib in range(32):
            for wi, tw, thr in ((0, twg, 1), (1, twu, 3)):
                tile_i = 2 * ib + wi

                @sec("vector")
                def _(v, tile_i=tile_i, tw=tw, thr=thr, wi=wi):
                    wdring.wait_ready(v, tile_i)
                    (btg if wi == 0 else btu).acquire(v)
                    src = wstr[:, tile_i % 2, :, :].rearrange("p c n -> p (c n)")
                    twv = tw[:].rearrange("p c n -> p (c n)")
                    _li = v.tensor_scalar(twv, src, colA[:, thr:thr + 1], None, ALU.is_ge)
                    _li = v.tensor_scalar(qscr[:], src, colA[:, thr + 1:thr + 2], None,
                                    ALU.is_le)
                    sv.inc(v.tensor_sub(twv, twv, qscr[:]))
                    wdring.release(v, sv)

            for c2 in range(2):
                ic = 2 * ib + c2
                par = ic % 2
                pgt, pgb = (pg0, bpg[0]) if par == 0 else (pg1, bpg[1])

                @sec("tensor")
                def _(t, c2=c2, pgt=pgt, pgb=pgb):
                    sv.wait(t)
                    pgb.acquire(t)
                    for k in range(DC):
                        for th in range(2):
                            _li = t.matmul(pgt[:, 512 * th:512 * (th + 1)],
                                     twg[:, k, 128 * c2:128 * (c2 + 1)],
                                     n_x[:, k, 512 * th:512 * (th + 1)],
                                     start=(k == 0), stop=(k == DC - 1))
                    st.inc(_li)
                    if c2 == 1:
                        btg.release(st)
                    bpu.acquire(t)
                    for k in range(DC):
                        for th in range(2):
                            _li = t.matmul(pu[:, 512 * th:512 * (th + 1)],
                                     twu[:, k, 128 * c2:128 * (c2 + 1)],
                                     n_x[:, k, 512 * th:512 * (th + 1)],
                                     start=(k == 0), stop=(k == DC - 1))
                    st.inc(_li)
                    if c2 == 1:
                        btu.release(st)

                @sec("scalar")
                def _(sc, pgt=pgt):
                    st.wait(sc, st.n - 1)
                    brg.acquire(sc)
                    _li = sc.activation(rg_f[:], pgt[:], AF.Relu)
                    _li = sc.activation(rg2[:], rg_f[:], AF.Square)
                    ss.inc(_li)

                @sec("vector")
                def _(v, ic=ic, pgb=pgb):
                    ss.wait(v)
                    pgb.release(ss)
                    st.wait(v)
                    if ic % 4 == 0:
                        brs.acquire(v)
                    _li = v.tensor_mul(rawstg[:, ic % 8, :], rg2[:], pu[:])
                    sv.inc(_li)
                    brg.release(sv)
                    bpu.release(sv)

                if DEBUG:
                    @sec("scalar")
                    def _(sc, ic=ic):
                        sv.wait(sc)
                        qa.inc(sc.dma_start(
                            out=dbg_rg.rearrange("(c p) t -> p c t", p=128)[:, ic:ic + 1, :],
                            in_=rg_f[:]), 16)

                @sec("scalar")
                def _(sc, ic=ic):
                    sv.wait(sc)
                    brb.acquire(sc)
                    _li = sc.activation(raw2b[:], rawstg[:, ic % 8, :], AF.Square)
                    _li = sc.activation(absg[:], rawstg[:, ic % 8, :], AF.Abs,
                                  scale=gcol_s[:, ic:ic + 1])
                    ss.inc(_li)

                @sec("tensor")
                def _(t, ic=ic):
                    ss.wait(t)
                    if ic == 0:
                        bm2.acquire(t)
                    if not NO_M2:
                        for th in range(2):
                            _li = t.matmul(pm2[32 * th:32 * th + 1, 0:512], ones_b[:, 0:1],
                                     raw2b[:, 512 * th:512 * (th + 1)],
                                     start=(ic == 0), stop=(ic == ICN - 1))
                        st.inc(_li)
                    else:
                        st.inc(t.nop())
                    brb.release(st)

                @sec("vector")
                def _(v, ic=ic):
                    ss.wait(v)
                    if ic == 0:
                        sv.inc(v.tensor_copy(mxrun[:], absg[:]))
                    else:
                        sv.inc(v.tensor_max(mxrun[:], mxrun[:], absg[:]))

                if ic % 4 == 3:
                    grp = ic // 4

                    @sec("scalar")
                    def _(sc, ic=ic, grp=grp):
                        sv.wait(sc)
                        c0 = ic - 3
                        qa.inc(sc.dma_start(
                            out=act_scr.rearrange("(c p) t -> p c t", p=128)[:, c0:c0 + 4, :],
                            in_=rawstg[:, (grp % 2) * 4:(grp % 2) * 4 + 4, :]), 16)
                        brs.release(ss, qa)
                if SERIAL_D1:
                    barrier()

        @sec("vector")
        def _(v):
            st.wait(v)
            _li = v.tensor_copy(row(R_S1, 0), pm2[0:1, 0:512])
            _li = v.tensor_copy(row(R_S1, 1), pm2[32:33, 0:512])
            sv.inc(_li)
            bm2.release(sv)

        def qmx2_col(tb):
            return stg[:, 16 + tb:17 + tb]

        for tb in range(8):
            @sec("tensor")
            def _(t, tb=tb):
                sv.wait(t)
                if tb % 4 == 0:
                    bm2.acquire(t)
                _li = t.transpose(pg0[:, 128 * (tb % 4):128 * (tb % 4 + 1)],
                            mxrun[:, 128 * tb:128 * (tb + 1)], iden_f[:])
                st.inc(_li)

            @sec("vector")
            def _(v, tb=tb):
                st.wait(v)
                _li = v.tensor_reduce(qmx2_col(tb), pg0[:, 128 * (tb % 4):128 * (tb % 4 + 1)],
                                AX.X, ALU.max)
                sv.inc(_li)
                if tb % 4 == 3:
                    bm2.release(sv)

        @sec("vector")
        def _(v):
            sv.wait(v)
            _li = v.memset(stg[:, 0:8], 0.0)
            vsafe(v, _li)
            sv.inc(v.tensor_copy(stg[:, 0:8], stg[:, 16:24]))

        @sec("tensor")
        def _(t):
            sv.wait(t)
            bm2.acquire(t)
            _li = t.transpose(pg0[:, 0:128], stg[:], iden_f[:])
            st.inc(_li)

        @sec("vector")
        def _(v):
            st.wait(v)
            v.wait_ge(qs.h, qs.n)
            _li = v.tensor_copy(trow2[0:8, 0:128], pg0[0:8, 0:128])
            sv.inc(_li)
            bm2.release(sv)

        @sec("tensor")
        def _(t):
            sv.wait(t)
            bm2.acquire(t)
            for half in range(2):
                for cc in range(4):
                    _li = t.matmul(pm2[32 * half:32 * half + 1, 128 * cc:128 * (cc + 1)],
                                   iden_f[0:8, 4 * half + cc:4 * half + cc + 1],
                                   trow2[0:8, 0:128], start=True, stop=True)
            st.inc(_li)

        @sec("vector")
        def _(v):
            st.wait(v)
            _li = v.tensor_copy(row(R_S2, 0), pm2[0:1, 0:512])
            _li = v.tensor_copy(row(R_S2, 1), pm2[32:33, 0:512])
            sv.inc(_li)
            bm2.release(sv)

        for half in range(2):
            @sec("vector")
            def _(v, half=half):
                sv.wait(v)
                _li = v.tensor_mul(row(R_A, half), row(R_DD, half), row(R_DD, half))
                _li = v.tensor_scalar_mul(row(R_B, half), row(R_S1, half), 1.0 / II)
                _li = v.tensor_mul(row(R_B, half), row(R_B, half), row(R_A, half))
                _li = v.tensor_scalar_add(row(R_B, half), row(R_B, half), SUB_EPS)
                _li = v.reciprocal(row(R_C, half), row(R_B, half))
                sv.inc(_li)

            @sec("scalar")
            def _(sc, half=half):
                sv.wait(sc)
                _li = sc.activation(row(R_RSTD, half), row(R_C, half), AF.Sqrt)
                ss.inc(_li)

            @sec("vector")
            def _(v, half=half):
                ss.wait(v)
                _li = v.tensor_mul(row(R_A, half), row(R_RSTD, half), row(R_RSTD, half))
                _li = v.tensor_mul(row(R_A, half), row(R_A, half), row(R_B, half))
                _li = v.tensor_scalar(row(R_A, half), row(R_A, half), -0.5, 1.5,
                                ALU.mult, ALU.add)
                _li = v.tensor_mul(row(R_RSTD, half), row(R_RSTD, half), row(R_A, half))
                _li = v.tensor_mul(row(R_MEAN, half), row(R_DD, half), row(R_RSTD, half))
                _li = v.tensor_mul(row(R_B, half), row(R_MEAN, half), row(R_S2, half))
                _li = v.tensor_scalar_max(row(R_B, half), row(R_B, half), EPS_Q)
                _li = v.reciprocal(row(R_C, half), row(R_B, half))
                _li = v.tensor_scalar_mul(row(R_C, half), row(R_C, half), 127.0)
                _li = v.tensor_mul(row(R_A, half), row(R_MEAN, half), row(R_C, half))
                _li = v.tensor_scalar_mul(row(R_B, half), row(R_B, half), 1.0 / 127.0)
                sv.inc(v.tensor_scalar(row(R_B, half), row(R_B, half),
                                       scl[0:1, 13:14], None, ALU.mult))

        barrier()

    stQ.close()

    with ExitStack() as stD2:
        fs2b = sb(stD2, "fs2b", (128, T), F32)
        dsb = sb(stD2, "dsb", (128, T), F32)
        wdstr = sb(stD2, "wdstr", (128, 2, 8, 512), F32)
        twd = sb(stD2, "twd", (128, 8, 512))
        qscr2 = sb(stD2, "qscr2", (128, 4096))
        n_act = sb(stD2, "n_act", (128, ICN, 512))
        actrd = sb(stD2, "actrd", (128, 4, 512), F32)
        xq_t = sb(stD2, "xq_t", (128, 4, 512), F32)
        y2stg = sb(stD2, "y2stg", (128, 4, 512), F32)
        ct1 = sb(stD2, "ct1", (128, 512), F32)
        cta = sb(stD2, "cta", (128, 2, 2048), F32)
        cti = sb(stD2, "cti", (128, 2, 2048), I8)
        ctb = sb(stD2, "ctb", (128, 2, 2048))
        ctr = sb(stD2, "ctr", (128, 2048), F32)
        pd = [pt(stD2, "pd%d" % i, (128, 512)) for i in range(4)]
        px0 = pt(stD2, "px0", (128, 1024))
        px1 = pt(stD2, "px1", (128, 1024))

        bbx = Buf(sem("bbx"))
        by2 = Buf()
        bpd = Buf(sem("bpd"))
        btw = Buf(sem("btw"))
        wd2ring = Ring([sem("wd2_r0"), sem("wd2_r1")], sem("rf6"), 2)
        acring = Ring([sem("ac_r0")], sem("rf7"), 1)
        xqring = Ring([sem("xq_r0")], sem("rf8"), 1)
        castq_mark = {}

        broadcast_row(R_A, fs2b[:], px0, bbx)
        broadcast_row(R_B, dsb[:], px0, bbx)

        for th in range(2):
            for blk in range(16):
                @sec("scalar")
                def _(sc, th=th, blk=blk):
                    acring.produce(sc)
                    acring.mark(sc.dma_start(
                        out=actrd[:],
                        in_=act_scr.rearrange("(c p) t -> p c t", p=128)
                        [:, 4 * blk:4 * (blk + 1), 512 * th:512 * (th + 1)]))

                gblk = 16 * th + blk
                par = gblk % 2

                @sec("vector")
                def _(v, th=th, blk=blk, gblk=gblk, par=par):
                    acring.wait_ready(v, gblk)
                    if gblk >= 2:
                        v.wait_ge(qg.h, castq_mark[gblk - 2])
                    for c4 in range(4):
                        ic = 4 * blk + c4
                        _li = v.scalar_tensor_tensor(
                            cta[:, par, 512 * c4:512 * (c4 + 1)], actrd[:, c4, :],
                            gcol_s[:, ic:ic + 1],
                            fs2b[:, 512 * th:512 * (th + 1)],
                            ALU.mult, ALU.mult)
                    sv.inc(_li)
                    acring.release(v, sv)

                @sec("gpsimd")
                def _(g, par=par, gblk=gblk):
                    g.wait_ge(sv.h, sv.n)
                    g.wait_ge(qg.h, qg.n)
                    qg.inc(g.dma_start(out=cti[:, par, :], in_=cta[:, par, :]), 16)
                    g.wait_ge(qg.h, qg.n)
                    qg.inc(g.dma_start(out=ctb[:, par, :], in_=cti[:, par, :]), 16)
                    castq_mark[gblk] = qg.n

                @sec("vector")
                def _(v, blk=blk, par=par):
                    v.wait_ge(qg.h, qg.n)
                    dst = n_act[:, 4 * blk:4 * (blk + 1), :].rearrange("p c n -> p (c n)")
                    src = cta[:, par, :]
                    cab = qscr2[:, 2048:4096]
                    cbb = qscr2[:, 0:2048]
                    _li = v.tensor_copy(ctr[:], ctb[:, par, :])
                    _li = v.tensor_sub(ctr[:], src, ctr[:])
                    _li = v.tensor_single_scalar(cab, ctr[:], 0.5, ALU.is_ge)
                    _li = v.tensor_single_scalar(cbb, ctr[:], -0.5, ALU.is_lt)
                    _li = v.tensor_add(cab, ctb[:, par, :], cab)
                    _li = v.tensor_sub(dst, cab, cbb)
                    sv.inc(_li)

            for hq in range(4):
                @sec("sync")
                def _(s, th=th, hq=hq):
                    if th == 0 and hq == 0:
                        s.wait_ge(qa.h, xln1_store_mark[0])
                    xqring.produce(s)
                    xqring.mark(s.dma_start(
                        out=xq_t[:],
                        in_=xln1_scr.rearrange("(c p) t -> p c t", p=128)
                        [:, 4 * hq:4 * (hq + 1), 512 * th:512 * (th + 1)]))

                for ib8 in range(8):
                    tile_i = (th * 4 + hq) * 8 + ib8

                    @sec("sync")
                    def _(s, hq=hq, ib8=ib8):
                        slot = wd2ring.produce(s)
                        wd2ring.mark(s.dma_start(
                            out=wdstr[:, slot, :, :],
                            in_=wdT.rearrange("(c p) h -> p c h", p=128)
                            [:, 8 * ib8:8 * (ib8 + 1), 512 * hq:512 * (hq + 1)]))

                    @sec("vector")
                    def _(v, tile_i=tile_i):
                        wd2ring.wait_ready(v, tile_i)
                        btw.acquire(v)
                        src = wdstr[:, tile_i % 2, :, :].rearrange("p c n -> p (c n)")
                        twv = twd[:].rearrange("p c n -> p (c n)")
                        _li = v.tensor_scalar(twv, src, colA[:, 5:6], None, ALU.is_ge)
                        _li = v.tensor_scalar(qscr2[:], src, colA[:, 6:7], None, ALU.is_le)
                        sv.inc(v.tensor_sub(twv, twv, qscr2[:]))
                        wd2ring.release(v, sv)

                    @sec("tensor")
                    def _(t, ib8=ib8):
                        sv.wait(t)
                        if ib8 == 0:
                            bpd.acquire(t)
                        for ic8 in range(8):
                            ic = 8 * ib8 + ic8
                            for hc in range(4):
                                _li = t.matmul(pd[hc][:, 0:512],
                                         twd[:, ic8, 128 * hc:128 * (hc + 1)],
                                         n_act[:, ic, :],
                                         start=(ic == 0), stop=(ic == ICN - 1))
                        st.inc(_li)
                        btw.release(st)

                for hc in range(4):
                    @sec("vector")
                    def _(v, th=th, hq=hq, hc=hc):
                        st.wait(v)
                        xqring.wait_ready(v, th * 4 + hq)
                        if hc == 0:
                            by2.acquire(v)
                        _li = v.tensor_mul(ct1[:], pd[hc][:, 0:512],
                                     dsb[:, 512 * th:512 * (th + 1)])
                        _li = v.tensor_add(y2stg[:, hc, :], ct1[:], xq_t[:, hc, :])
                        sv.inc(_li)
                        if hc == 3:
                            bpd.release(sv)
                            xqring.release(v, sv)

                @sec("scalar")
                def _(sc, th=th, hq=hq):
                    sv.wait(sc)
                    qa.inc(sc.dma_start(
                        out=y2_scr.rearrange("(c p) t -> p c t", p=128)
                        [:, 4 * hq:4 * (hq + 1), 512 * th:512 * (th + 1)],
                        in_=y2stg[:]), 16)
                    by2.release(qa)

        barrier()

    with ExitStack() as stE:
        y2f = sb(stE, "y2f", (128, DC, T), F32)
        cbf1k = sb(stE, "cbf1k", (128, 1024))
        cbf1k2 = sb(stE, "cbf1k2", (128, 1024))
        bcst2 = sb(stE, "bcst2", (128, 2, 1024), F32)
        ytm = sb(stE, "ytm", (128, 2, 2048), F32)
        ctk = sb(stE, "ctk", (128, 1024), F32)
        trow3 = sb(stE, "trow3", (128, 512), F32)
        pe0 = pt(stE, "pe0", (128, 512))
        pe1 = pt(stE, "pe1", (128, 512))
        pe2 = pt(stE, "pe2", (128, 512))
        pe3 = pt(stE, "pe3", (128, 512))

        brw2 = Buf(sem("brw2"))
        bce = Buf()
        bbx2 = Buf(sem("bbx2"))
        bpe = [Buf(sem("bpe0")), Buf(sem("bpe1"))]

        @sec("sync")
        def _(s):
            qa.wait(s)
            qs.inc(s.dma_start(out=y2f[:],
                               in_=y2_scr.rearrange("(c p) t -> p c t", p=128)), 16)
        y2_mark = qs.n

        for dc in range(DC):
            @sec("scalar")
            def _(sc, dc=dc):
                if dc == 0:
                    sc.wait_ge(qs.h, y2_mark)
                bce.acquire(sc)
                _li = sc.copy(cbf1k[:], y2f[:, dc, :])
                _li = sc.activation(cbf1k2[:], y2f[:, dc, :], AF.Square)
                ss.inc(_li)

            @sec("tensor")
            def _(t, dc=dc):
                ss.wait(t)
                if dc == 0:
                    brw2.acquire(t)
                for th in range(2):
                    _li = t.matmul(pe0[32 * th:32 * th + 1, 0:512], ones_b[:, 0:1],
                             cbf1k[:, 512 * th:512 * (th + 1)],
                             start=(dc == 0), stop=(dc == DC - 1))
                    _li = t.matmul(pe1[32 * th:32 * th + 1, 0:512], ones_b[:, 0:1],
                             cbf1k2[:, 512 * th:512 * (th + 1)],
                             start=(dc == 0), stop=(dc == DC - 1))
                st.inc(_li)
                bce.release(st)

        @sec("vector")
        def _(v):
            st.wait(v)
            _li = v.tensor_copy(row(R_S1, 0), pe0[0:1, 0:512])
            _li = v.tensor_copy(row(R_S1, 1), pe0[32:33, 0:512])
            _li = v.tensor_copy(row(R_S2, 0), pe1[0:1, 0:512])
            _li = v.tensor_copy(row(R_S2, 1), pe1[32:33, 0:512])
            sv.inc(_li)
            brw2.release(sv)

        for half in range(2):
            @sec("vector")
            def _(v, half=half):
                sv.wait(v)
                _li = v.tensor_scalar_mul(row(R_MEAN, half), row(R_S1, half), 1.0 / D)
                _li = v.tensor_mul(row(R_A, half), row(R_MEAN, half), row(R_MEAN, half))
                _li = v.scalar_tensor_tensor(row(R_B, half), row(R_S2, half), 1.0 / D,
                                       row(R_A, half), ALU.mult, ALU.subtract)
                _li = v.tensor_scalar_add(row(R_B, half), row(R_B, half), LN_EPS)
                _li = v.reciprocal(row(R_C, half), row(R_B, half))
                sv.inc(_li)

            @sec("scalar")
            def _(sc, half=half):
                sv.wait(sc)
                _li = sc.activation(row(R_RSTD, half), row(R_C, half), AF.Sqrt)
                ss.inc(_li)

            @sec("vector")
            def _(v, half=half):
                ss.wait(v)
                _li = v.tensor_mul(row(R_A, half), row(R_RSTD, half), row(R_RSTD, half))
                _li = v.tensor_mul(row(R_A, half), row(R_A, half), row(R_B, half))
                _li = v.tensor_scalar(row(R_A, half), row(R_A, half), -0.5, 1.5,
                                ALU.mult, ALU.add)
                _li = v.tensor_mul(row(R_RSTD, half), row(R_RSTD, half), row(R_A, half))
                sv.inc(_li)

        broadcast_row(R_MEAN, bcst2[:, 0, :], pe2, bbx2)
        broadcast_row(R_RSTD, bcst2[:, 1, :], pe2, bbx2)

        for dc in range(DC):
            @sec("vector")
            def _(v, dc=dc):
                sv.wait(v)
                _li = v.scalar_tensor_tensor(ctk[:], y2f[:, dc, :], 0.0,
                                       bcst2[:, 0, :], ALU.add, ALU.subtract)
                sv.inc(v.tensor_mul(y2f[:, dc, :], ctk[:], bcst2[:, 1, :]))

        ytm_mark = {}
        for tb in range(8):
            for dc in range(DC):
                par = dc % 2
                pxt = pe2 if par == 0 else pe3
                pxb = bpe[par]

                @sec("tensor")
                def _(t, tb=tb, dc=dc, pxt=pxt, pxb=pxb):
                    sv.wait(t)
                    pxb.acquire(t)
                    _li = t.transpose(pxt[:, 0:128], y2f[:, dc, 128 * tb:128 * (tb + 1)],
                                iden_f[:])
                    st.inc(_li)

                @sec("vector")
                def _(v, tb=tb, dc=dc, pxt=pxt, pxb=pxb):
                    st.wait(v)
                    if dc == 0 and tb >= 2:
                        v.wait_ge(qa.h, ytm_mark[tb - 2])
                    _li = v.tensor_copy(ytm[:, tb % 2, 128 * dc:128 * (dc + 1)],
                                      pxt[:, 0:128])
                    sv.inc(_li)
                    pxb.release(sv)

            @sec("scalar")
            def _(sc, tb=tb):
                sv.wait(sc)
                qa.inc(sc.dma_start(out=y_out[128 * tb:128 * (tb + 1), :],
                                    in_=ytm[:, tb % 2, :]), 16)
            ytm_mark[tb] = qa.n
        @sec("scalar")
        def _(sc):
            qa.wait(sc)

    blkstack.close()
    stRows.close()
    st0.close()
    return nc


_NC_CACHE = None


def _get_nc():
    global _NC_CACHE
    if _NC_CACHE is None:
        _NC_CACHE = build_program()
    return _NC_CACHE


def _host_tables():
    import jax
    import jax.numpy as jnp
    with jax.default_device(jax.devices("cpu")[0]):
        inv_freq = 1.0 / (10000.0 ** (jnp.arange(0, HD, 2, dtype=jnp.float32) / HD))
        ang = jnp.arange(S, dtype=jnp.float32)[:, None] * inv_freq[None, :]
        c_tab = np.asarray(jnp.cos(ang), dtype=np.float32)
        s_tab = np.asarray(jnp.sin(ang), dtype=np.float32)

    def rope_expand(positions):
        c = c_tab[positions]
        s = s_tab[positions]
        ce = np.empty((HD, len(positions)), np.float32)
        se = np.empty((HD, len(positions)), np.float32)
        ce[0::2] = c.T
        ce[1::2] = c.T
        se[0::2] = -s.T
        se[1::2] = s.T
        cf = np.vstack([ce, ce])
        sf = np.vstack([se, se])
        return np.ascontiguousarray(cf), np.ascontiguousarray(sf)

    cf_full, sf_full = rope_expand(np.arange(S))
    perm = np.zeros((128, 128), np.float32)
    for i in range(0, 128, 2):
        perm[i, i + 1] = 1.0
        perm[i + 1, i] = 1.0
    iden = np.eye(128, dtype=np.float32)
    ones = np.ones((128, 128), np.float32)
    e2 = np.zeros((128, 2), np.float32)
    e2[0:64, 0] = 1.0
    e2[64:128, 1] = 1.0
    dm = np.zeros((128, 2048), np.float32)
    q = np.arange(512)
    for kb in range(4):
        k_abs = 128 * kb + np.arange(128)
        dm[:, 512 * kb:512 * (kb + 1)] = (k_abs[:, None] <= q[None, :])
    return cf_full, sf_full, perm, iden, ones, e2, dm, rope_expand


def kernel(L0_attn_x, L0_attn_wq, L0_attn_wk, L0_attn_wv, L0_attn_wo,
           L0_ffn_w_gate, L0_ffn_w_up, L0_ffn_w_down, L0_ffn_subln_weight):
    x = np.ascontiguousarray(np.asarray(L0_attn_x, dtype=np.float32))
    wq = np.ascontiguousarray(np.asarray(L0_attn_wq, dtype=np.float32))
    wk = np.ascontiguousarray(np.asarray(L0_attn_wk, dtype=np.float32))
    wv = np.ascontiguousarray(np.asarray(L0_attn_wv, dtype=np.float32))
    wo = np.ascontiguousarray(np.asarray(L0_attn_wo, dtype=np.float32))
    wg = np.asarray(L0_ffn_w_gate, dtype=np.float32)
    wu = np.asarray(L0_ffn_w_up, dtype=np.float32)
    wd = np.asarray(L0_ffn_w_down, dtype=np.float32)
    g = np.asarray(L0_ffn_subln_weight, dtype=np.float32)

    b0heads = [h for h in range(NH) if (h // 4) % 2 == 0]
    b64heads = [h for h in range(NH) if (h // 4) % 2 == 1]
    perm_a = np.empty(D, np.int64)
    for i in range(16):
        perm_a[128 * i:128 * i + 64] = 64 * b0heads[i] + np.arange(64)
        perm_a[128 * i + 64:128 * (i + 1)] = 64 * b64heads[i] + np.arange(64)
    wq = np.ascontiguousarray(wq[:, perm_a])
    wo = np.ascontiguousarray(wo[perm_a, :])
    xT = [np.ascontiguousarray(x[b].T) for b in range(2)]
    wgT_h = np.ascontiguousarray(wg.T)
    wuT_h = np.ascontiguousarray(wu.T)
    wdT_h = np.ascontiguousarray(wd.T)
    g_col_h = np.ascontiguousarray(g.reshape(ICN, 128).T)

    cf_full, sf_full, perm, iden, ones, e2, dm, rope_expand = _host_tables()

    in_maps = []
    meta = []
    for c in range(8):
        b = c // 4
        j = c % 4
        pA, pB = j, 7 - j
        meta.append((b, pA, pB))
        pos = np.concatenate([np.arange(512 * pA, 512 * (pA + 1)),
                              np.arange(512 * pB, 512 * (pB + 1))])
        xo = np.ascontiguousarray(
            np.concatenate([xT[b][:, 512 * pA:512 * (pA + 1)],
                            xT[b][:, 512 * pB:512 * (pB + 1)]], axis=1))
        co, so = rope_expand(pos)
        pat = np.zeros((128, NGT), np.float32)
        in_maps.append({
            "x_own": xo, "x_full": xT[b],
            "wq": wq, "wk": wk, "wv": wv, "wo": wo,
            "wgT": wgT_h, "wuT": wuT_h, "wdT": wdT_h, "g_col": g_col_h,
            "rope_cf": cf_full, "rope_sf": sf_full,
            "rope_co": co, "rope_so": so,
            "perm": perm, "iden": iden, "ones": ones, "e2": e2,
            "dmask": dm, "pat": pat,
        })

    nc = _get_nc()
    res = run_bass_kernel_spmd(nc, in_maps, core_ids=list(range(8)))

    out = np.empty((2, S, D), np.float32)
    for c in range(8):
        b, pA, pB = meta[c]
        yc = np.asarray(res.results[c]["y"])
        out[b, 512 * pA:512 * (pA + 1), :] = yc[0:512]
        out[b, 512 * pB:512 * (pB + 1), :] = yc[512:1024]
    return out
